# version 06
# baseline (speedup 1.0000x reference)
import numpy as np
import ml_dtypes
from contextlib import ExitStack
import concourse.bass as bass
import concourse.mybir as mybir
from concourse.bass_utils import run_bass_kernel_spmd

F32 = mybir.dt.float32
BF16 = mybir.dt.bfloat16
I32 = mybir.dt.int32
AF = mybir.ActivationFunctionType
ALU = mybir.AluOpType
AX = mybir.AxisListType

H = 4096
II = 11008
S = 2048
NKV = 8
NQ = 32
DH = 128
HC = H // 128
IC = II // 128
NCP = H // 256
EPS = 1e-5
RSQD = 1.0 / float(np.sqrt(DH))

EMAX = {0: [8, 8, 8, 8], 1: [13, 14, 15, 16]}
ROW_BASE = {0: (0, 1536), 1: (512, 1024)}
PERM_ABS = {0: [0, 1, 2, 3], 1: [1, 0, 3, 2]}
QPBASE = {0: 0, 1: 1536}


def _mask_offsets():
    offs = {}
    for qg in (0, 1):
        o = 0
        for qt in range(4):
            offs[(qg, qt)] = o
            o += EMAX[qg][qt] * 128
        offs[(qg, "total")] = o
    return offs


MOFF = _mask_offsets()
MC0 = MOFF[(0, "total")]
MC1 = MOFF[(1, "total")]


def _build_consts():
    ident = np.eye(128, dtype=ml_dtypes.bfloat16)
    ones_row = np.ones((1, 128), dtype=np.float32)
    half_row = np.full((1, 128), 0.5, dtype=np.float32)
    return ident, ones_row, half_row


def _perm_rows(half):
    p = np.arange(S)
    blk = p // 512
    return np.array([PERM_ABS[half][b] * 512 + (r % 512)
                     for r, b in zip(p, blk)], dtype=np.int64)


def _host_tables(half):
    inv = 1.0 / (10000.0 ** (np.arange(0, DH, 2, dtype=np.float64)[: DH // 2]
                             / DH))
    perm = _perm_rows(half)
    fr = perm[None, :].astype(np.float64) * inv[:, None]
    ck = np.cos(fr).astype(np.float32)
    sk = np.sin(fr).astype(np.float32)
    cq = (np.cos(fr) * RSQD).astype(np.float32)
    sq = (np.sin(fr) * RSQD).astype(np.float32)
    masks = np.zeros((128, MC0 + MC1), dtype=ml_dtypes.bfloat16)
    for qg in (0, 1):
        for qt in range(4):
            o = (0 if qg == 0 else MC0) + MOFF[(qg, qt)]
            E = EMAX[qg][qt] * 128
            qrows = perm[QPBASE[qg] + qt * 128 + np.arange(128)][:, None]
            krows = perm[np.arange(E)][None, :]
            masks[:, o: o + E] = (krows <= qrows).astype(ml_dtypes.bfloat16)
    return ck, sk, cq, sq, masks


NROT = 24


class Prog:
    ENGS = ("t", "v", "a", "g", "y")

    def __init__(self, sem):
        self.ops = {e: [] for e in self.ENGS}
        self.cnt = {k: 0 for k in sem}
        self.sem = sem
        self.rot = {"y": 0, "g": 0, "a": 0}
        self.fresh = {"y": False, "g": False, "a": False}
        self.gidx = 0

    QPFX = {"y": "dS%d", "g": "dG%d", "a": "dA%d"}

    def qsem(self, q):
        return self.QPFX[q] % self.rot[q]

    def on(self, e, fn):
        self.gidx += 1
        self.ops[e].append(fn)

    def inc(self, nm, n=1):
        self.cnt[nm] += n
        return self.cnt[nm]

    ESEM = {"t": "sP", "v": "sV", "a": "sA"}

    def wait(self, e, nm, v):
        if v <= 0:
            return
        s = self.sem[nm]
        self.ops[e].append(lambda eng, s=s, v=v: eng.wait_ge(s, v))

    def op(self, e, fn, selfwait=False):
        nm = self.ESEM[e]
        if selfwait and self.cnt[nm] > 0:
            self.wait(e, nm, self.cnt[nm])
        s = self.sem[nm]
        self.on(e, lambda eng, fn=fn, s=s: fn(eng).then_inc(s, 1))
        return self.inc(nm)

    def barrier(self):
        snap = dict(self.cnt)
        for e in self.ENGS:
            for nm, v in snap.items():
                self.wait(e, nm, v)


def build(dump=None, phases=99, sub=99, npass=2):
    nc = bass.Bass("TRN2", target_bir_lowering=False)

    xin = nc.dram_tensor("xin", [S, H], F32, kind="ExternalInput").ap()
    gq = nc.dram_tensor("gq", [IC, 128, H], I32, kind="ExternalInput").ap()
    uq = nc.dram_tensor("uq", [IC, 128, H], I32, kind="ExternalInput").ap()
    dq = nc.dram_tensor("dq", [128, IC, 256 * NCP], I32,
                        kind="ExternalInput").ap()
    gs = nc.dram_tensor("gs", [IC, H], F32, kind="ExternalInput").ap()
    us = nc.dram_tensor("us", [IC, H], F32, kind="ExternalInput").ap()
    dsc = nc.dram_tensor("dsc", [NCP, IC * 256], F32,
                         kind="ExternalInput").ap()
    wq = nc.dram_tensor("wq", [16, 128, HC * 256], F32,
                        kind="ExternalInput").ap()
    wk = nc.dram_tensor("wk", [2, 128, HC * 512], F32,
                        kind="ExternalInput").ap()
    wv = nc.dram_tensor("wv", [2, 128, HC * 512], F32,
                        kind="ExternalInput").ap()
    wo = nc.dram_tensor("wo", [16, 128, HC * 256], F32,
                        kind="ExternalInput").ap()
    ck_in = nc.dram_tensor("ck", [64, S], F32, kind="ExternalInput").ap()
    sk_in = nc.dram_tensor("sk", [64, S], F32, kind="ExternalInput").ap()
    cq_in = nc.dram_tensor("cq", [64, S], F32, kind="ExternalInput").ap()
    sq_in = nc.dram_tensor("sq", [64, S], F32, kind="ExternalInput").ap()
    cmask = nc.dram_tensor("cmask", [128, MC0 + MC1], BF16,
                           kind="ExternalInput").ap()

    xout = nc.dram_tensor("xout", [1024, H], F32, kind="ExternalOutput").ap()
    o_dbg = None
    if dump is not None:
        o_dbg = nc.dram_tensor("o_dbg", [S, H], F32,
                               kind="ExternalOutput").ap()

    ident_np, ones_np, half_np = _build_consts()
    c_ident = nc.inline_tensor(ident_np, name="c_ident").ap()
    c_ones = nc.inline_tensor(ones_np, name="c_ones").ap()
    c_half = nc.inline_tensor(half_np, name="c_half").ap()

    wg_bf = nc.dram_tensor("wg_bf", [IC, 128, H], BF16, kind="Internal").ap()
    wu_bf = nc.dram_tensor("wu_bf", [IC, 128, H], BF16, kind="Internal").ap()
    wd_bf = nc.dram_tensor("wd_bf", [NCP, 128, IC * 256], BF16,
                           kind="Internal").ap()
    hT_scr = nc.dram_tensor("hT_scr", [2, 128, IC * 512], BF16,
                            kind="Internal").ap()
    x_scr = nc.dram_tensor("x_scr", [S, H], F32, kind="Internal").ap()

    es = ExitStack()
    with es:
        sbt = lambda n, sh, dt: es.enter_context(nc.sbuf_tensor(n, sh, dt))
        ident = sbt("ident", [128, 128], BF16)
        onesr = sbt("onesr", [1, 128], F32)
        halfr = sbt("halfr", [1, 128], F32)
        st1 = sbt("st1", [128, 1], F32)
        st2 = sbt("st2", [128, 1], F32)
        st3 = sbt("st3", [128, 1], F32)
        st4 = sbt("st4", [128, 1], F32)
        dsumb = sbt("dsumb", [128, 1], F32)
        rdsum = sbt("rdsum", [128, 1], F32)
        negm = sbt("negm", [128, 1], F32)

        ps_a = es.enter_context(nc.psum_tensor("ps_a", [128, 2048], F32))
        ps_b = es.enter_context(nc.psum_tensor("ps_b", [128, 1024], BF16))
        ps_b2 = es.enter_context(nc.psum_tensor("ps_b2", [128, 1024], BF16))
        ps_c = es.enter_context(nc.psum_tensor("ps_c", [128, 512], F32))
        ps_d = es.enter_context(nc.psum_tensor("ps_d", [128, 512], F32))

        sem = {}
        semnames = ["sP", "sV", "sA"]
        semnames += ["dS%d" % i for i in range(NROT)]
        semnames += ["dG%d" % i for i in range(NROT)]
        semnames += ["dA%d" % i for i in range(NROT)]
        for nm in semnames:
            sem[nm] = es.enter_context(nc.semaphore(nm))
        block = es.enter_context(nc.Block())
        pr = Prog(sem)
        SEM = sem

        def dma(q, out, in_, nm=None):
            nm = pr.qsem(q)
            v = pr.inc(nm, 16)
            s = sem[nm]
            pr.fresh[q] = True

            def go(e, out=out, in_=in_, s=s):
                o = out(e) if callable(out) else out
                i = in_(e) if callable(in_) else in_
                e.dma_start(out=o, in_=i).then_inc(s, 16)
            pr.on(q, go)
            return (nm, v)

        def relay(q):
            nm = pr.qsem(q)
            h = (nm, pr.cnt[nm])
            if pr.fresh[q]:
                pr.rot[q] = (pr.rot[q] + 1) % NROT
                pr.fresh[q] = False
            return h

        def x_rows(qg, rt):
            a = QPBASE[qg] + rt * 128
            return x_scr[a:a + 128, :]

        def x_strip_ap(qg, rt, cp):
            a = QPBASE[qg] + rt * 128
            return x_scr[a:a + 128, cp * 256:(cp + 1) * 256]

        ln_prev = {"ay": 0, "tr": 0}

        def ln_tile(src_rows, ybig, R, col0, xtile, ybuf):
            pr.wait("y", "sV", ln_prev["ay"])
            dma("y", xtile[:], src_rows)
            vload = relay("y")
            pr.wait("v", *vload)
            pr.op("v", lambda v: v.reduce_sum(st1[:], xtile[:], axis=AX.X,
                                              negate=True), selfwait=True)
            v_mu = pr.op("v", lambda v: v.tensor_scalar(
                st1[:], st1[:], 1.0 / H, None, op0=ALU.mult), selfwait=True)
            pr.wait("a", "sV", v_mu)
            pr.wait("a", *vload)
            pr.wait("a", "sP", ln_prev["tr"])
            a_sq = pr.op("a", lambda a: a.activation(
                ybuf[:], xtile[:], AF.Square, accum_out=st2[:]),
                selfwait=True)
            pr.wait("v", "sA", a_sq)
            pr.op("v", lambda v: v.tensor_mul(st3[:], st1[:], st1[:]),
                  selfwait=True)
            pr.op("v", lambda v: v.scalar_tensor_tensor(
                st3[:], st2[:], 1.0 / H, st3[:], op0=ALU.mult,
                op1=ALU.subtract), selfwait=True)
            v_var = pr.op("v", lambda v: v.tensor_scalar(
                st3[:], st3[:], EPS, None, op0=ALU.add), selfwait=True)
            pr.wait("a", "sV", v_var)
            a_ln = pr.op("a", lambda a: a.activation(st2[:], st3[:], AF.Ln),
                         selfwait=True)
            pr.wait("v", "sA", a_ln)
            v_nh = pr.op("v", lambda v: v.tensor_scalar(
                st3[:], st2[:], -0.5, None, op0=ALU.mult), selfwait=True)
            pr.wait("a", "sV", v_nh)
            a_rstd = pr.op("a", lambda a: a.activation(st2[:], st3[:],
                                                       AF.Exp),
                           selfwait=True)
            pr.wait("v", "sA", a_rstd)
            a_y = pr.op("v", lambda v: v.tensor_scalar(
                ybuf[:], xtile[:], st1[:, 0:1], st2[:, 0:1],
                op0=ALU.add, op1=ALU.mult), selfwait=True)
            pr.wait("t", "sV", a_y)
            cps = []
            y3 = ybig[:].rearrange("p (hc r) -> p hc r", hc=HC)
            for g8 in range(4):
                pb = ps_b if (g8 % 2 == 0) else ps_b2
                if g8 >= 2:
                    pr.wait("t", "sV", cps[g8 - 2])
                for j in range(8):
                    hc = g8 * 8 + j
                    if j < 7:
                        pr.on("t", lambda t, pb=pb, j=j, hc=hc:
                              t.transpose(pb[:, j * 128:(j + 1) * 128],
                                          ybuf[:, hc * 128:(hc + 1) * 128],
                                          ident[:]))
                    else:
                        pr.op("t", lambda t, pb=pb, j=j, hc=hc:
                              t.transpose(pb[:, j * 128:(j + 1) * 128],
                                          ybuf[:, hc * 128:(hc + 1) * 128],
                                          ident[:]))
                p_tr = pr.cnt["sP"]
                pr.wait("v", "sP", p_tr)
                cps.append(pr.op("v", lambda v, pb=pb, g8=g8, y3=y3,
                                 col0=col0:
                                 v.tensor_copy(
                                     y3[:, g8 * 8:(g8 + 1) * 8,
                                        col0:col0 + 128], pb[:]),
                                 selfwait=True))
            ln_prev["ay"] = a_y
            ln_prev["tr"] = pr.cnt["sP"]

        def ffn_pass(tag, src512, sstrip, dstrip, fused, wd_fused,
                     sub=99):
            sc = ExitStack()
            ybig = sc.enter_context(nc.sbuf_tensor(f"ybig_{tag}",
                                                   [128, HC * 1024], BF16))
            with sc:
                sgu = ExitStack()
                wblk = sgu.enter_context(nc.sbuf_tensor(f"wblk_{tag}",
                                                        [128, 4 * H], BF16))

                hstage = sgu.enter_context(nc.sbuf_tensor(f"hst_{tag}",
                                                          [128, 1024], BF16))
                silu_t = sgu.enter_context(nc.sbuf_tensor(f"sil_{tag}",
                                                          [128, 1024], BF16))
                with sgu:
                    if sub < 1:
                        return
                    sln = ExitStack()
                    xtile = sln.enter_context(nc.sbuf_tensor(
                        f"xt_{tag}", [128, H], F32))
                    ybuf = sln.enter_context(nc.sbuf_tensor(
                        f"yb_{tag}", [128, H], BF16))
                    with sln:
                        for gi in range(2):
                            for rt in range(4):
                                ln_tile(src512(gi, rt), ybig, 1024,
                                        gi * 512 + rt * 128, xtile, ybuf)
                        pr.barrier()
                    qint = sgu.enter_context(nc.sbuf_tensor(f"qint_{tag}",
                                                            [128, 3 * H],
                                                            I32))
                    sF = sgu.enter_context(nc.sbuf_tensor(f"sF_{tag}",
                                                          [128, 2 * H],
                                                          F32))
                    if sub < 2:
                        return
                    sttj = {}
                    stt_last = {}
                    mm_done = {}
                    ev_done = {}
                    h_store = {}
                    w_load = {}
                    scr_w = {}
                    for ib in range(IC):
                        slot = (ib % 2) * 2 * H
                        if fused:
                            ldq_m = {}
                            qs_m = {}
                            flc = {}
                            for m in range(2):
                                qs_m[m] = ((2 * ib + m) % 3) * H
                                if 2 * ib + m >= 3:
                                    pm = 2 * ib + m - 3
                                    pr.wait("y", "sV",
                                            sttj[(pm // 2,
                                                  (pm % 2) * 8 + 7)])
                                dma("y", qint[:, qs_m[m]:qs_m[m] + H],
                                    (gq if m == 0 else uq)[ib])
                                ldq_m[m] = relay("y")
                            for m in range(2):
                                src = gs if m == 0 else us
                                for c in range(4):
                                    if ib >= 1:
                                        pr.wait("g", "sV",
                                                sttj[(ib - 1,
                                                      m * 8 + 2 * c + 1)])
                                    dma("g", sF[:, m * H + c * 1024:
                                                m * H + (c + 1) * 1024],
                                        src[ib:ib + 1,
                                            c * 1024:(c + 1) * 1024]
                                        .partition_broadcast(128), "dG")
                                    flc[(m, c)] = relay("g")
                            if ib >= 2:
                                pr.wait("t", "sV", ev_done[(ib - 2) * 2 + 1])
                            for m in range(2):
                                for sp in range(8):
                                    j = m * 8 + sp
                                    if sp == 0:
                                        pr.wait("v", *ldq_m[m])
                                    if sp % 2 == 0:
                                        pr.wait("v", *flc[(m, sp // 2)])
                                    if j == 0 and ib >= 2:
                                        pr.wait("v", "sP",
                                                mm_done[(ib - 2) * 2
                                                        + 1])
                                        if (ib - 2) in scr_w:
                                            pr.wait("v",
                                                    *scr_w[ib - 2])
                                    pr.on("v", lambda v, m=m, sp=sp,
                                          slot=slot, qs=qs_m[m]:
                                          v.scalar_tensor_tensor(
                                              wblk[:, slot + m * H +
                                                   sp * 512:
                                                   slot + m * H +
                                                   (sp + 1) * 512],
                                              qint[:, qs + sp * 512:
                                                   qs + (sp + 1) * 512],
                                              8.0,
                                              sF[:, m * H + sp * 512:
                                                 m * H + (sp + 1) * 512],
                                              op0=ALU.subtract,
                                              op1=ALU.mult)
                                          .then_inc(SEM["sV"], 1))
                                    sttj[(ib, j)] = pr.inc("sV")
                            stt_last[ib] = sttj[(ib, 15)]
                            pr.wait("a", "sV", stt_last[ib])
                            dma("a", wg_bf[ib], wblk[:, slot:slot + H])
                            dma("a", wu_bf[ib],
                                wblk[:, slot + H:slot + 2 * H])
                            scr_w[ib] = relay("a")
                        else:
                            if ib >= 2:
                                pr.wait("y", "sP", mm_done[(ib - 2) * 2 + 1])
                            dma("y", wblk[:, slot:slot + H], wg_bf[ib])
                            dma("y", wblk[:, slot + H:slot + 2 * H],
                                wu_bf[ib])
                            w_load[ib] = relay("y")
                        for rc in range(2):
                            step = ib * 2 + rc
                            setb = (step % 2) * 1024
                            if fused:
                                pr.wait("t", "sV", stt_last[ib])
                            else:
                                pr.wait("t", *w_load[ib])
                            if step >= 2:
                                pr.wait("t", "sV", ev_done[step - 2])
                            for m in range(2):
                                for hc in range(HC):
                                    last = (hc == HC - 1)

                                    def mmf(t, m=m, hc=hc, slot=slot,
                                            setb=setb, rc=rc, last=last):
                                        ins = t.matmul(
                                            ps_a[:, setb + m * 512:
                                                 setb + (m + 1) * 512],
                                            wblk[:, slot + m * H + hc * 128:
                                                 slot + m * H +
                                                 (hc + 1) * 128],
                                            ybig[:, hc * 1024 + rc * 512:
                                                 hc * 1024 +
                                                 (rc + 1) * 512],
                                            start=(hc == 0), stop=last)
                                        if last:
                                            ins.then_inc(SEM["sP"], 1)
                                    pr.on("t", mmf)
                                pr.inc("sP")
                            mm_done[step] = pr.cnt["sP"]
                            par = step % 2
                            pr.wait("a", "sP", mm_done[step])
                            if step >= 2:
                                pr.wait("a", *h_store[step - 2])
                            pr.on("a", lambda a, setb=setb, par=par:
                                  a.activation(silu_t[:, par * 512:
                                                      (par + 1) * 512],
                                               ps_a[:, setb:setb + 512],
                                               AF.Silu)
                                  .then_inc(SEM["sA"], 1))
                            a_si = pr.inc("sA")
                            pr.wait("v", "sA", a_si)
                            if step >= 2:
                                pr.wait("v", *h_store[step - 2])
                            pr.on("v", lambda v, setb=setb, par=par:
                                  v.scalar_tensor_tensor(
                                      hstage[:, par * 512:(par + 1) * 512],
                                      silu_t[:, par * 512:(par + 1) * 512],
                                      0.5,
                                      ps_a[:, setb + 512:setb + 1024],
                                      op0=ALU.mult, op1=ALU.mult)
                                  .then_inc(SEM["sV"], 1))
                            ev_done[step] = pr.inc("sV")
                            pr.wait("a", "sV", ev_done[step])
                            dma("a",
                                hT_scr[rc, :, ib * 512:(ib + 1) * 512],
                                hstage[:, par * 512:(par + 1) * 512])
                            h_store[step] = relay("a")
                    pr.barrier()
                sc.close()

                if sub < 3:
                    return
                if wd_fused:
                    sd = ExitStack()
                    wdh = (IC + 1) // 2
                    qd = sd.enter_context(nc.sbuf_tensor(f"qd_{tag}",
                                                         [128, wdh * 256],
                                                         I32))
                    wdp = sd.enter_context(nc.sbuf_tensor(f"wdp_{tag}",
                                                          [128, IC * 256],
                                                          BF16))
                    sFd = sd.enter_context(nc.sbuf_tensor(f"sFd_{tag}",
                                                          [128, wdh * 256],
                                                          F32))
                    with sd:
                        wd_stt = {}
                        wd_str = {}
                        for cp in range(NCP):
                            for hb in range(2):
                                ic0 = hb * wdh
                                nic = min(wdh, IC - ic0)
                                if cp + hb > 0:
                                    prevk = (cp, ic0 - 1) if hb == 1 else \
                                        (cp - 1, IC - 1)
                                    pr.wait("y", "sV", wd_stt[prevk])
                                    pr.wait("g", "sV", wd_stt[prevk])
                                dma("y", qd[:, 0:nic * 256],
                                    dq[:, ic0:ic0 + nic,
                                       cp * 256:(cp + 1) * 256])
                                ldq = relay("y")
                                dma("g", sFd[:, 0:nic * 256],
                                    dsc[cp:cp + 1,
                                        ic0 * 256:(ic0 + nic) * 256]
                                    .partition_broadcast(128))
                                lds = relay("g")
                                pr.wait("v", *ldq)
                                pr.wait("v", *lds)
                                if cp >= 1 and hb == 0:
                                    pr.wait("v", *wd_str[cp - 1])
                                for ii in range(nic):
                                    icb = ic0 + ii
                                    pr.on("v", lambda v, ii=ii, icb=icb:
                                          v.scalar_tensor_tensor(
                                              wdp[:, icb * 256:
                                                  (icb + 1) * 256],
                                              qd[:, ii * 256:
                                                 (ii + 1) * 256],
                                              8.0,
                                              sFd[:, ii * 256:
                                                  (ii + 1) * 256],
                                              op0=ALU.subtract,
                                              op1=ALU.mult)
                                          .then_inc(SEM["sV"], 1))
                                    wd_stt[(cp, icb)] = pr.inc("sV")
                            pr.wait("a", "sV", wd_stt[(cp, IC - 1)])
                            dma("a", wd_bf[cp], wdp[:])
                            wd_str[cp] = relay("a")
                    pr.barrier()

                if sub < 4:
                    return
                for gi in range(2):
                    sg = ExitStack()
                    hTg = sg.enter_context(nc.sbuf_tensor(
                        f"hTg_{tag}{gi}", [128, IC * 512], BF16))
                    wdpan = sg.enter_context(nc.sbuf_tensor(
                        f"wdpan_{tag}{gi}", [128, 2 * IC * 256], BF16))
                    xstrip = sg.enter_context(nc.sbuf_tensor(
                        f"xs_{tag}{gi}", [128, 4 * 256], F32))
                    xnw = sg.enter_context(nc.sbuf_tensor(
                        f"xn_{tag}{gi}", [128, 4 * 256], F32))
                    with sg:
                        dma("y", hTg[:], hT_scr[gi])
                        mm_d = {}
                        ev_d = {}
                        st_d = {}
                        for cp in range(NCP):
                            slot = (cp % 2) * IC * 256
                            if cp >= 2:
                                pr.wait("y", "sP", mm_d[(cp - 2, 3)])
                            dma("y", wdpan[:, slot:slot + IC * 256],
                                wd_bf[cp])
                            ldw = relay("y")
                            for rt in range(4):
                                pr.wait("t", *ldw)
                                if cp >= 1:
                                    pr.wait("t", "sV", ev_d[(cp - 1, rt)])
                                for icb in range(IC):
                                    last = (icb == IC - 1)

                                    def mmd(t, icb=icb, rt=rt, slot=slot,
                                            last=last, hTg=hTg,
                                            wdpan=wdpan):
                                        ins = t.matmul(
                                            ps_a[:, rt * 512:
                                                 rt * 512 + 256],
                                            hTg[:, icb * 512 + rt * 128:
                                                icb * 512 +
                                                (rt + 1) * 128],
                                            wdpan[:, slot + icb * 256:
                                                  slot + (icb + 1) * 256],
                                            start=(icb == 0), stop=last)
                                        if last:
                                            ins.then_inc(SEM["sP"], 1)
                                    pr.on("t", mmd)
                                mm_d[(cp, rt)] = pr.inc("sP")
                                sx = rt
                                if cp >= 1:
                                    pr.wait("y", "sV", ev_d[(cp - 1, rt)])
                                dma("y", xstrip[:, sx * 256:
                                                (sx + 1) * 256],
                                    sstrip(gi, rt, cp))
                                lx = relay("y")
                                pr.wait("v", "sP", mm_d[(cp, rt)])
                                pr.wait("v", *lx)
                                if cp >= 1:
                                    pr.wait("v", *st_d[(cp - 1, rt)])
                                pr.on("v", lambda v, sx=sx, rt=rt,
                                      xnw=xnw, xstrip=xstrip:
                                      v.scalar_tensor_tensor(
                                          xnw[:, sx * 256:(sx + 1) * 256],
                                          ps_a[:, rt * 512:rt * 512 + 256],
                                          1.0,
                                          xstrip[:, sx * 256:
                                                 (sx + 1) * 256],
                                          op0=ALU.mult, op1=ALU.add)
                                      .then_inc(SEM["sV"], 1))
                                ev_d[(cp, rt)] = pr.inc("sV")
                                pr.wait("g", "sV", ev_d[(cp, rt)])
                                dma("g", dstrip(gi, rt, cp),
                                    xnw[:, sx * 256:(sx + 1) * 256], "dG")
                                st_d[(cp, rt)] = relay("g")
                    pr.barrier()

        def run_attn_qg(qg, kbuf, vbuf):
            sa = ExitStack()
            qbuf = sa.enter_context(nc.sbuf_tensor(f"qbuf{qg}",
                                                   [128, NQ * 512], BF16))
            cosq = sa.enter_context(nc.sbuf_tensor(f"cosq{qg}", [64, 512],
                                                   F32))
            sinq = sa.enter_context(nc.sbuf_tensor(f"sinq{qg}", [64, 512],
                                                   F32))
            with sa:
                dma("y", cosq[:], cq_in[:, QPBASE[qg]:QPBASE[qg] + 512])
                dma("y", sinq[:], sq_in[:, QPBASE[qg]:QPBASE[qg] + 512])
                relay("y")
                sq = ExitStack()
                ybig = sq.enter_context(nc.sbuf_tensor(f"ybig_q{qg}",
                                                       [128, HC * 512],
                                                       BF16))
                with sq:
                    sln2 = ExitStack()
                    xtile = sln2.enter_context(nc.sbuf_tensor(
                        f"xt_q{qg}", [128, H], F32))
                    ybuf = sln2.enter_context(nc.sbuf_tensor(
                        f"yb_q{qg}", [128, H], BF16))
                    with sln2:
                        for rt in range(4):
                            ln_tile(x_rows(qg, rt), ybig, 512, rt * 128,
                                    xtile, ybuf)
                        pr.barrier()
                    wqblk = sq.enter_context(nc.sbuf_tensor(
                        f"wqb{qg}", [128, 2 * HC * 256], BF16))
                    q_rope = {}
                    q_mm = {}
                    for kvh in range(NKV):
                        for hf in range(2):
                            blkid = kvh * 2 + hf
                            slot = (blkid % 2) * HC * 256
                            if blkid >= 2:
                                pr.wait("g", "sP", q_mm[blkid - 2])
                            dma("g", wqblk[:, slot:slot + HC * 256],
                                wq[blkid], "dG")
                            lw = relay("g")
                            for qh2 in range(2):
                                qh = kvh * 4 + hf * 2 + qh2
                                pb = ps_c if (qh % 2 == 0) else ps_d
                                pr.wait("t", *lw)
                                if qh >= 2:
                                    pr.wait("t", "sV", q_rope[qh - 2])
                                for hc in range(HC):
                                    last = (hc == HC - 1)

                                    def mq(t, pb=pb, hc=hc, slot=slot,
                                           qh2=qh2, last=last):
                                        ins = t.matmul(
                                            pb[:],
                                            wqblk[:, slot + hc * 256 +
                                                  qh2 * 128:
                                                  slot + hc * 256 +
                                                  (qh2 + 1) * 128],
                                            ybig[:, hc * 512:
                                                 (hc + 1) * 512],
                                            start=(hc == 0), stop=last)
                                        if last:
                                            ins.then_inc(SEM["sP"], 1)
                                    pr.on("t", mq)
                                q_mm[blkid] = pr.inc("sP")
                                pr.wait("v", "sP", q_mm[blkid])
                                co = qh * 512
                                pr.op("v", lambda v, pb=pb:
                                      v.tensor_mul(rt1[:], pb[0:64, :],
                                                   cosq[:]), selfwait=True)
                                pr.op("v", lambda v, pb=pb:
                                      v.tensor_mul(rt2[:], pb[64:128, :],
                                                   sinq[:]), selfwait=True)
                                pr.op("v", lambda v, pb=pb:
                                      v.tensor_mul(rt3[:], pb[64:128, :],
                                                   cosq[:]), selfwait=True)
                                pr.op("v", lambda v, pb=pb:
                                      v.tensor_mul(rt4[:], pb[0:64, :],
                                                   sinq[:]), selfwait=True)
                                pr.op("v", lambda v, co=co:
                                      v.tensor_sub(qbuf[0:64, co:co + 512],
                                                   rt1[:], rt2[:]),
                                      selfwait=True)
                                q_rope[qh] = pr.op(
                                    "v", lambda v, co=co:
                                    v.tensor_add(qbuf[64:128, co:co + 512],
                                                 rt3[:], rt4[:]),
                                    selfwait=True)
                    pr.barrier()

                sc2 = ExitStack()
                outT = sc2.enter_context(nc.sbuf_tensor(f"outT{qg}",
                                                        [128, NQ * 512],
                                                        BF16))
                sc3 = sc2.enter_context(ExitStack())
                maskg = sc3.enter_context(nc.sbuf_tensor(f"maskg{qg}",
                                                         [128, MC1], BF16))
                pbuf = sc3.enter_context(nc.sbuf_tensor(f"pbuf{qg}",
                                                        [128, 2048], BF16))
                ptb = sc3.enter_context(nc.sbuf_tensor(f"ptb{qg}",
                                                       [128, 16 * 512],
                                                       BF16))
                if True:
                    mcols = MC0 if qg == 0 else MC1
                    moff0 = 0 if qg == 0 else MC0
                    dma("y", maskg[:, 0:mcols],
                        cmask[:, moff0:moff0 + mcols])
                    vm = relay("y")
                    ptb3 = ptb[:].rearrange("p (kt q) -> p kt q", kt=16)
                    exp_dn = {}
                    ptc_dn = {}
                    ptm_dn = {}
                    av_ev = {}
                    v_sc = {}

                    def emit_S(kvh, qh, qt):
                        E = EMAX[qg][qt]
                        W = E * 128
                        nch = (W + 511) // 512
                        if (qh, qt - 1) in exp_dn:
                            pr.wait("t", "sA", exp_dn[(qh, qt - 1)])
                        elif qt == 0 and qh > 0:
                            pr.wait("t", "sA", exp_dn[(qh - 1, 3)])
                        for c in range(nch):
                            Wc = min(512, W - c * 512)
                            last = (c == nch - 1)

                            def ms(t, c=c, Wc=Wc, qh=qh, qt=qt, kvh=kvh,
                                   last=last):
                                ins = t.matmul(
                                    ps_a[:, c * 512:c * 512 + Wc],
                                    qbuf[:, qh * 512 + qt * 128:
                                         qh * 512 + (qt + 1) * 128],
                                    kbuf[:, kvh * S + c * 512:
                                         kvh * S + c * 512 + Wc],
                                    start=True, stop=True)
                                if last:
                                    ins.then_inc(SEM["sP"], 1)
                            pr.on("t", ms)
                        return pr.inc("sP")

                    def emit_soft(qh, qt, p_s):
                        E = EMAX[qg][qt]
                        W = E * 128
                        mo = MOFF[(qg, qt)]
                        pr.wait("v", "sP", p_s)
                        v_mx = pr.op(
                            "v", lambda v, W=W:
                            v.reduce_max(negm[:], ps_a[:, 0:W],
                                         axis=AX.X, negate=True),
                            selfwait=True)
                        pr.wait("a", "sV", v_mx)
                        if (qh, qt - 1) in ptm_dn:
                            pr.wait("a", "sP", ptm_dn[(qh, qt - 1)])
                        elif qt == 0 and qh > 0:
                            pr.wait("a", "sP", ptm_dn[(qh - 1, 3)])
                        exp_dn[(qh, qt)] = pr.op(
                            "a", lambda a, W=W:
                            a.activation(pbuf[:, 0:W], ps_a[:, 0:W],
                                         AF.Exp, bias=negm[:, 0:1]),
                            selfwait=True)
                        pr.wait("v", "sA", exp_dn[(qh, qt)])
                        pr.wait("v", *vm)
                        pr.op("v", lambda v, W=W, mo=mo:
                              v.scalar_tensor_tensor(
                                  pbuf[:, 0:W], pbuf[:, 0:W], 1.0,
                                  maskg[:, mo:mo + W],
                                  op0=ALU.mult, op1=ALU.mult,
                                  accum_out=dsumb[:]), selfwait=True)
                        pr.op("v", lambda v: v.reciprocal(rdsum[:],
                                                          dsumb[:]),
                              selfwait=True)
                        v_sc[(qh, qt)] = pr.op(
                            "v", lambda v, W=W:
                            v.tensor_scalar(pbuf[:, 0:W], pbuf[:, 0:W],
                                            rdsum[:, 0:1], None,
                                            op0=ALU.mult), selfwait=True)

                    def emit_PT(qh, qt):
                        E = EMAX[qg][qt]
                        pr.wait("t", "sV", v_sc[(qh, qt)])
                        if (qh, qt - 1) in ptc_dn:
                            pr.wait("t", "sV", ptc_dn[(qh, qt - 1)])
                        elif qt == 0 and qh > 0:
                            pr.wait("t", "sV", ptc_dn[(qh - 1, 3)])
                        if qt == 0 and qh > 0:
                            pr.wait("t", "sV", av_ev[qh - 1])
                        for kt in range(E):
                            pb = ps_b if kt < 8 else ps_b2
                            jj = kt % 8
                            last = (kt == E - 1)

                            def mt(t, pb=pb, jj=jj, kt=kt, last=last):
                                ins = t.transpose(
                                    pb[:, jj * 128:(jj + 1) * 128],
                                    pbuf[:, kt * 128:(kt + 1) * 128],
                                    ident[:])
                                if last:
                                    ins.then_inc(SEM["sP"], 1)
                            pr.on("t", mt)
                        ptm_dn[(qh, qt)] = pr.inc("sP")
                        pr.wait("v", "sP", ptm_dn[(qh, qt)])
                        n1 = min(E, 8)
                        pr.op("v", lambda v, n1=n1, qt=qt:
                              v.tensor_copy(
                                  ptb3[:, 0:n1, qt * 128:(qt + 1) * 128],
                                  ps_b[:, 0:n1 * 128]), selfwait=True)
                        if E > 8:
                            pr.op("v", lambda v, E=E, qt=qt:
                                  v.tensor_copy(
                                      ptb3[:, 8:E,
                                           qt * 128:(qt + 1) * 128],
                                      ps_b2[:, 0:(E - 8) * 128]),
                                  selfwait=True)
                        ptc_dn[(qh, qt)] = pr.cnt["sV"]

                    for kvh in range(NKV):
                        for qh4 in range(4):
                            qh = kvh * 4 + qh4
                            p_s = emit_S(kvh, qh, 0)
                            for qt in range(4):
                                emit_soft(qh, qt, p_s)
                                if qt < 3:
                                    p_s = emit_S(kvh, qh, qt + 1)
                                emit_PT(qh, qt)
                            pr.wait("t", "sV", ptc_dn[(qh, 3)])
                            if qh > 0:
                                pr.wait("t", "sV", av_ev[qh - 1])
                            Emax3 = EMAX[qg][3]
                            for kt in range(Emax3):
                                qtmin = 0 if qg == 0 else max(0, kt - 12)
                                off = qtmin * 128
                                last = (kt == Emax3 - 1)

                                def mav(t, kt=kt, off=off, kvh=kvh,
                                        last=last):
                                    ins = t.matmul(
                                        ps_c[:, off:512],
                                        vbuf[:, (kt * NKV + kvh) * 128:
                                             (kt * NKV + kvh + 1) * 128],
                                        ptb[:, kt * 512 + off:
                                            (kt + 1) * 512],
                                        start=(kt == 0), stop=last)
                                    if last:
                                        ins.then_inc(SEM["sP"], 1)
                                pr.on("t", mav)
                            av_mm = pr.inc("sP")
                            pr.wait("v", "sP", av_mm)
                            av_ev[qh] = pr.op(
                                "v", lambda v, qh=qh:
                                v.tensor_copy(outT[:, qh * 512:
                                                   (qh + 1) * 512],
                                              ps_c[:]), selfwait=True)
                    pr.barrier()
                sc3.close()

                sw = ExitStack()
                woblk = sw.enter_context(nc.sbuf_tensor(f"wob{qg}",
                                                        [128, 2 * HC * 256],
                                                        BF16))
                xstrip = sw.enter_context(nc.sbuf_tensor(f"xsw{qg}",
                                                         [128, 4 * 256],
                                                         F32))
                xnw = sw.enter_context(nc.sbuf_tensor(f"xnw{qg}",
                                                      [128, 4 * 256], F32))
                with sw:
                    mm_o = {}
                    ev_o = {}
                    st_o = {}
                    for cp in range(NCP):
                        slot = (cp % 2) * HC * 256
                        if cp >= 2:
                            pr.wait("g", "sP", mm_o[(cp - 2, 3)])
                        dma("g", woblk[:, slot:slot + HC * 256],
                            wo[cp], "dG")
                        lw = relay("g")
                        for rt in range(4):
                            pr.wait("t", *lw)
                            if cp >= 1:
                                pr.wait("t", "sV", ev_o[(cp - 1, rt)])
                            for hc in range(HC):
                                last = (hc == HC - 1)

                                def mo_(t, hc=hc, rt=rt, slot=slot,
                                        last=last):
                                    ins = t.matmul(
                                        ps_a[:, rt * 512:rt * 512 + 256],
                                        outT[:, hc * 512 + rt * 128:
                                             hc * 512 + (rt + 1) * 128],
                                        woblk[:, slot + hc * 256:
                                              slot + (hc + 1) * 256],
                                        start=(hc == 0), stop=last)
                                    if last:
                                        ins.then_inc(SEM["sP"], 1)
                                pr.on("t", mo_)
                            mm_o[(cp, rt)] = pr.inc("sP")
                            sx = rt
                            if cp >= 1:
                                pr.wait("y", "sV", ev_o[(cp - 1, rt)])
                            dma("y",
                                xstrip[:, sx * 256:(sx + 1) * 256],
                                x_strip_ap(qg, rt, cp))
                            lx = relay("y")
                            pr.wait("v", "sP", mm_o[(cp, rt)])
                            pr.wait("v", *lx)
                            if cp >= 1:
                                pr.wait("v", *st_o[(cp - 1, rt)])
                            pr.on("v", lambda v, sx=sx, rt=rt:
                                  v.scalar_tensor_tensor(
                                      xnw[:, sx * 256:(sx + 1) * 256],
                                      ps_a[:, rt * 512:rt * 512 + 256],
                                      1.0,
                                      xstrip[:, sx * 256:(sx + 1) * 256],
                                      op0=ALU.mult, op1=ALU.add)
                                  .then_inc(SEM["sV"], 1))
                            ev_o[(cp, rt)] = pr.inc("sV")
                            pr.wait("g", "sV", ev_o[(cp, rt)])
                            dma("g", x_strip_ap(qg, rt, cp),
                                xnw[:, sx * 256:(sx + 1) * 256], "dG")
                            st_o[(cp, rt)] = relay("g")
                sc2.close()
            pr.barrier()

        dma("y", ident[:], c_ident)
        dma("y", onesr[:], c_ones)
        dma("y", halfr[:], c_half)
        relay("y")
        pr.barrier()

        for pi in range(npass):
            r0 = pi * 1024

            def s512(gi, rt, r0=r0):
                a = r0 + gi * 512 + rt * 128
                return xin[a:a + 128, :]

            def sstrip(gi, rt, cp, r0=r0):
                a = r0 + gi * 512 + rt * 128
                return xin[a:a + 128, cp * 256:(cp + 1) * 256]

            def dstrip(gi, rt, cp, r0=r0):
                a = r0 + gi * 512 + rt * 128
                return x_scr[a:a + 128, cp * 256:(cp + 1) * 256]

            ffn_pass(f"f1{pi}", s512, sstrip, dstrip,
                     fused=(pi == 0), wd_fused=(pi == 0), sub=sub)

        if dump == "x1":
            dma("y", o_dbg, x_scr)
            pr.barrier()

        if phases >= 2:
            kv_outer = ExitStack()
            cosk = kv_outer.enter_context(nc.sbuf_tensor("cosk", [64, S],
                                                         F32))
            sink = kv_outer.enter_context(nc.sbuf_tensor("sink", [64, S],
                                                         F32))
            rt1 = kv_outer.enter_context(nc.sbuf_tensor("rt1", [64, 512],
                                                        F32))
            rt2 = kv_outer.enter_context(nc.sbuf_tensor("rt2", [64, 512],
                                                        F32))
            rt3 = kv_outer.enter_context(nc.sbuf_tensor("rt3", [64, 512],
                                                        F32))
            rt4 = kv_outer.enter_context(nc.sbuf_tensor("rt4", [64, 512],
                                                        F32))
            kbuf = kv_outer.enter_context(
                nc.sbuf_tensor("kbuf", [128, NKV * S], BF16))
            vbuf = kv_outer.enter_context(
                nc.sbuf_tensor("vbuf", [128, (S // 128) * NKV * DH], BF16))
            with kv_outer:
                dma("y", cosk[:], ck_in)
                dma("y", sink[:], sk_in)
                relay("y")
                sk = ExitStack()
                wkvbf = sk.enter_context(nc.sbuf_tensor(
                    "wkvbf", [128, HC * 512], BF16))
                ybig_kv = sk.enter_context(nc.sbuf_tensor(
                    "ybig_kv", [128, HC * 512], BF16))
                xt_kv = sk.enter_context(nc.sbuf_tensor("xt_kv", [128, H],
                                                        F32))
                yb_kv = sk.enter_context(nc.sbuf_tensor("yb_kv", [128, H],
                                                        BF16))
                with sk:
                    wkbf = wkvbf
                    wvbf = wkvbf
                    for ch in range(4):
                        for rt in range(4):
                            ln_tile(x_scr[ch * 512 + rt * 128:
                                          ch * 512 + (rt + 1) * 128, :],
                                    ybig_kv, 512, rt * 128, xt_kv, yb_kv)
                        pr.barrier()
                        rope_dn = {}
                        for kvh in range(NKV):
                            if kvh % 4 == 0:
                                kh = kvh // 4
                                if ch + kh > 0:
                                    pr.wait("g", "sP", pr.cnt["sP"])
                                dma("g", wkvbf[:], wk[kh], "dG")
                                lw = relay("g")
                                pr.wait("t", *lw)
                            kvl = kvh % 4
                            pb = ps_c if (kvh % 2 == 0) else ps_d
                            if kvh >= 2:
                                pr.wait("t", "sV", rope_dn[kvh - 2])
                            for hc in range(HC):
                                last = (hc == HC - 1)

                                def mk(t, pb=pb, hc=hc, kvl=kvl, last=last):
                                    ins = t.matmul(
                                        pb[:],
                                        wkvbf[:, hc * 512 + kvl * 128:
                                              hc * 512 + (kvl + 1) * 128],
                                        ybig_kv[:, hc * 512:
                                                (hc + 1) * 512],
                                        start=(hc == 0), stop=last)
                                    if last:
                                        ins.then_inc(SEM["sP"], 1)
                                pr.on("t", mk)
                            p_k = pr.inc("sP")
                            pr.wait("v", "sP", p_k)
                            co = kvh * S + ch * 512
                            cs = (ch * 512, (ch + 1) * 512)
                            pr.op("v", lambda v, pb=pb, cs=cs:
                                  v.tensor_mul(rt1[:], pb[0:64, :],
                                               cosk[:, cs[0]:cs[1]]),
                                  selfwait=True)
                            pr.op("v", lambda v, pb=pb, cs=cs:
                                  v.tensor_mul(rt2[:], pb[64:128, :],
                                               sink[:, cs[0]:cs[1]]),
                                  selfwait=True)
                            pr.op("v", lambda v, pb=pb, cs=cs:
                                  v.tensor_mul(rt3[:], pb[64:128, :],
                                               cosk[:, cs[0]:cs[1]]),
                                  selfwait=True)
                            pr.op("v", lambda v, pb=pb, cs=cs:
                                  v.tensor_mul(rt4[:], pb[0:64, :],
                                               sink[:, cs[0]:cs[1]]),
                                  selfwait=True)
                            pr.op("v", lambda v, co=co:
                                  v.tensor_sub(kbuf[0:64, co:co + 512],
                                               rt1[:], rt2[:]),
                                  selfwait=True)
                            rope_dn[kvh] = pr.op(
                                "v", lambda v, co=co:
                                v.tensor_add(kbuf[64:128, co:co + 512],
                                             rt3[:], rt4[:]),
                                selfwait=True)
                        vev = {}
                        for j in range(8):
                            dsp, kt = divmod(j, 4)
                            if kt == 0:
                                pr.wait("g", "sP", pr.cnt["sP"])
                                dma("g", wkvbf[:], wv[dsp], "dG")
                                lwv = relay("g")
                                pr.wait("t", *lwv)
                            pb = ps_c if (j % 2 == 0) else ps_d
                            if j >= 2:
                                pr.wait("t", "sV", vev[j - 2])
                            else:
                                pr.wait("t", "sV", rope_dn[NKV - 2 + j])
                            for hc in range(HC):
                                last = (hc == HC - 1)

                                def mv(t, pb=pb, hc=hc, kt=kt, last=last):
                                    ins = t.matmul(
                                        pb[:],
                                        ybig_kv[:, hc * 512 + kt * 128:
                                                hc * 512 +
                                                (kt + 1) * 128],
                                        wkvbf[:, hc * 512:(hc + 1) * 512],
                                        start=(hc == 0), stop=last)
                                    if last:
                                        ins.then_inc(SEM["sP"], 1)
                                pr.on("t", mv)
                            p_v = pr.inc("sP")
                            pr.wait("v", "sP", p_v)
                            ktg = ch * 4 + kt
                            vev[j] = pr.op(
                                "v", lambda v, pb=pb, ktg=ktg, dsp=dsp:
                                v.tensor_copy(
                                    vbuf[:, ktg * 1024 + dsp * 512:
                                         ktg * 1024 + (dsp + 1) * 512],
                                    pb[:]), selfwait=True)
                        pr.barrier()

                if phases >= 3:
                    for qg in range(2):
                        run_attn_qg(qg, kbuf, vbuf)

            if dump == "x2":
                dma("y", o_dbg, x_scr)
                pr.barrier()

            if phases >= 4:
                def s512(gi, rt):
                    return x_rows(gi, rt)

                def sstrip(gi, rt, cp):
                    return x_strip_ap(gi, rt, cp)

                def dstrip(gi, rt, cp):
                    a = gi * 512 + rt * 128
                    return xout[a:a + 128, cp * 256:(cp + 1) * 256]

                ffn_pass("f3", s512, sstrip, dstrip, fused=False,
                         wd_fused=False)

        pr.barrier()

        @block.tensor
        def _(t):
            for f in pr.ops["t"]:
                f(t)

        @block.vector
        def _(v):
            for f in pr.ops["v"]:
                f(v)

        @block.scalar
        def _(a):
            for f in pr.ops["a"]:
                f(a)

        @block.gpsimd
        def _(g):
            for f in pr.ops["g"]:
                f(g)

        @block.sync
        def _(y):
            for f in pr.ops["y"]:
                f(y)

    return nc


def prepare_in_maps(inp):
    x = np.ascontiguousarray(np.asarray(inp["L0_attn_x"]))
    gq_a = np.asarray(inp["L0_ffn_gate_q"])
    uq_a = np.asarray(inp["L0_ffn_up_q"])
    dq_a = np.asarray(inp["L0_ffn_down_q"])
    gs_a = np.asarray(inp["L0_ffn_gate_s"])
    us_a = np.asarray(inp["L0_ffn_up_s"])
    ds_a = np.asarray(inp["L0_ffn_down_s"])
    wq_a = np.asarray(inp["L0_attn_Wq"])
    wk_a = np.asarray(inp["L0_attn_Wk"])
    wv_a = np.asarray(inp["L0_attn_Wv"])
    wo_a = np.asarray(inp["L0_attn_Wo"])

    tables = {h: _host_tables(h) for h in (0, 1)}
    perms = {h: _perm_rows(h) for h in (0, 1)}

    gq_b = np.ascontiguousarray(
        gq_a.reshape(HC, 128, IC, 128).transpose(2, 1, 0, 3)
        .reshape(IC, 128, H))
    uq_b = np.ascontiguousarray(
        uq_a.reshape(HC, 128, IC, 128).transpose(2, 1, 0, 3)
        .reshape(IC, 128, H))
    dq_b = np.ascontiguousarray(
        dq_a.reshape(IC, 128, NCP, 256).transpose(1, 0, 2, 3)
        .reshape(128, IC, NCP * 256))
    gs_b = np.ascontiguousarray(
        gs_a.reshape(HC, IC, 128).transpose(1, 0, 2).reshape(IC, H))
    us_b = np.ascontiguousarray(
        us_a.reshape(HC, IC, 128).transpose(1, 0, 2).reshape(IC, H))
    ds_b = np.ascontiguousarray(
        ds_a.reshape(IC, NCP, 256).transpose(1, 0, 2).reshape(NCP, IC * 256))
    wq_b = np.ascontiguousarray(
        wq_a.reshape(HC, 128, 16, 256).transpose(2, 1, 0, 3)
        .reshape(16, 128, HC * 256))
    wk_b = np.ascontiguousarray(
        wk_a.reshape(HC, 128, 2, 512).transpose(2, 1, 0, 3)
        .reshape(2, 128, HC * 512))
    wv_b = np.ascontiguousarray(
        wv_a.reshape(HC, 128, 2, 512).transpose(2, 1, 0, 3)
        .reshape(2, 128, HC * 512))
    wo_b = np.ascontiguousarray(
        wo_a.reshape(HC, 128, 16, 256).transpose(2, 1, 0, 3)
        .reshape(16, 128, HC * 256))

    in_maps = []
    for c in range(8):
        b, h = c // 2, c % 2
        ck, sk, cq, sq, masks = tables[h]
        in_maps.append(dict(
            xin=np.ascontiguousarray(x[b][perms[h]]),
            gq=gq_b, uq=uq_b, dq=dq_b, gs=gs_b, us=us_b, dsc=ds_b,
            wq=wq_b, wk=wk_b, wv=wv_b, wo=wo_b,
            ck=ck, sk=sk, cq=cq, sq=sq, cmask=masks,
        ))
    return in_maps


def assemble_out(results):
    out = np.empty((4, S, H), np.float32)
    for c in range(8):
        b, h = c // 2, c % 2
        xo = results[c]["xout"]
        for qg in (0, 1):
            base = ROW_BASE[h][qg]
            out[b, base:base + 512] = xo[qg * 512:(qg + 1) * 512]
    return out


def kernel(L0_attn_x, L0_attn_Wq, L0_attn_Wk, L0_attn_Wv, L0_attn_Wo,
           L0_ffn_gate_q, L0_ffn_gate_s, L0_ffn_up_q, L0_ffn_up_s,
           L0_ffn_down_q, L0_ffn_down_s):
    inp = dict(L0_attn_x=L0_attn_x, L0_attn_Wq=L0_attn_Wq,
               L0_attn_Wk=L0_attn_Wk, L0_attn_Wv=L0_attn_Wv,
               L0_attn_Wo=L0_attn_Wo, L0_ffn_gate_q=L0_ffn_gate_q,
               L0_ffn_gate_s=L0_ffn_gate_s, L0_ffn_up_q=L0_ffn_up_q,
               L0_ffn_up_s=L0_ffn_up_s, L0_ffn_down_q=L0_ffn_down_q,
               L0_ffn_down_s=L0_ffn_down_s)
    in_maps = prepare_in_maps(inp)
    nc = build()
    res = run_bass_kernel_spmd(nc, in_maps, core_ids=list(range(8)))
    return assemble_out(res.results)


# version 07
# speedup vs baseline: 1.1402x; 1.1402x over previous
import numpy as np
import ml_dtypes
from contextlib import ExitStack
import concourse.bass as bass
import concourse.mybir as mybir
from concourse.bass_utils import run_bass_kernel_spmd

F32 = mybir.dt.float32
BF16 = mybir.dt.bfloat16
I32 = mybir.dt.int32
AF = mybir.ActivationFunctionType
ALU = mybir.AluOpType
AX = mybir.AxisListType

H = 4096
II = 11008
S = 2048
NKV = 8
NQ = 32
DH = 128
HC = H // 128
IC = II // 128
NCP = H // 256
EPS = 1e-5
RSQD = 1.0 / float(np.sqrt(DH))

EMAX = {0: [8, 8, 8, 8], 1: [13, 14, 15, 16]}
ROW_BASE = {0: (0, 1536), 1: (512, 1024)}
PERM_ABS = {0: [0, 1, 2, 3], 1: [1, 0, 3, 2]}
QPBASE = {0: 0, 1: 1536}


def _mask_offsets():
    offs = {}
    for qg in (0, 1):
        o = 0
        for qt in range(4):
            offs[(qg, qt)] = o
            o += EMAX[qg][qt] * 128
        offs[(qg, "total")] = o
    return offs


MOFF = _mask_offsets()
MC0 = MOFF[(0, "total")]
MC1 = MOFF[(1, "total")]


def _build_consts():
    ident = np.eye(128, dtype=ml_dtypes.bfloat16)
    ones_row = np.ones((1, 128), dtype=np.float32)
    half_row = np.full((1, 128), 0.5, dtype=np.float32)
    return ident, ones_row, half_row


def _perm_rows(half):
    p = np.arange(S)
    blk = p // 512
    return np.array([PERM_ABS[half][b] * 512 + (r % 512)
                     for r, b in zip(p, blk)], dtype=np.int64)


def _host_tables(half):
    inv = 1.0 / (10000.0 ** (np.arange(0, DH, 2, dtype=np.float64)[: DH // 2]
                             / DH))
    perm = _perm_rows(half)
    fr = perm[None, :].astype(np.float64) * inv[:, None]
    ck = np.cos(fr).astype(np.float32)
    sk = np.sin(fr).astype(np.float32)
    cq = (np.cos(fr) * RSQD).astype(np.float32)
    sq = (np.sin(fr) * RSQD).astype(np.float32)
    masks = np.zeros((128, MC0 + MC1), dtype=ml_dtypes.bfloat16)
    for qg in (0, 1):
        for qt in range(4):
            o = (0 if qg == 0 else MC0) + MOFF[(qg, qt)]
            E = EMAX[qg][qt] * 128
            qrows = perm[QPBASE[qg] + qt * 128 + np.arange(128)][:, None]
            krows = perm[np.arange(E)][None, :]
            masks[:, o: o + E] = (krows <= qrows).astype(ml_dtypes.bfloat16)
    return ck, sk, cq, sq, masks


NROT = 24


class Prog:
    ENGS = ("t", "v", "a", "g", "y")

    def __init__(self, sem):
        self.ops = {e: [] for e in self.ENGS}
        self.cnt = {k: 0 for k in sem}
        self.sem = sem
        self.rot = {"y": 0, "g": 0, "a": 0}
        self.fresh = {"y": False, "g": False, "a": False}
        self.gidx = 0

    QPFX = {"y": "dS%d", "g": "dG%d", "a": "dA%d"}

    def qsem(self, q):
        return self.QPFX[q] % self.rot[q]

    def on(self, e, fn):
        self.gidx += 1
        self.ops[e].append(fn)

    def inc(self, nm, n=1):
        self.cnt[nm] += n
        return self.cnt[nm]

    ESEM = {"t": "sP", "v": "sV", "a": "sA"}

    def wait(self, e, nm, v):
        if v <= 0:
            return
        s = self.sem[nm]
        self.ops[e].append(lambda eng, s=s, v=v: eng.wait_ge(s, v))

    def op(self, e, fn, selfwait=False):
        nm = self.ESEM[e]
        if selfwait and self.cnt[nm] > 0:
            self.wait(e, nm, self.cnt[nm])
        s = self.sem[nm]
        self.on(e, lambda eng, fn=fn, s=s: fn(eng).then_inc(s, 1))
        return self.inc(nm)

    def barrier(self):
        snap = dict(self.cnt)
        for e in self.ENGS:
            for nm, v in snap.items():
                self.wait(e, nm, v)


def build(dump=None, phases=99, sub=99, npass=2):
    nc = bass.Bass("TRN2", target_bir_lowering=False)

    xin = nc.dram_tensor("xin", [S, H], F32, kind="ExternalInput").ap()
    gq = nc.dram_tensor("gq", [IC, 128, H], I32, kind="ExternalInput").ap()
    uq = nc.dram_tensor("uq", [IC, 128, H], I32, kind="ExternalInput").ap()
    dq = nc.dram_tensor("dq", [128, IC, 256 * NCP], I32,
                        kind="ExternalInput").ap()
    gs = nc.dram_tensor("gs", [IC, H], F32, kind="ExternalInput").ap()
    us = nc.dram_tensor("us", [IC, H], F32, kind="ExternalInput").ap()
    dsc = nc.dram_tensor("dsc", [NCP, IC * 256], F32,
                         kind="ExternalInput").ap()
    wq = nc.dram_tensor("wq", [16, 128, HC * 256], F32,
                        kind="ExternalInput").ap()
    wk = nc.dram_tensor("wk", [2, 128, HC * 512], F32,
                        kind="ExternalInput").ap()
    wv = nc.dram_tensor("wv", [2, 128, HC * 512], F32,
                        kind="ExternalInput").ap()
    wo = nc.dram_tensor("wo", [16, 128, HC * 256], F32,
                        kind="ExternalInput").ap()
    ck_in = nc.dram_tensor("ck", [64, S], F32, kind="ExternalInput").ap()
    sk_in = nc.dram_tensor("sk", [64, S], F32, kind="ExternalInput").ap()
    cq_in = nc.dram_tensor("cq", [64, S], F32, kind="ExternalInput").ap()
    sq_in = nc.dram_tensor("sq", [64, S], F32, kind="ExternalInput").ap()
    cmask = nc.dram_tensor("cmask", [128, MC0 + MC1], BF16,
                           kind="ExternalInput").ap()

    xout = nc.dram_tensor("xout", [1024, H], F32, kind="ExternalOutput").ap()
    o_dbg = None
    if dump is not None:
        o_dbg = nc.dram_tensor("o_dbg", [S, H], F32,
                               kind="ExternalOutput").ap()

    ident_np, ones_np, half_np = _build_consts()
    c_ident = nc.inline_tensor(ident_np, name="c_ident").ap()
    c_ones = nc.inline_tensor(ones_np, name="c_ones").ap()
    c_half = nc.inline_tensor(half_np, name="c_half").ap()

    wg_bf = nc.dram_tensor("wg_bf", [IC, 128, H], BF16, kind="Internal").ap()
    wu_bf = nc.dram_tensor("wu_bf", [IC, 128, H], BF16, kind="Internal").ap()
    wd_bf = nc.dram_tensor("wd_bf", [NCP, 128, IC * 256], BF16,
                           kind="Internal").ap()
    hT_scr = nc.dram_tensor("hT_scr", [2, 128, IC * 512], BF16,
                            kind="Internal").ap()
    x_scr = nc.dram_tensor("x_scr", [S, H], F32, kind="Internal").ap()

    es = ExitStack()
    with es:
        sbt = lambda n, sh, dt: es.enter_context(nc.sbuf_tensor(n, sh, dt))
        ident = sbt("ident", [128, 128], BF16)
        onesr = sbt("onesr", [1, 128], F32)
        halfr = sbt("halfr", [1, 128], F32)
        st1 = sbt("st1", [128, 1], F32)
        st2 = sbt("st2", [128, 1], F32)
        st3 = sbt("st3", [128, 1], F32)
        st4 = sbt("st4", [128, 1], F32)
        dsumb = sbt("dsumb", [128, 1], F32)
        rdsum = sbt("rdsum", [128, 1], F32)
        negm = sbt("negm", [128, 1], F32)

        ps_a = es.enter_context(nc.psum_tensor("ps_a", [128, 2048], F32))
        ps_b = es.enter_context(nc.psum_tensor("ps_b", [128, 1024], BF16))
        ps_b2 = es.enter_context(nc.psum_tensor("ps_b2", [128, 1024], BF16))
        ps_c = es.enter_context(nc.psum_tensor("ps_c", [128, 512], F32))
        ps_d = es.enter_context(nc.psum_tensor("ps_d", [128, 512], F32))

        sem = {}
        semnames = ["sP", "sV", "sA"]
        semnames += ["dS%d" % i for i in range(NROT)]
        semnames += ["dG%d" % i for i in range(NROT)]
        semnames += ["dA%d" % i for i in range(NROT)]
        for nm in semnames:
            sem[nm] = es.enter_context(nc.semaphore(nm))
        block = es.enter_context(nc.Block())
        pr = Prog(sem)
        SEM = sem

        def dma(q, out, in_, nm=None):
            nm = pr.qsem(q)
            v = pr.inc(nm, 16)
            s = sem[nm]
            pr.fresh[q] = True

            def go(e, out=out, in_=in_, s=s):
                o = out(e) if callable(out) else out
                i = in_(e) if callable(in_) else in_
                e.dma_start(out=o, in_=i).then_inc(s, 16)
            pr.on(q, go)
            return (nm, v)

        def relay(q):
            nm = pr.qsem(q)
            h = (nm, pr.cnt[nm])
            if pr.fresh[q]:
                pr.rot[q] = (pr.rot[q] + 1) % NROT
                pr.fresh[q] = False
            return h

        def x_rows(qg, rt):
            a = QPBASE[qg] + rt * 128
            return x_scr[a:a + 128, :]

        def x_strip_ap(qg, rt, cp):
            a = QPBASE[qg] + rt * 128
            return x_scr[a:a + 128, cp * 256:(cp + 1) * 256]

        ln_prev = {"ay": 0, "tr": 0}

        def ln_tile(src_rows, ybig, R, col0, xtile, ybuf):
            pr.wait("y", "sV", ln_prev["ay"])
            dma("y", xtile[:], src_rows)
            vload = relay("y")
            pr.wait("v", *vload)
            pr.op("v", lambda v: v.reduce_sum(st1[:], xtile[:], axis=AX.X,
                                              negate=True), selfwait=True)
            v_mu = pr.op("v", lambda v: v.tensor_scalar(
                st1[:], st1[:], 1.0 / H, None, op0=ALU.mult), selfwait=True)
            pr.wait("a", "sV", v_mu)
            pr.wait("a", *vload)
            pr.wait("a", "sP", ln_prev["tr"])
            a_sq = pr.op("a", lambda a: a.activation(
                ybuf[:], xtile[:], AF.Square, accum_out=st2[:]),
                selfwait=True)
            pr.wait("v", "sA", a_sq)
            pr.op("v", lambda v: v.tensor_mul(st3[:], st1[:], st1[:]),
                  selfwait=True)
            pr.op("v", lambda v: v.scalar_tensor_tensor(
                st3[:], st2[:], 1.0 / H, st3[:], op0=ALU.mult,
                op1=ALU.subtract), selfwait=True)
            v_var = pr.op("v", lambda v: v.tensor_scalar(
                st3[:], st3[:], EPS, None, op0=ALU.add), selfwait=True)
            pr.wait("a", "sV", v_var)
            a_ln = pr.op("a", lambda a: a.activation(st2[:], st3[:], AF.Ln),
                         selfwait=True)
            pr.wait("v", "sA", a_ln)
            v_nh = pr.op("v", lambda v: v.tensor_scalar(
                st3[:], st2[:], -0.5, None, op0=ALU.mult), selfwait=True)
            pr.wait("a", "sV", v_nh)
            a_rstd = pr.op("a", lambda a: a.activation(st2[:], st3[:],
                                                       AF.Exp),
                           selfwait=True)
            pr.wait("v", "sA", a_rstd)
            a_y = pr.op("v", lambda v: v.tensor_scalar(
                ybuf[:], xtile[:], st1[:, 0:1], st2[:, 0:1],
                op0=ALU.add, op1=ALU.mult), selfwait=True)
            pr.wait("t", "sV", a_y)
            cps = []
            y3 = ybig[:].rearrange("p (hc r) -> p hc r", hc=HC)
            for g8 in range(4):
                pb = ps_b if (g8 % 2 == 0) else ps_b2
                if g8 >= 2:
                    pr.wait("t", "sV", cps[g8 - 2])
                for j in range(8):
                    hc = g8 * 8 + j
                    if j < 7:
                        pr.on("t", lambda t, pb=pb, j=j, hc=hc:
                              t.transpose(pb[:, j * 128:(j + 1) * 128],
                                          ybuf[:, hc * 128:(hc + 1) * 128],
                                          ident[:]))
                    else:
                        pr.op("t", lambda t, pb=pb, j=j, hc=hc:
                              t.transpose(pb[:, j * 128:(j + 1) * 128],
                                          ybuf[:, hc * 128:(hc + 1) * 128],
                                          ident[:]))
                p_tr = pr.cnt["sP"]
                pr.wait("v", "sP", p_tr)
                cps.append(pr.op("v", lambda v, pb=pb, g8=g8, y3=y3,
                                 col0=col0:
                                 v.tensor_copy(
                                     y3[:, g8 * 8:(g8 + 1) * 8,
                                        col0:col0 + 128], pb[:]),
                                 selfwait=True))
            ln_prev["ay"] = a_y
            ln_prev["tr"] = pr.cnt["sP"]

        def ffn_pass(tag, src512, sstrip, dstrip, fused, wd_fused,
                     sub=99):
            sc = ExitStack()
            ybig = sc.enter_context(nc.sbuf_tensor(f"ybig_{tag}",
                                                   [128, HC * 1024], BF16))
            with sc:
                sgu = ExitStack()
                wblk = sgu.enter_context(nc.sbuf_tensor(f"wblk_{tag}",
                                                        [128, 4 * H], BF16))

                hstage = sgu.enter_context(nc.sbuf_tensor(f"hst_{tag}",
                                                          [128, 1024], BF16))
                silu_t = sgu.enter_context(nc.sbuf_tensor(f"sil_{tag}",
                                                          [128, 1024], BF16))
                with sgu:
                    if sub < 1:
                        return
                    sln = ExitStack()
                    xtile = sln.enter_context(nc.sbuf_tensor(
                        f"xt_{tag}", [128, H], F32))
                    ybuf = sln.enter_context(nc.sbuf_tensor(
                        f"yb_{tag}", [128, H], BF16))
                    with sln:
                        for gi in range(2):
                            for rt in range(4):
                                ln_tile(src512(gi, rt), ybig, 1024,
                                        gi * 512 + rt * 128, xtile, ybuf)
                        pr.barrier()
                    qint = sgu.enter_context(nc.sbuf_tensor(f"qint_{tag}",
                                                            [128, 3 * H],
                                                            I32))
                    sF = sgu.enter_context(nc.sbuf_tensor(f"sF_{tag}",
                                                          [128, 2 * H],
                                                          F32))
                    if sub < 2:
                        return
                    sttj = {}
                    stt_last = {}
                    mm_done = {}
                    ev_done = {}
                    h_store = {}
                    w_load = {}
                    scr_w = {}
                    for ib in range(IC):
                        slot = (ib % 2) * 2 * H
                        if fused:
                            ldq_m = {}
                            qs_m = {}
                            flc = {}
                            for m in range(2):
                                qs_m[m] = ((2 * ib + m) % 3) * H
                                if 2 * ib + m >= 3:
                                    pm = 2 * ib + m - 3
                                    pr.wait("y", "sV",
                                            sttj[(pm // 2,
                                                  (pm % 2) * 8 + 7)])
                                dma("y", qint[:, qs_m[m]:qs_m[m] + H],
                                    (gq if m == 0 else uq)[ib])
                                ldq_m[m] = relay("y")
                            for m in range(2):
                                src = gs if m == 0 else us
                                for c in range(4):
                                    if ib >= 1:
                                        pr.wait("g", "sV",
                                                sttj[(ib - 1,
                                                      m * 8 + 2 * c + 1)])
                                    dma("g", sF[:, m * H + c * 1024:
                                                m * H + (c + 1) * 1024],
                                        src[ib:ib + 1,
                                            c * 1024:(c + 1) * 1024]
                                        .partition_broadcast(128), "dG")
                                    flc[(m, c)] = relay("g")
                            if ib >= 2:
                                pr.wait("t", "sV", ev_done[(ib - 2) * 2 + 1])
                            for m in range(2):
                                for sp in range(8):
                                    j = m * 8 + sp
                                    if sp == 0:
                                        pr.wait("v", *ldq_m[m])
                                    if sp % 2 == 0:
                                        pr.wait("v", *flc[(m, sp // 2)])
                                    if j == 0 and ib >= 2:
                                        pr.wait("v", "sP",
                                                mm_done[(ib - 2) * 2
                                                        + 1])
                                        if (ib - 2) in scr_w:
                                            pr.wait("v",
                                                    *scr_w[ib - 2])
                                    pr.on("v", lambda v, m=m, sp=sp,
                                          slot=slot, qs=qs_m[m]:
                                          v.scalar_tensor_tensor(
                                              wblk[:, slot + m * H +
                                                   sp * 512:
                                                   slot + m * H +
                                                   (sp + 1) * 512],
                                              qint[:, qs + sp * 512:
                                                   qs + (sp + 1) * 512],
                                              8.0,
                                              sF[:, m * H + sp * 512:
                                                 m * H + (sp + 1) * 512],
                                              op0=ALU.subtract,
                                              op1=ALU.mult)
                                          .then_inc(SEM["sV"], 1))
                                    sttj[(ib, j)] = pr.inc("sV")
                            stt_last[ib] = sttj[(ib, 15)]
                            pr.wait("a", "sV", stt_last[ib])
                            dma("a", wg_bf[ib], wblk[:, slot:slot + H])
                            dma("a", wu_bf[ib],
                                wblk[:, slot + H:slot + 2 * H])
                            scr_w[ib] = relay("a")
                        else:
                            if ib >= 2:
                                pr.wait("y", "sP", mm_done[(ib - 2) * 2 + 1])
                            dma("y", wblk[:, slot:slot + H], wg_bf[ib])
                            dma("y", wblk[:, slot + H:slot + 2 * H],
                                wu_bf[ib])
                            w_load[ib] = relay("y")
                        for rc in range(2):
                            step = ib * 2 + rc
                            setb = (step % 2) * 1024
                            if fused:
                                pr.wait("t", "sV", stt_last[ib])
                            else:
                                pr.wait("t", *w_load[ib])
                            if step >= 2:
                                pr.wait("t", "sV", ev_done[step - 2])
                            for m in range(2):
                                for hc in range(HC):
                                    last = (hc == HC - 1)

                                    def mmf(t, m=m, hc=hc, slot=slot,
                                            setb=setb, rc=rc, last=last):
                                        ins = t.matmul(
                                            ps_a[:, setb + m * 512:
                                                 setb + (m + 1) * 512],
                                            wblk[:, slot + m * H + hc * 128:
                                                 slot + m * H +
                                                 (hc + 1) * 128],
                                            ybig[:, hc * 1024 + rc * 512:
                                                 hc * 1024 +
                                                 (rc + 1) * 512],
                                            start=(hc == 0), stop=last)
                                        if last:
                                            ins.then_inc(SEM["sP"], 1)
                                    pr.on("t", mmf)
                                pr.inc("sP")
                            mm_done[step] = pr.cnt["sP"]
                            par = step % 2
                            pr.wait("a", "sP", mm_done[step])
                            if step >= 2:
                                pr.wait("a", *h_store[step - 2])
                            pr.on("a", lambda a, setb=setb, par=par:
                                  a.activation(silu_t[:, par * 512:
                                                      (par + 1) * 512],
                                               ps_a[:, setb:setb + 512],
                                               AF.Silu)
                                  .then_inc(SEM["sA"], 1))
                            a_si = pr.inc("sA")
                            pr.wait("v", "sA", a_si)
                            if step >= 2:
                                pr.wait("v", *h_store[step - 2])
                            pr.on("v", lambda v, setb=setb, par=par:
                                  v.scalar_tensor_tensor(
                                      hstage[:, par * 512:(par + 1) * 512],
                                      silu_t[:, par * 512:(par + 1) * 512],
                                      0.5,
                                      ps_a[:, setb + 512:setb + 1024],
                                      op0=ALU.mult, op1=ALU.mult)
                                  .then_inc(SEM["sV"], 1))
                            ev_done[step] = pr.inc("sV")
                            pr.wait("a", "sV", ev_done[step])
                            dma("a",
                                hT_scr[rc, :, ib * 512:(ib + 1) * 512],
                                hstage[:, par * 512:(par + 1) * 512])
                            h_store[step] = relay("a")
                    pr.barrier()
                sc.close()

                if sub < 3:
                    return
                if wd_fused:
                    sd = ExitStack()
                    wdq = (IC + 3) // 4
                    qd = sd.enter_context(nc.sbuf_tensor(f"qd_{tag}",
                                                         [128,
                                                          2 * wdq * 256],
                                                         I32))
                    wdp = sd.enter_context(nc.sbuf_tensor(f"wdp_{tag}",
                                                          [128, IC * 256],
                                                          BF16))
                    srd = sd.enter_context(nc.sbuf_tensor(f"srd_{tag}",
                                                          [1, 2 * wdq * 256],
                                                          F32))
                    with sd:
                        import itertools
                        qpick = itertools.cycle(("y", "g", "a"))
                        wd_bcq = {}
                        wd_str = {}
                        slot_prev = {0: 0, 1: 0}
                        gpair = 0
                        for cp in range(NCP):
                            for qt4 in range(4):
                                ic0 = qt4 * wdq
                                nic = min(wdq, IC - ic0)
                                if nic <= 0:
                                    continue
                                qs = (qt4 % 2) * wdq * 256
                                qq = next(qpick)
                                pr.wait(qq, "sV", slot_prev[qt4 % 2])
                                dma(qq, qd[:, qs:qs + nic * 256],
                                    dq[:, ic0:ic0 + nic,
                                       cp * 256:(cp + 1) * 256])
                                dma(qq, srd[:, qs:qs + nic * 256],
                                    dsc[cp:cp + 1,
                                        ic0 * 256:(ic0 + nic) * 256])
                                ldq = relay(qq)
                                pr.wait("t", *ldq)
                                for pj in range((nic + 1) // 2):
                                    np2 = min(2, nic - pj * 2)
                                    pb = ps_c if (gpair % 2 == 0) else ps_d
                                    if gpair >= 2 and (gpair - 2) in wd_bcq:
                                        pr.wait("t", "sV", wd_bcq[gpair - 2])
                                    pr.on("t", lambda t, pb=pb, qs=qs,
                                          pj=pj, np2=np2:
                                          t.matmul(pb[:, 0:np2 * 256],
                                                   onesr[:],
                                                   srd[0:1,
                                                       qs + pj * 512:
                                                       qs + pj * 512 +
                                                       np2 * 256],
                                                   start=True, stop=True)
                                          .then_inc(SEM["sP"], 1))
                                    p_bc = pr.inc("sP")
                                    pr.wait("v", "sP", p_bc)
                                    if pj == 0:
                                        pr.wait("v", *ldq)
                                        if cp >= 1 and qt4 == 0:
                                            pr.wait("v", *wd_str[cp - 1])
                                    for s2 in range(np2):
                                        ii = pj * 2 + s2
                                        icb = ic0 + ii
                                        pr.on("v", lambda v, pb=pb, qs=qs,
                                              ii=ii, icb=icb, s2=s2:
                                              v.scalar_tensor_tensor(
                                                  wdp[:, icb * 256:
                                                      (icb + 1) * 256],
                                                  qd[:, qs + ii * 256:
                                                     qs + (ii + 1) * 256],
                                                  8.0,
                                                  pb[:, s2 * 256:
                                                     (s2 + 1) * 256],
                                                  op0=ALU.subtract,
                                                  op1=ALU.mult)
                                              .then_inc(SEM["sV"], 1))
                                        last_stt = pr.inc("sV")
                                    wd_bcq[gpair] = last_stt
                                    gpair += 1
                                slot_prev[qt4 % 2] = last_stt
                            sq4 = next(qpick)
                            pr.wait(sq4, "sV", last_stt)
                            dma(sq4, wd_bf[cp], wdp[:])
                            wd_str[cp] = relay(sq4)
                    pr.barrier()

                if sub < 4:
                    return
                for gi in range(2):
                    sg = ExitStack()
                    hTg = sg.enter_context(nc.sbuf_tensor(
                        f"hTg_{tag}{gi}", [128, IC * 512], BF16))
                    wdpan = sg.enter_context(nc.sbuf_tensor(
                        f"wdpan_{tag}{gi}", [128, 2 * IC * 256], BF16))
                    xstrip = sg.enter_context(nc.sbuf_tensor(
                        f"xs_{tag}{gi}", [128, 4 * 256], F32))
                    xnw = sg.enter_context(nc.sbuf_tensor(
                        f"xn_{tag}{gi}", [128, 4 * 256], F32))
                    with sg:
                        dma("y", hTg[:], hT_scr[gi])
                        mm_d = {}
                        ev_d = {}
                        st_d = {}
                        for cp in range(NCP):
                            slot = (cp % 2) * IC * 256
                            if cp >= 2:
                                pr.wait("y", "sP", mm_d[(cp - 2, 3)])
                            dma("y", wdpan[:, slot:slot + IC * 256],
                                wd_bf[cp])
                            ldw = relay("y")
                            for rt in range(4):
                                pr.wait("t", *ldw)
                                if cp >= 1:
                                    pr.wait("t", "sV", ev_d[(cp - 1, rt)])
                                for icb in range(IC):
                                    last = (icb == IC - 1)

                                    def mmd(t, icb=icb, rt=rt, slot=slot,
                                            last=last, hTg=hTg,
                                            wdpan=wdpan):
                                        ins = t.matmul(
                                            ps_a[:, rt * 512:
                                                 rt * 512 + 256],
                                            hTg[:, icb * 512 + rt * 128:
                                                icb * 512 +
                                                (rt + 1) * 128],
                                            wdpan[:, slot + icb * 256:
                                                  slot + (icb + 1) * 256],
                                            start=(icb == 0), stop=last)
                                        if last:
                                            ins.then_inc(SEM["sP"], 1)
                                    pr.on("t", mmd)
                                mm_d[(cp, rt)] = pr.inc("sP")
                                sx = rt
                                if cp >= 1:
                                    pr.wait("y", "sV", ev_d[(cp - 1, rt)])
                                dma("y", xstrip[:, sx * 256:
                                                (sx + 1) * 256],
                                    sstrip(gi, rt, cp))
                                lx = relay("y")
                                pr.wait("v", "sP", mm_d[(cp, rt)])
                                pr.wait("v", *lx)
                                if cp >= 1:
                                    pr.wait("v", *st_d[(cp - 1, rt)])
                                pr.on("v", lambda v, sx=sx, rt=rt,
                                      xnw=xnw, xstrip=xstrip:
                                      v.scalar_tensor_tensor(
                                          xnw[:, sx * 256:(sx + 1) * 256],
                                          ps_a[:, rt * 512:rt * 512 + 256],
                                          1.0,
                                          xstrip[:, sx * 256:
                                                 (sx + 1) * 256],
                                          op0=ALU.mult, op1=ALU.add)
                                      .then_inc(SEM["sV"], 1))
                                ev_d[(cp, rt)] = pr.inc("sV")
                                pr.wait("g", "sV", ev_d[(cp, rt)])
                                dma("g", dstrip(gi, rt, cp),
                                    xnw[:, sx * 256:(sx + 1) * 256], "dG")
                                st_d[(cp, rt)] = relay("g")
                    pr.barrier()

        def run_attn_qg(qg, kbuf, vbuf):
            sa = ExitStack()
            qbuf = sa.enter_context(nc.sbuf_tensor(f"qbuf{qg}",
                                                   [128, NQ * 512], BF16))
            cosq = sa.enter_context(nc.sbuf_tensor(f"cosq{qg}", [64, 512],
                                                   F32))
            sinq = sa.enter_context(nc.sbuf_tensor(f"sinq{qg}", [64, 512],
                                                   F32))
            with sa:
                dma("y", cosq[:], cq_in[:, QPBASE[qg]:QPBASE[qg] + 512])
                dma("y", sinq[:], sq_in[:, QPBASE[qg]:QPBASE[qg] + 512])
                relay("y")
                sq = ExitStack()
                ybig = sq.enter_context(nc.sbuf_tensor(f"ybig_q{qg}",
                                                       [128, HC * 512],
                                                       BF16))
                with sq:
                    sln2 = ExitStack()
                    xtile = sln2.enter_context(nc.sbuf_tensor(
                        f"xt_q{qg}", [128, H], F32))
                    ybuf = sln2.enter_context(nc.sbuf_tensor(
                        f"yb_q{qg}", [128, H], BF16))
                    with sln2:
                        for rt in range(4):
                            ln_tile(x_rows(qg, rt), ybig, 512, rt * 128,
                                    xtile, ybuf)
                        pr.barrier()
                    wqblk = sq.enter_context(nc.sbuf_tensor(
                        f"wqb{qg}", [128, 2 * HC * 256], BF16))
                    q_rope = {}
                    q_mm = {}
                    for kvh in range(NKV):
                        for hf in range(2):
                            blkid = kvh * 2 + hf
                            slot = (blkid % 2) * HC * 256
                            if blkid >= 2:
                                pr.wait("g", "sP", q_mm[blkid - 2])
                            dma("g", wqblk[:, slot:slot + HC * 256],
                                wq[blkid], "dG")
                            lw = relay("g")
                            for qh2 in range(2):
                                qh = kvh * 4 + hf * 2 + qh2
                                pb = ps_c if (qh % 2 == 0) else ps_d
                                pr.wait("t", *lw)
                                if qh >= 2:
                                    pr.wait("t", "sV", q_rope[qh - 2])
                                for hc in range(HC):
                                    last = (hc == HC - 1)

                                    def mq(t, pb=pb, hc=hc, slot=slot,
                                           qh2=qh2, last=last):
                                        ins = t.matmul(
                                            pb[:],
                                            wqblk[:, slot + hc * 256 +
                                                  qh2 * 128:
                                                  slot + hc * 256 +
                                                  (qh2 + 1) * 128],
                                            ybig[:, hc * 512:
                                                 (hc + 1) * 512],
                                            start=(hc == 0), stop=last)
                                        if last:
                                            ins.then_inc(SEM["sP"], 1)
                                    pr.on("t", mq)
                                q_mm[blkid] = pr.inc("sP")
                                pr.wait("v", "sP", q_mm[blkid])
                                co = qh * 512
                                pr.op("v", lambda v, pb=pb:
                                      v.tensor_mul(rt1[:], pb[0:64, :],
                                                   cosq[:]), selfwait=True)
                                pr.op("v", lambda v, pb=pb:
                                      v.tensor_mul(rt2[:], pb[64:128, :],
                                                   sinq[:]), selfwait=True)
                                pr.op("v", lambda v, pb=pb:
                                      v.tensor_mul(rt3[:], pb[64:128, :],
                                                   cosq[:]), selfwait=True)
                                pr.op("v", lambda v, pb=pb:
                                      v.tensor_mul(rt4[:], pb[0:64, :],
                                                   sinq[:]), selfwait=True)
                                pr.op("v", lambda v, co=co:
                                      v.tensor_sub(qbuf[0:64, co:co + 512],
                                                   rt1[:], rt2[:]),
                                      selfwait=True)
                                q_rope[qh] = pr.op(
                                    "v", lambda v, co=co:
                                    v.tensor_add(qbuf[64:128, co:co + 512],
                                                 rt3[:], rt4[:]),
                                    selfwait=True)
                    pr.barrier()

                sc2 = ExitStack()
                outT = sc2.enter_context(nc.sbuf_tensor(f"outT{qg}",
                                                        [128, NQ * 512],
                                                        BF16))
                sc3 = sc2.enter_context(ExitStack())
                maskg = sc3.enter_context(nc.sbuf_tensor(f"maskg{qg}",
                                                         [128, MC1], BF16))
                pbuf = sc3.enter_context(nc.sbuf_tensor(f"pbuf{qg}",
                                                        [128, 2048], BF16))
                ptb = sc3.enter_context(nc.sbuf_tensor(f"ptb{qg}",
                                                       [128, 16 * 512],
                                                       BF16))
                if True:
                    mcols = MC0 if qg == 0 else MC1
                    moff0 = 0 if qg == 0 else MC0
                    dma("y", maskg[:, 0:mcols],
                        cmask[:, moff0:moff0 + mcols])
                    vm = relay("y")
                    ptb3 = ptb[:].rearrange("p (kt q) -> p kt q", kt=16)
                    exp_dn = {}
                    ptc_dn = {}
                    ptm_dn = {}
                    av_ev = {}
                    v_sc = {}

                    def emit_S(kvh, qh, qt):
                        E = EMAX[qg][qt]
                        W = E * 128
                        nch = (W + 511) // 512
                        if (qh, qt - 1) in exp_dn:
                            pr.wait("t", "sA", exp_dn[(qh, qt - 1)])
                        elif qt == 0 and qh > 0:
                            pr.wait("t", "sA", exp_dn[(qh - 1, 3)])
                        for c in range(nch):
                            Wc = min(512, W - c * 512)
                            last = (c == nch - 1)

                            def ms(t, c=c, Wc=Wc, qh=qh, qt=qt, kvh=kvh,
                                   last=last):
                                ins = t.matmul(
                                    ps_a[:, c * 512:c * 512 + Wc],
                                    qbuf[:, qh * 512 + qt * 128:
                                         qh * 512 + (qt + 1) * 128],
                                    kbuf[:, kvh * S + c * 512:
                                         kvh * S + c * 512 + Wc],
                                    start=True, stop=True)
                                if last:
                                    ins.then_inc(SEM["sP"], 1)
                            pr.on("t", ms)
                        return pr.inc("sP")

                    def emit_soft(qh, qt, p_s):
                        E = EMAX[qg][qt]
                        W = E * 128
                        mo = MOFF[(qg, qt)]
                        pr.wait("v", "sP", p_s)
                        v_mx = pr.op(
                            "v", lambda v, W=W:
                            v.reduce_max(negm[:], ps_a[:, 0:W],
                                         axis=AX.X, negate=True),
                            selfwait=True)
                        pr.wait("a", "sV", v_mx)
                        if (qh, qt - 1) in ptm_dn:
                            pr.wait("a", "sP", ptm_dn[(qh, qt - 1)])
                        elif qt == 0 and qh > 0:
                            pr.wait("a", "sP", ptm_dn[(qh - 1, 3)])
                        exp_dn[(qh, qt)] = pr.op(
                            "a", lambda a, W=W:
                            a.activation(pbuf[:, 0:W], ps_a[:, 0:W],
                                         AF.Exp, bias=negm[:, 0:1]),
                            selfwait=True)
                        pr.wait("v", "sA", exp_dn[(qh, qt)])
                        pr.wait("v", *vm)
                        pr.op("v", lambda v, W=W, mo=mo:
                              v.scalar_tensor_tensor(
                                  pbuf[:, 0:W], pbuf[:, 0:W], 1.0,
                                  maskg[:, mo:mo + W],
                                  op0=ALU.mult, op1=ALU.mult,
                                  accum_out=dsumb[:]), selfwait=True)
                        pr.op("v", lambda v: v.reciprocal(rdsum[:],
                                                          dsumb[:]),
                              selfwait=True)
                        v_sc[(qh, qt)] = pr.op(
                            "v", lambda v, W=W:
                            v.tensor_scalar(pbuf[:, 0:W], pbuf[:, 0:W],
                                            rdsum[:, 0:1], None,
                                            op0=ALU.mult), selfwait=True)

                    def emit_PT(qh, qt):
                        E = EMAX[qg][qt]
                        pr.wait("t", "sV", v_sc[(qh, qt)])
                        if (qh, qt - 1) in ptc_dn:
                            pr.wait("t", "sV", ptc_dn[(qh, qt - 1)])
                        elif qt == 0 and qh > 0:
                            pr.wait("t", "sV", ptc_dn[(qh - 1, 3)])
                        if qt == 0 and qh > 0:
                            pr.wait("t", "sV", av_ev[qh - 1])
                        for kt in range(E):
                            pb = ps_b if kt < 8 else ps_b2
                            jj = kt % 8
                            last = (kt == E - 1)

                            def mt(t, pb=pb, jj=jj, kt=kt, last=last):
                                ins = t.transpose(
                                    pb[:, jj * 128:(jj + 1) * 128],
                                    pbuf[:, kt * 128:(kt + 1) * 128],
                                    ident[:])
                                if last:
                                    ins.then_inc(SEM["sP"], 1)
                            pr.on("t", mt)
                        ptm_dn[(qh, qt)] = pr.inc("sP")
                        pr.wait("v", "sP", ptm_dn[(qh, qt)])
                        n1 = min(E, 8)
                        pr.op("v", lambda v, n1=n1, qt=qt:
                              v.tensor_copy(
                                  ptb3[:, 0:n1, qt * 128:(qt + 1) * 128],
                                  ps_b[:, 0:n1 * 128]), selfwait=True)
                        if E > 8:
                            pr.op("v", lambda v, E=E, qt=qt:
                                  v.tensor_copy(
                                      ptb3[:, 8:E,
                                           qt * 128:(qt + 1) * 128],
                                      ps_b2[:, 0:(E - 8) * 128]),
                                  selfwait=True)
                        ptc_dn[(qh, qt)] = pr.cnt["sV"]

                    for kvh in range(NKV):
                        for qh4 in range(4):
                            qh = kvh * 4 + qh4
                            p_s = emit_S(kvh, qh, 0)
                            for qt in range(4):
                                emit_soft(qh, qt, p_s)
                                if qt < 3:
                                    p_s = emit_S(kvh, qh, qt + 1)
                                emit_PT(qh, qt)
                            pr.wait("t", "sV", ptc_dn[(qh, 3)])
                            if qh > 0:
                                pr.wait("t", "sV", av_ev[qh - 1])
                            Emax3 = EMAX[qg][3]
                            for kt in range(Emax3):
                                qtmin = 0 if qg == 0 else max(0, kt - 12)
                                off = qtmin * 128
                                last = (kt == Emax3 - 1)

                                def mav(t, kt=kt, off=off, kvh=kvh,
                                        last=last):
                                    ins = t.matmul(
                                        ps_c[:, off:512],
                                        vbuf[:, (kt * NKV + kvh) * 128:
                                             (kt * NKV + kvh + 1) * 128],
                                        ptb[:, kt * 512 + off:
                                            (kt + 1) * 512],
                                        start=(kt == 0), stop=last)
                                    if last:
                                        ins.then_inc(SEM["sP"], 1)
                                pr.on("t", mav)
                            av_mm = pr.inc("sP")
                            pr.wait("v", "sP", av_mm)
                            av_ev[qh] = pr.op(
                                "v", lambda v, qh=qh:
                                v.tensor_copy(outT[:, qh * 512:
                                                   (qh + 1) * 512],
                                              ps_c[:]), selfwait=True)
                    pr.barrier()
                sc3.close()

                sw = ExitStack()
                woblk = sw.enter_context(nc.sbuf_tensor(f"wob{qg}",
                                                        [128, 2 * HC * 256],
                                                        BF16))
                xstrip = sw.enter_context(nc.sbuf_tensor(f"xsw{qg}",
                                                         [128, 4 * 256],
                                                         F32))
                xnw = sw.enter_context(nc.sbuf_tensor(f"xnw{qg}",
                                                      [128, 4 * 256], F32))
                with sw:
                    mm_o = {}
                    ev_o = {}
                    st_o = {}
                    for cp in range(NCP):
                        slot = (cp % 2) * HC * 256
                        if cp >= 2:
                            pr.wait("g", "sP", mm_o[(cp - 2, 3)])
                        dma("g", woblk[:, slot:slot + HC * 256],
                            wo[cp], "dG")
                        lw = relay("g")
                        for rt in range(4):
                            pr.wait("t", *lw)
                            if cp >= 1:
                                pr.wait("t", "sV", ev_o[(cp - 1, rt)])
                            for hc in range(HC):
                                last = (hc == HC - 1)

                                def mo_(t, hc=hc, rt=rt, slot=slot,
                                        last=last):
                                    ins = t.matmul(
                                        ps_a[:, rt * 512:rt * 512 + 256],
                                        outT[:, hc * 512 + rt * 128:
                                             hc * 512 + (rt + 1) * 128],
                                        woblk[:, slot + hc * 256:
                                              slot + (hc + 1) * 256],
                                        start=(hc == 0), stop=last)
                                    if last:
                                        ins.then_inc(SEM["sP"], 1)
                                pr.on("t", mo_)
                            mm_o[(cp, rt)] = pr.inc("sP")
                            sx = rt
                            if cp >= 1:
                                pr.wait("y", "sV", ev_o[(cp - 1, rt)])
                            dma("y",
                                xstrip[:, sx * 256:(sx + 1) * 256],
                                x_strip_ap(qg, rt, cp))
                            lx = relay("y")
                            pr.wait("v", "sP", mm_o[(cp, rt)])
                            pr.wait("v", *lx)
                            if cp >= 1:
                                pr.wait("v", *st_o[(cp - 1, rt)])
                            pr.on("v", lambda v, sx=sx, rt=rt:
                                  v.scalar_tensor_tensor(
                                      xnw[:, sx * 256:(sx + 1) * 256],
                                      ps_a[:, rt * 512:rt * 512 + 256],
                                      1.0,
                                      xstrip[:, sx * 256:(sx + 1) * 256],
                                      op0=ALU.mult, op1=ALU.add)
                                  .then_inc(SEM["sV"], 1))
                            ev_o[(cp, rt)] = pr.inc("sV")
                            pr.wait("g", "sV", ev_o[(cp, rt)])
                            dma("g", x_strip_ap(qg, rt, cp),
                                xnw[:, sx * 256:(sx + 1) * 256], "dG")
                            st_o[(cp, rt)] = relay("g")
                sc2.close()
            pr.barrier()

        dma("y", ident[:], c_ident)
        dma("y", onesr[:], c_ones)
        dma("y", halfr[:], c_half)
        relay("y")
        pr.barrier()

        for pi in range(npass):
            r0 = pi * 1024

            def s512(gi, rt, r0=r0):
                a = r0 + gi * 512 + rt * 128
                return xin[a:a + 128, :]

            def sstrip(gi, rt, cp, r0=r0):
                a = r0 + gi * 512 + rt * 128
                return xin[a:a + 128, cp * 256:(cp + 1) * 256]

            def dstrip(gi, rt, cp, r0=r0):
                a = r0 + gi * 512 + rt * 128
                return x_scr[a:a + 128, cp * 256:(cp + 1) * 256]

            ffn_pass(f"f1{pi}", s512, sstrip, dstrip,
                     fused=(pi == 0), wd_fused=(pi == 0), sub=sub)

        if dump == "x1":
            dma("y", o_dbg, x_scr)
            pr.barrier()

        if phases >= 2:
            kv_outer = ExitStack()
            cosk = kv_outer.enter_context(nc.sbuf_tensor("cosk", [64, S],
                                                         F32))
            sink = kv_outer.enter_context(nc.sbuf_tensor("sink", [64, S],
                                                         F32))
            rt1 = kv_outer.enter_context(nc.sbuf_tensor("rt1", [64, 512],
                                                        F32))
            rt2 = kv_outer.enter_context(nc.sbuf_tensor("rt2", [64, 512],
                                                        F32))
            rt3 = kv_outer.enter_context(nc.sbuf_tensor("rt3", [64, 512],
                                                        F32))
            rt4 = kv_outer.enter_context(nc.sbuf_tensor("rt4", [64, 512],
                                                        F32))
            kbuf = kv_outer.enter_context(
                nc.sbuf_tensor("kbuf", [128, NKV * S], BF16))
            vbuf = kv_outer.enter_context(
                nc.sbuf_tensor("vbuf", [128, (S // 128) * NKV * DH], BF16))
            with kv_outer:
                dma("y", cosk[:], ck_in)
                dma("y", sink[:], sk_in)
                relay("y")
                sk = ExitStack()
                wkvbf = sk.enter_context(nc.sbuf_tensor(
                    "wkvbf", [128, HC * 512], BF16))
                ybig_kv = sk.enter_context(nc.sbuf_tensor(
                    "ybig_kv", [128, HC * 512], BF16))
                xt_kv = sk.enter_context(nc.sbuf_tensor("xt_kv", [128, H],
                                                        F32))
                yb_kv = sk.enter_context(nc.sbuf_tensor("yb_kv", [128, H],
                                                        BF16))
                with sk:
                    wkbf = wkvbf
                    wvbf = wkvbf
                    for ch in range(4):
                        for rt in range(4):
                            ln_tile(x_scr[ch * 512 + rt * 128:
                                          ch * 512 + (rt + 1) * 128, :],
                                    ybig_kv, 512, rt * 128, xt_kv, yb_kv)
                        pr.barrier()
                        rope_dn = {}
                        for kvh in range(NKV):
                            if kvh % 4 == 0:
                                kh = kvh // 4
                                if ch + kh > 0:
                                    pr.wait("g", "sP", pr.cnt["sP"])
                                dma("g", wkvbf[:], wk[kh], "dG")
                                lw = relay("g")
                                pr.wait("t", *lw)
                            kvl = kvh % 4
                            pb = ps_c if (kvh % 2 == 0) else ps_d
                            if kvh >= 2:
                                pr.wait("t", "sV", rope_dn[kvh - 2])
                            for hc in range(HC):
                                last = (hc == HC - 1)

                                def mk(t, pb=pb, hc=hc, kvl=kvl, last=last):
                                    ins = t.matmul(
                                        pb[:],
                                        wkvbf[:, hc * 512 + kvl * 128:
                                              hc * 512 + (kvl + 1) * 128],
                                        ybig_kv[:, hc * 512:
                                                (hc + 1) * 512],
                                        start=(hc == 0), stop=last)
                                    if last:
                                        ins.then_inc(SEM["sP"], 1)
                                pr.on("t", mk)
                            p_k = pr.inc("sP")
                            pr.wait("v", "sP", p_k)
                            co = kvh * S + ch * 512
                            cs = (ch * 512, (ch + 1) * 512)
                            pr.op("v", lambda v, pb=pb, cs=cs:
                                  v.tensor_mul(rt1[:], pb[0:64, :],
                                               cosk[:, cs[0]:cs[1]]),
                                  selfwait=True)
                            pr.op("v", lambda v, pb=pb, cs=cs:
                                  v.tensor_mul(rt2[:], pb[64:128, :],
                                               sink[:, cs[0]:cs[1]]),
                                  selfwait=True)
                            pr.op("v", lambda v, pb=pb, cs=cs:
                                  v.tensor_mul(rt3[:], pb[64:128, :],
                                               cosk[:, cs[0]:cs[1]]),
                                  selfwait=True)
                            pr.op("v", lambda v, pb=pb, cs=cs:
                                  v.tensor_mul(rt4[:], pb[0:64, :],
                                               sink[:, cs[0]:cs[1]]),
                                  selfwait=True)
                            pr.op("v", lambda v, co=co:
                                  v.tensor_sub(kbuf[0:64, co:co + 512],
                                               rt1[:], rt2[:]),
                                  selfwait=True)
                            rope_dn[kvh] = pr.op(
                                "v", lambda v, co=co:
                                v.tensor_add(kbuf[64:128, co:co + 512],
                                             rt3[:], rt4[:]),
                                selfwait=True)
                        vev = {}
                        for j in range(8):
                            dsp, kt = divmod(j, 4)
                            if kt == 0:
                                pr.wait("g", "sP", pr.cnt["sP"])
                                dma("g", wkvbf[:], wv[dsp], "dG")
                                lwv = relay("g")
                                pr.wait("t", *lwv)
                            pb = ps_c if (j % 2 == 0) else ps_d
                            if j >= 2:
                                pr.wait("t", "sV", vev[j - 2])
                            else:
                                pr.wait("t", "sV", rope_dn[NKV - 2 + j])
                            for hc in range(HC):
                                last = (hc == HC - 1)

                                def mv(t, pb=pb, hc=hc, kt=kt, last=last):
                                    ins = t.matmul(
                                        pb[:],
                                        ybig_kv[:, hc * 512 + kt * 128:
                                                hc * 512 +
                                                (kt + 1) * 128],
                                        wkvbf[:, hc * 512:(hc + 1) * 512],
                                        start=(hc == 0), stop=last)
                                    if last:
                                        ins.then_inc(SEM["sP"], 1)
                                pr.on("t", mv)
                            p_v = pr.inc("sP")
                            pr.wait("v", "sP", p_v)
                            ktg = ch * 4 + kt
                            vev[j] = pr.op(
                                "v", lambda v, pb=pb, ktg=ktg, dsp=dsp:
                                v.tensor_copy(
                                    vbuf[:, ktg * 1024 + dsp * 512:
                                         ktg * 1024 + (dsp + 1) * 512],
                                    pb[:]), selfwait=True)
                        pr.barrier()

                if phases >= 3:
                    for qg in range(2):
                        run_attn_qg(qg, kbuf, vbuf)

            if dump == "x2":
                dma("y", o_dbg, x_scr)
                pr.barrier()

            if phases >= 4:
                def s512(gi, rt):
                    return x_rows(gi, rt)

                def sstrip(gi, rt, cp):
                    return x_strip_ap(gi, rt, cp)

                def dstrip(gi, rt, cp):
                    a = gi * 512 + rt * 128
                    return xout[a:a + 128, cp * 256:(cp + 1) * 256]

                ffn_pass("f3", s512, sstrip, dstrip, fused=False,
                         wd_fused=False)

        pr.barrier()

        @block.tensor
        def _(t):
            for f in pr.ops["t"]:
                f(t)

        @block.vector
        def _(v):
            for f in pr.ops["v"]:
                f(v)

        @block.scalar
        def _(a):
            for f in pr.ops["a"]:
                f(a)

        @block.gpsimd
        def _(g):
            for f in pr.ops["g"]:
                f(g)

        @block.sync
        def _(y):
            for f in pr.ops["y"]:
                f(y)

    return nc


def prepare_in_maps(inp):
    x = np.ascontiguousarray(np.asarray(inp["L0_attn_x"]))
    gq_a = np.asarray(inp["L0_ffn_gate_q"])
    uq_a = np.asarray(inp["L0_ffn_up_q"])
    dq_a = np.asarray(inp["L0_ffn_down_q"])
    gs_a = np.asarray(inp["L0_ffn_gate_s"])
    us_a = np.asarray(inp["L0_ffn_up_s"])
    ds_a = np.asarray(inp["L0_ffn_down_s"])
    wq_a = np.asarray(inp["L0_attn_Wq"])
    wk_a = np.asarray(inp["L0_attn_Wk"])
    wv_a = np.asarray(inp["L0_attn_Wv"])
    wo_a = np.asarray(inp["L0_attn_Wo"])

    tables = {h: _host_tables(h) for h in (0, 1)}
    perms = {h: _perm_rows(h) for h in (0, 1)}

    gq_b = np.ascontiguousarray(
        gq_a.reshape(HC, 128, IC, 128).transpose(2, 1, 0, 3)
        .reshape(IC, 128, H))
    uq_b = np.ascontiguousarray(
        uq_a.reshape(HC, 128, IC, 128).transpose(2, 1, 0, 3)
        .reshape(IC, 128, H))
    dq_b = np.ascontiguousarray(
        dq_a.reshape(IC, 128, NCP, 256).transpose(1, 0, 2, 3)
        .reshape(128, IC, NCP * 256))
    gs_b = np.ascontiguousarray(
        gs_a.reshape(HC, IC, 128).transpose(1, 0, 2).reshape(IC, H))
    us_b = np.ascontiguousarray(
        us_a.reshape(HC, IC, 128).transpose(1, 0, 2).reshape(IC, H))
    ds_b = np.ascontiguousarray(
        ds_a.reshape(IC, NCP, 256).transpose(1, 0, 2).reshape(NCP, IC * 256))
    wq_b = np.ascontiguousarray(
        wq_a.reshape(HC, 128, 16, 256).transpose(2, 1, 0, 3)
        .reshape(16, 128, HC * 256))
    wk_b = np.ascontiguousarray(
        wk_a.reshape(HC, 128, 2, 512).transpose(2, 1, 0, 3)
        .reshape(2, 128, HC * 512))
    wv_b = np.ascontiguousarray(
        wv_a.reshape(HC, 128, 2, 512).transpose(2, 1, 0, 3)
        .reshape(2, 128, HC * 512))
    wo_b = np.ascontiguousarray(
        wo_a.reshape(HC, 128, 16, 256).transpose(2, 1, 0, 3)
        .reshape(16, 128, HC * 256))

    in_maps = []
    for c in range(8):
        b, h = c // 2, c % 2
        ck, sk, cq, sq, masks = tables[h]
        in_maps.append(dict(
            xin=np.ascontiguousarray(x[b][perms[h]]),
            gq=gq_b, uq=uq_b, dq=dq_b, gs=gs_b, us=us_b, dsc=ds_b,
            wq=wq_b, wk=wk_b, wv=wv_b, wo=wo_b,
            ck=ck, sk=sk, cq=cq, sq=sq, cmask=masks,
        ))
    return in_maps


def assemble_out(results):
    out = np.empty((4, S, H), np.float32)
    for c in range(8):
        b, h = c // 2, c % 2
        xo = results[c]["xout"]
        for qg in (0, 1):
            base = ROW_BASE[h][qg]
            out[b, base:base + 512] = xo[qg * 512:(qg + 1) * 512]
    return out


def kernel(L0_attn_x, L0_attn_Wq, L0_attn_Wk, L0_attn_Wv, L0_attn_Wo,
           L0_ffn_gate_q, L0_ffn_gate_s, L0_ffn_up_q, L0_ffn_up_s,
           L0_ffn_down_q, L0_ffn_down_s):
    inp = dict(L0_attn_x=L0_attn_x, L0_attn_Wq=L0_attn_Wq,
               L0_attn_Wk=L0_attn_Wk, L0_attn_Wv=L0_attn_Wv,
               L0_attn_Wo=L0_attn_Wo, L0_ffn_gate_q=L0_ffn_gate_q,
               L0_ffn_gate_s=L0_ffn_gate_s, L0_ffn_up_q=L0_ffn_up_q,
               L0_ffn_up_s=L0_ffn_up_s, L0_ffn_down_q=L0_ffn_down_q,
               L0_ffn_down_s=L0_ffn_down_s)
    in_maps = prepare_in_maps(inp)
    nc = build()
    res = run_bass_kernel_spmd(nc, in_maps, core_ids=list(range(8)))
    return assemble_out(res.results)


# version 08
# speedup vs baseline: 1.2132x; 1.0641x over previous
import numpy as np
import ml_dtypes
from contextlib import ExitStack
import concourse.bass as bass
import concourse.mybir as mybir
from concourse.bass_utils import run_bass_kernel_spmd

F32 = mybir.dt.float32
BF16 = mybir.dt.bfloat16
I32 = mybir.dt.int32
AF = mybir.ActivationFunctionType
ALU = mybir.AluOpType
AX = mybir.AxisListType

H = 4096
II = 11008
S = 2048
NKV = 8
NQ = 32
DH = 128
HC = H // 128
IC = II // 128
NCP = H // 256
EPS = 1e-5
RSQD = 1.0 / float(np.sqrt(DH))

EMAX = {0: [8, 8, 8, 8], 1: [13, 14, 15, 16]}
ROW_BASE = {0: (0, 1536), 1: (512, 1024)}
PERM_ABS = {0: [0, 1, 2, 3], 1: [1, 0, 3, 2]}
QPBASE = {0: 0, 1: 1536}


def _mask_offsets():
    offs = {}
    for qg in (0, 1):
        o = 0
        for qt in range(4):
            offs[(qg, qt)] = o
            o += EMAX[qg][qt] * 128
        offs[(qg, "total")] = o
    return offs


MOFF = _mask_offsets()
MC0 = MOFF[(0, "total")]
MC1 = MOFF[(1, "total")]


def _build_consts():
    ident = np.eye(128, dtype=ml_dtypes.bfloat16)
    ones_row = np.ones((1, 128), dtype=np.float32)
    half_row = np.full((1, 128), 0.5, dtype=np.float32)
    return ident, ones_row, half_row


def _perm_rows(half):
    p = np.arange(S)
    blk = p // 512
    return np.array([PERM_ABS[half][b] * 512 + (r % 512)
                     for r, b in zip(p, blk)], dtype=np.int64)


def _host_tables(half):
    inv = 1.0 / (10000.0 ** (np.arange(0, DH, 2, dtype=np.float64)[: DH // 2]
                             / DH))
    perm = _perm_rows(half)
    fr = perm[None, :].astype(np.float64) * inv[:, None]
    ck = np.cos(fr).astype(np.float32)
    sk = np.sin(fr).astype(np.float32)
    cq = (np.cos(fr) * RSQD).astype(np.float32)
    sq = (np.sin(fr) * RSQD).astype(np.float32)
    masks = np.zeros((128, MC0 + MC1), dtype=ml_dtypes.bfloat16)
    for qg in (0, 1):
        for qt in range(4):
            o = (0 if qg == 0 else MC0) + MOFF[(qg, qt)]
            E = EMAX[qg][qt] * 128
            qrows = perm[QPBASE[qg] + qt * 128 + np.arange(128)][:, None]
            krows = perm[np.arange(E)][None, :]
            masks[:, o: o + E] = (krows <= qrows).astype(ml_dtypes.bfloat16)
    return ck, sk, cq, sq, masks


NROT = 24


class Prog:
    ENGS = ("t", "v", "a", "g", "y")

    def __init__(self, sem):
        self.ops = {e: [] for e in self.ENGS}
        self.cnt = {k: 0 for k in sem}
        self.sem = sem
        self.rot = {"y": 0, "g": 0, "a": 0}
        self.fresh = {"y": False, "g": False, "a": False}
        self.gidx = 0

    QPFX = {"y": "dS%d", "g": "dG%d", "a": "dA%d"}

    def qsem(self, q):
        return self.QPFX[q] % self.rot[q]

    def on(self, e, fn):
        self.gidx += 1
        self.ops[e].append(fn)

    def inc(self, nm, n=1):
        self.cnt[nm] += n
        return self.cnt[nm]

    ESEM = {"t": "sP", "v": "sV", "a": "sA"}

    def wait(self, e, nm, v):
        if v <= 0:
            return
        s = self.sem[nm]
        self.ops[e].append(lambda eng, s=s, v=v: eng.wait_ge(s, v))

    def op(self, e, fn, selfwait=False):
        nm = self.ESEM[e]
        if selfwait and self.cnt[nm] > 0:
            self.wait(e, nm, self.cnt[nm])
        s = self.sem[nm]
        self.on(e, lambda eng, fn=fn, s=s: fn(eng).then_inc(s, 1))
        return self.inc(nm)

    def barrier(self):
        snap = dict(self.cnt)
        for e in self.ENGS:
            for nm, v in snap.items():
                self.wait(e, nm, v)


def build(dump=None, phases=99, sub=99, npass=2):
    nc = bass.Bass("TRN2", target_bir_lowering=False)

    xin = nc.dram_tensor("xin", [S, H], F32, kind="ExternalInput").ap()
    gq = nc.dram_tensor("gq", [IC, 128, H], I32, kind="ExternalInput").ap()
    uq = nc.dram_tensor("uq", [IC, 128, H], I32, kind="ExternalInput").ap()
    dq = nc.dram_tensor("dq", [128, IC, 256 * NCP], I32,
                        kind="ExternalInput").ap()
    gs = nc.dram_tensor("gs", [IC, H], F32, kind="ExternalInput").ap()
    us = nc.dram_tensor("us", [IC, H], F32, kind="ExternalInput").ap()
    dsc = nc.dram_tensor("dsc", [NCP, IC * 256], F32,
                         kind="ExternalInput").ap()
    wq = nc.dram_tensor("wq", [16, 128, HC * 256], F32,
                        kind="ExternalInput").ap()
    wk = nc.dram_tensor("wk", [2, 128, HC * 512], F32,
                        kind="ExternalInput").ap()
    wv = nc.dram_tensor("wv", [2, 128, HC * 512], F32,
                        kind="ExternalInput").ap()
    wo = nc.dram_tensor("wo", [16, 128, HC * 256], F32,
                        kind="ExternalInput").ap()
    ck_in = nc.dram_tensor("ck", [64, S], F32, kind="ExternalInput").ap()
    sk_in = nc.dram_tensor("sk", [64, S], F32, kind="ExternalInput").ap()
    cq_in = nc.dram_tensor("cq", [64, S], F32, kind="ExternalInput").ap()
    sq_in = nc.dram_tensor("sq", [64, S], F32, kind="ExternalInput").ap()
    cmask = nc.dram_tensor("cmask", [128, MC0 + MC1], BF16,
                           kind="ExternalInput").ap()

    xout = nc.dram_tensor("xout", [1024, H], F32, kind="ExternalOutput").ap()
    o_dbg = None
    if dump is not None:
        o_dbg = nc.dram_tensor("o_dbg", [S, H], F32,
                               kind="ExternalOutput").ap()

    ident_np, ones_np, half_np = _build_consts()
    c_ident = nc.inline_tensor(ident_np, name="c_ident").ap()
    c_ones = nc.inline_tensor(ones_np, name="c_ones").ap()
    c_half = nc.inline_tensor(half_np, name="c_half").ap()

    wg_bf = nc.dram_tensor("wg_bf", [IC, 128, H], BF16, kind="Internal").ap()
    wu_bf = nc.dram_tensor("wu_bf", [IC, 128, H], BF16, kind="Internal").ap()
    wd_bf = nc.dram_tensor("wd_bf", [NCP, 128, IC * 256], BF16,
                           kind="Internal").ap()
    hT_scr = nc.dram_tensor("hT_scr", [2, 128, IC * 512], BF16,
                            kind="Internal").ap()
    x_scr = nc.dram_tensor("x_scr", [S, H], F32, kind="Internal").ap()

    es = ExitStack()
    with es:
        sbt = lambda n, sh, dt: es.enter_context(nc.sbuf_tensor(n, sh, dt))
        ident = sbt("ident", [128, 128], BF16)
        onesr = sbt("onesr", [1, 128], F32)
        halfr = sbt("halfr", [1, 128], F32)
        st1 = sbt("st1", [128, 1], F32)
        st2 = sbt("st2", [128, 1], F32)
        st3 = sbt("st3", [128, 1], F32)
        st4 = sbt("st4", [128, 1], F32)
        dsumb = sbt("dsumb", [128, 1], F32)
        rdsum = sbt("rdsum", [128, 1], F32)
        negm = sbt("negm", [128, 1], F32)

        ps_a = es.enter_context(nc.psum_tensor("ps_a", [128, 2048], F32))
        ps_b = es.enter_context(nc.psum_tensor("ps_b", [128, 1024], BF16))
        ps_b2 = es.enter_context(nc.psum_tensor("ps_b2", [128, 1024], BF16))
        ps_c = es.enter_context(nc.psum_tensor("ps_c", [128, 512], F32))
        ps_d = es.enter_context(nc.psum_tensor("ps_d", [128, 512], F32))

        sem = {}
        semnames = ["sP", "sV", "sA"]
        semnames += ["dS%d" % i for i in range(NROT)]
        semnames += ["dG%d" % i for i in range(NROT)]
        semnames += ["dA%d" % i for i in range(NROT)]
        for nm in semnames:
            sem[nm] = es.enter_context(nc.semaphore(nm))
        block = es.enter_context(nc.Block())
        pr = Prog(sem)
        SEM = sem

        def dma(q, out, in_, nm=None):
            nm = pr.qsem(q)
            v = pr.inc(nm, 16)
            s = sem[nm]
            pr.fresh[q] = True

            def go(e, out=out, in_=in_, s=s):
                o = out(e) if callable(out) else out
                i = in_(e) if callable(in_) else in_
                e.dma_start(out=o, in_=i).then_inc(s, 16)
            pr.on(q, go)
            return (nm, v)

        def relay(q):
            nm = pr.qsem(q)
            h = (nm, pr.cnt[nm])
            if pr.fresh[q]:
                pr.rot[q] = (pr.rot[q] + 1) % NROT
                pr.fresh[q] = False
            return h

        def x_rows(qg, rt):
            a = QPBASE[qg] + rt * 128
            return x_scr[a:a + 128, :]

        def x_strip_ap(qg, rt, cp):
            a = QPBASE[qg] + rt * 128
            return x_scr[a:a + 128, cp * 256:(cp + 1) * 256]

        ln_prev = {"ay": 0, "tr": 0}

        def ln_tile(src_rows, ybig, R, col0, xtile, ybuf):
            pr.wait("y", "sV", ln_prev["ay"])
            dma("y", xtile[:], src_rows)
            vload = relay("y")
            pr.wait("v", *vload)
            pr.op("v", lambda v: v.reduce_sum(st1[:], xtile[:], axis=AX.X,
                                              negate=True), selfwait=True)
            v_mu = pr.op("v", lambda v: v.tensor_scalar(
                st1[:], st1[:], 1.0 / H, None, op0=ALU.mult), selfwait=True)
            pr.wait("a", "sV", v_mu)
            pr.wait("a", *vload)
            pr.wait("a", "sP", ln_prev["tr"])
            a_sq = pr.op("a", lambda a: a.activation(
                ybuf[:], xtile[:], AF.Square, accum_out=st2[:]),
                selfwait=True)
            pr.wait("v", "sA", a_sq)
            pr.op("v", lambda v: v.tensor_mul(st3[:], st1[:], st1[:]),
                  selfwait=True)
            pr.op("v", lambda v: v.scalar_tensor_tensor(
                st3[:], st2[:], 1.0 / H, st3[:], op0=ALU.mult,
                op1=ALU.subtract), selfwait=True)
            v_var = pr.op("v", lambda v: v.tensor_scalar(
                st3[:], st3[:], EPS, None, op0=ALU.add), selfwait=True)
            pr.wait("a", "sV", v_var)
            a_ln = pr.op("a", lambda a: a.activation(st2[:], st3[:], AF.Ln),
                         selfwait=True)
            pr.wait("v", "sA", a_ln)
            v_nh = pr.op("v", lambda v: v.tensor_scalar(
                st3[:], st2[:], -0.5, None, op0=ALU.mult), selfwait=True)
            pr.wait("a", "sV", v_nh)
            a_rstd = pr.op("a", lambda a: a.activation(st2[:], st3[:],
                                                       AF.Exp),
                           selfwait=True)
            pr.wait("v", "sA", a_rstd)
            a_y = pr.op("v", lambda v: v.tensor_scalar(
                ybuf[:], xtile[:], st1[:, 0:1], st2[:, 0:1],
                op0=ALU.add, op1=ALU.mult), selfwait=True)
            pr.wait("t", "sV", a_y)
            cps = []
            y3 = ybig[:].rearrange("p (hc r) -> p hc r", hc=HC)
            for g8 in range(4):
                pb = ps_b if (g8 % 2 == 0) else ps_b2
                if g8 >= 2:
                    pr.wait("t", "sV", cps[g8 - 2])
                for j in range(8):
                    hc = g8 * 8 + j
                    if j < 7:
                        pr.on("t", lambda t, pb=pb, j=j, hc=hc:
                              t.transpose(pb[:, j * 128:(j + 1) * 128],
                                          ybuf[:, hc * 128:(hc + 1) * 128],
                                          ident[:]))
                    else:
                        pr.op("t", lambda t, pb=pb, j=j, hc=hc:
                              t.transpose(pb[:, j * 128:(j + 1) * 128],
                                          ybuf[:, hc * 128:(hc + 1) * 128],
                                          ident[:]))
                p_tr = pr.cnt["sP"]
                pr.wait("v", "sP", p_tr)
                cps.append(pr.op("v", lambda v, pb=pb, g8=g8, y3=y3,
                                 col0=col0:
                                 v.tensor_copy(
                                     y3[:, g8 * 8:(g8 + 1) * 8,
                                        col0:col0 + 128], pb[:]),
                                 selfwait=True))
            ln_prev["ay"] = a_y
            ln_prev["tr"] = pr.cnt["sP"]

        def ffn_pass(tag, src512, sstrip, dstrip, fused, wd_fused,
                     sub=99):
            sc = ExitStack()
            ybig = sc.enter_context(nc.sbuf_tensor(f"ybig_{tag}",
                                                   [128, HC * 1024], BF16))
            with sc:
                sgu = ExitStack()
                wblk = sgu.enter_context(nc.sbuf_tensor(f"wblk_{tag}",
                                                        [128, 4 * H], BF16))

                hstage = sgu.enter_context(nc.sbuf_tensor(f"hst_{tag}",
                                                          [128, 1024], BF16))
                silu_t = sgu.enter_context(nc.sbuf_tensor(f"sil_{tag}",
                                                          [128, 1024], BF16))
                with sgu:
                    if sub < 1:
                        return
                    sln = ExitStack()
                    xtile = sln.enter_context(nc.sbuf_tensor(
                        f"xt_{tag}", [128, H], F32))
                    ybuf = sln.enter_context(nc.sbuf_tensor(
                        f"yb_{tag}", [128, H], BF16))
                    with sln:
                        for gi in range(2):
                            for rt in range(4):
                                ln_tile(src512(gi, rt), ybig, 1024,
                                        gi * 512 + rt * 128, xtile, ybuf)
                        pr.barrier()
                    qint = sgu.enter_context(nc.sbuf_tensor(f"qint_{tag}",
                                                            [128, 3 * H],
                                                            I32))
                    sF = sgu.enter_context(nc.sbuf_tensor(f"sF_{tag}",
                                                          [128, 2 * H],
                                                          F32))
                    if sub < 2:
                        return
                    sttj = {}
                    stt_last = {}
                    mm_done = {}
                    ev_done = {}
                    h_store = {}
                    w_load = {}
                    scr_w = {}
                    for ib in range(IC):
                        slot = (ib % 2) * 2 * H
                        if fused:
                            ldq_m = {}
                            qs_m = {}
                            flc = {}
                            for m in range(2):
                                qs_m[m] = ((2 * ib + m) % 3) * H
                                if 2 * ib + m >= 3:
                                    pm = 2 * ib + m - 3
                                    pr.wait("y", "sV",
                                            sttj[(pm // 2,
                                                  (pm % 2) * 8 + 7)])
                                dma("y", qint[:, qs_m[m]:qs_m[m] + H],
                                    (gq if m == 0 else uq)[ib])
                                ldq_m[m] = relay("y")
                            for m in range(2):
                                src = gs if m == 0 else us
                                for c in range(4):
                                    if ib >= 1:
                                        pr.wait("g", "sV",
                                                sttj[(ib - 1,
                                                      m * 8 + 2 * c + 1)])
                                    dma("g", sF[:, m * H + c * 1024:
                                                m * H + (c + 1) * 1024],
                                        src[ib:ib + 1,
                                            c * 1024:(c + 1) * 1024]
                                        .partition_broadcast(128), "dG")
                                    flc[(m, c)] = relay("g")
                            if ib >= 2:
                                pr.wait("t", "sV", ev_done[(ib - 2) * 2 + 1])
                            for m in range(2):
                                for sp in range(8):
                                    j = m * 8 + sp
                                    if sp == 0:
                                        pr.wait("v", *ldq_m[m])
                                    if sp % 2 == 0:
                                        pr.wait("v", *flc[(m, sp // 2)])
                                    if j == 0 and ib >= 2:
                                        pr.wait("v", "sP",
                                                mm_done[(ib - 2) * 2
                                                        + 1])
                                        if (ib - 2) in scr_w:
                                            pr.wait("v",
                                                    *scr_w[ib - 2])
                                    pr.on("v", lambda v, m=m, sp=sp,
                                          slot=slot, qs=qs_m[m]:
                                          v.scalar_tensor_tensor(
                                              wblk[:, slot + m * H +
                                                   sp * 512:
                                                   slot + m * H +
                                                   (sp + 1) * 512],
                                              qint[:, qs + sp * 512:
                                                   qs + (sp + 1) * 512],
                                              8.0,
                                              sF[:, m * H + sp * 512:
                                                 m * H + (sp + 1) * 512],
                                              op0=ALU.subtract,
                                              op1=ALU.mult)
                                          .then_inc(SEM["sV"], 1))
                                    sttj[(ib, j)] = pr.inc("sV")
                            stt_last[ib] = sttj[(ib, 15)]
                            pr.wait("a", "sV", stt_last[ib])
                            dma("a", wg_bf[ib], wblk[:, slot:slot + H])
                            dma("a", wu_bf[ib],
                                wblk[:, slot + H:slot + 2 * H])
                            scr_w[ib] = relay("a")
                        else:
                            if ib >= 2:
                                pr.wait("y", "sP", mm_done[(ib - 2) * 2 + 1])
                            dma("y", wblk[:, slot:slot + H], wg_bf[ib])
                            dma("y", wblk[:, slot + H:slot + 2 * H],
                                wu_bf[ib])
                            w_load[ib] = relay("y")
                        for rc in range(2):
                            step = ib * 2 + rc
                            setb = (step % 2) * 1024
                            if fused:
                                pr.wait("t", "sV", stt_last[ib])
                            else:
                                pr.wait("t", *w_load[ib])
                            if step >= 2:
                                pr.wait("t", "sV", ev_done[step - 2])
                            for m in range(2):
                                for hc in range(HC):
                                    last = (hc == HC - 1)

                                    def mmf(t, m=m, hc=hc, slot=slot,
                                            setb=setb, rc=rc, last=last):
                                        ins = t.matmul(
                                            ps_a[:, setb + m * 512:
                                                 setb + (m + 1) * 512],
                                            wblk[:, slot + m * H + hc * 128:
                                                 slot + m * H +
                                                 (hc + 1) * 128],
                                            ybig[:, hc * 1024 + rc * 512:
                                                 hc * 1024 +
                                                 (rc + 1) * 512],
                                            start=(hc == 0), stop=last)
                                        if last:
                                            ins.then_inc(SEM["sP"], 1)
                                    pr.on("t", mmf)
                                pr.inc("sP")
                            mm_done[step] = pr.cnt["sP"]
                            par = step % 2
                            pr.wait("a", "sP", mm_done[step])
                            if step >= 2:
                                pr.wait("a", *h_store[step - 2])
                            pr.on("a", lambda a, setb=setb, par=par:
                                  a.activation(silu_t[:, par * 512:
                                                      (par + 1) * 512],
                                               ps_a[:, setb:setb + 512],
                                               AF.Silu)
                                  .then_inc(SEM["sA"], 1))
                            a_si = pr.inc("sA")
                            pr.wait("v", "sA", a_si)
                            if step >= 2:
                                pr.wait("v", *h_store[step - 2])
                            pr.on("v", lambda v, setb=setb, par=par:
                                  v.scalar_tensor_tensor(
                                      hstage[:, par * 512:(par + 1) * 512],
                                      silu_t[:, par * 512:(par + 1) * 512],
                                      0.5,
                                      ps_a[:, setb + 512:setb + 1024],
                                      op0=ALU.mult, op1=ALU.mult)
                                  .then_inc(SEM["sV"], 1))
                            ev_done[step] = pr.inc("sV")
                            pr.wait("a", "sV", ev_done[step])
                            dma("a",
                                hT_scr[rc, :, ib * 512:(ib + 1) * 512],
                                hstage[:, par * 512:(par + 1) * 512])
                            h_store[step] = relay("a")
                    pr.barrier()
                sc.close()

                if sub < 3:
                    return
                if wd_fused:
                    sd = ExitStack()
                    wdq = (IC + 3) // 4
                    qd = sd.enter_context(nc.sbuf_tensor(f"qd_{tag}",
                                                         [128,
                                                          2 * wdq * 256],
                                                         I32))
                    wdp = sd.enter_context(nc.sbuf_tensor(f"wdp_{tag}",
                                                          [128, IC * 256],
                                                          BF16))
                    srd = sd.enter_context(nc.sbuf_tensor(f"srd_{tag}",
                                                          [1, 2 * wdq * 256],
                                                          F32))
                    with sd:
                        import itertools
                        qpick = itertools.cycle(("y", "g", "a"))
                        wd_bcq = {}
                        wd_str = {}
                        slot_prev = {0: 0, 1: 0}
                        gpair = 0
                        for cp in range(NCP):
                            for qt4 in range(4):
                                ic0 = qt4 * wdq
                                nic = min(wdq, IC - ic0)
                                if nic <= 0:
                                    continue
                                qs = (qt4 % 2) * wdq * 256
                                qq = next(qpick)
                                pr.wait(qq, "sV", slot_prev[qt4 % 2])
                                dma(qq, qd[:, qs:qs + nic * 256],
                                    dq[:, ic0:ic0 + nic,
                                       cp * 256:(cp + 1) * 256])
                                dma(qq, srd[:, qs:qs + nic * 256],
                                    dsc[cp:cp + 1,
                                        ic0 * 256:(ic0 + nic) * 256])
                                ldq = relay(qq)
                                pr.wait("t", *ldq)
                                for pj in range((nic + 1) // 2):
                                    np2 = min(2, nic - pj * 2)
                                    pb = ps_c if (gpair % 2 == 0) else ps_d
                                    if gpair >= 2 and (gpair - 2) in wd_bcq:
                                        pr.wait("t", "sV", wd_bcq[gpair - 2])
                                    pr.on("t", lambda t, pb=pb, qs=qs,
                                          pj=pj, np2=np2:
                                          t.matmul(pb[:, 0:np2 * 256],
                                                   onesr[:],
                                                   srd[0:1,
                                                       qs + pj * 512:
                                                       qs + pj * 512 +
                                                       np2 * 256],
                                                   start=True, stop=True)
                                          .then_inc(SEM["sP"], 1))
                                    p_bc = pr.inc("sP")
                                    pr.wait("v", "sP", p_bc)
                                    if pj == 0:
                                        pr.wait("v", *ldq)
                                        if cp >= 1 and qt4 == 0:
                                            pr.wait("v", *wd_str[cp - 1])
                                    for s2 in range(np2):
                                        ii = pj * 2 + s2
                                        icb = ic0 + ii
                                        pr.on("v", lambda v, pb=pb, qs=qs,
                                              ii=ii, icb=icb, s2=s2:
                                              v.scalar_tensor_tensor(
                                                  wdp[:, icb * 256:
                                                      (icb + 1) * 256],
                                                  qd[:, qs + ii * 256:
                                                     qs + (ii + 1) * 256],
                                                  8.0,
                                                  pb[:, s2 * 256:
                                                     (s2 + 1) * 256],
                                                  op0=ALU.subtract,
                                                  op1=ALU.mult)
                                              .then_inc(SEM["sV"], 1))
                                        last_stt = pr.inc("sV")
                                    wd_bcq[gpair] = last_stt
                                    gpair += 1
                                slot_prev[qt4 % 2] = last_stt
                            sq4 = next(qpick)
                            pr.wait(sq4, "sV", last_stt)
                            dma(sq4, wd_bf[cp], wdp[:])
                            wd_str[cp] = relay(sq4)
                    pr.barrier()

                if sub < 4:
                    return
                for gi in range(2):
                    sg = ExitStack()
                    hTg = sg.enter_context(nc.sbuf_tensor(
                        f"hTg_{tag}{gi}", [128, IC * 512], BF16))
                    wdpan = sg.enter_context(nc.sbuf_tensor(
                        f"wdpan_{tag}{gi}", [128, 2 * IC * 256], BF16))
                    xstrip = sg.enter_context(nc.sbuf_tensor(
                        f"xs_{tag}{gi}", [128, 4 * 256], F32))
                    xnw = sg.enter_context(nc.sbuf_tensor(
                        f"xn_{tag}{gi}", [128, 4 * 256], F32))
                    with sg:
                        dma("y", hTg[:], hT_scr[gi])
                        mm_d = {}
                        ev_d = {}
                        st_d = {}
                        for cp in range(NCP):
                            slot = (cp % 2) * IC * 256
                            if cp >= 2:
                                pr.wait("y", "sP", mm_d[(cp - 2, 3)])
                            dma("y", wdpan[:, slot:slot + IC * 256],
                                wd_bf[cp])
                            ldw = relay("y")
                            for rt in range(4):
                                pr.wait("t", *ldw)
                                if cp >= 1:
                                    pr.wait("t", "sV", ev_d[(cp - 1, rt)])
                                for icb in range(IC):
                                    last = (icb == IC - 1)

                                    def mmd(t, icb=icb, rt=rt, slot=slot,
                                            last=last, hTg=hTg,
                                            wdpan=wdpan):
                                        ins = t.matmul(
                                            ps_a[:, rt * 512:
                                                 rt * 512 + 256],
                                            hTg[:, icb * 512 + rt * 128:
                                                icb * 512 +
                                                (rt + 1) * 128],
                                            wdpan[:, slot + icb * 256:
                                                  slot + (icb + 1) * 256],
                                            start=(icb == 0), stop=last)
                                        if last:
                                            ins.then_inc(SEM["sP"], 1)
                                    pr.on("t", mmd)
                                mm_d[(cp, rt)] = pr.inc("sP")
                                sx = rt
                                if cp >= 1:
                                    pr.wait("y", "sV", ev_d[(cp - 1, rt)])
                                dma("y", xstrip[:, sx * 256:
                                                (sx + 1) * 256],
                                    sstrip(gi, rt, cp))
                                lx = relay("y")
                                pr.wait("v", "sP", mm_d[(cp, rt)])
                                pr.wait("v", *lx)
                                if cp >= 1:
                                    pr.wait("v", *st_d[(cp - 1, rt)])
                                pr.on("v", lambda v, sx=sx, rt=rt,
                                      xnw=xnw, xstrip=xstrip:
                                      v.scalar_tensor_tensor(
                                          xnw[:, sx * 256:(sx + 1) * 256],
                                          ps_a[:, rt * 512:rt * 512 + 256],
                                          1.0,
                                          xstrip[:, sx * 256:
                                                 (sx + 1) * 256],
                                          op0=ALU.mult, op1=ALU.add)
                                      .then_inc(SEM["sV"], 1))
                                ev_d[(cp, rt)] = pr.inc("sV")
                                pr.wait("g", "sV", ev_d[(cp, rt)])
                                dma("g", dstrip(gi, rt, cp),
                                    xnw[:, sx * 256:(sx + 1) * 256], "dG")
                                st_d[(cp, rt)] = relay("g")
                    pr.barrier()

        def run_attn_qg(qg, kbuf, vbuf):
            sa = ExitStack()
            qbuf = sa.enter_context(nc.sbuf_tensor(f"qbuf{qg}",
                                                   [128, NQ * 512], BF16))
            cosq = sa.enter_context(nc.sbuf_tensor(f"cosq{qg}", [64, 512],
                                                   F32))
            sinq = sa.enter_context(nc.sbuf_tensor(f"sinq{qg}", [64, 512],
                                                   F32))
            with sa:
                dma("y", cosq[:], cq_in[:, QPBASE[qg]:QPBASE[qg] + 512])
                dma("y", sinq[:], sq_in[:, QPBASE[qg]:QPBASE[qg] + 512])
                relay("y")
                sq = ExitStack()
                ybig = sq.enter_context(nc.sbuf_tensor(f"ybig_q{qg}",
                                                       [128, HC * 512],
                                                       BF16))
                with sq:
                    sln2 = ExitStack()
                    xtile = sln2.enter_context(nc.sbuf_tensor(
                        f"xt_q{qg}", [128, H], F32))
                    ybuf = sln2.enter_context(nc.sbuf_tensor(
                        f"yb_q{qg}", [128, H], BF16))
                    with sln2:
                        for rt in range(4):
                            ln_tile(x_rows(qg, rt), ybig, 512, rt * 128,
                                    xtile, ybuf)
                        pr.barrier()
                    wqblk = sq.enter_context(nc.sbuf_tensor(
                        f"wqb{qg}", [128, 2 * HC * 256], BF16))
                    q_rope = {}
                    q_mm = {}
                    for kvh in range(NKV):
                        for hf in range(2):
                            blkid = kvh * 2 + hf
                            slot = (blkid % 2) * HC * 256
                            if blkid >= 2:
                                pr.wait("g", "sP", q_mm[blkid - 2])
                            dma("g", wqblk[:, slot:slot + HC * 256],
                                wq[blkid], "dG")
                            lw = relay("g")
                            for qh2 in range(2):
                                qh = kvh * 4 + hf * 2 + qh2
                                pb = ps_c if (qh % 2 == 0) else ps_d
                                pr.wait("t", *lw)
                                if qh >= 2:
                                    pr.wait("t", "sV", q_rope[qh - 2])
                                for hc in range(HC):
                                    last = (hc == HC - 1)

                                    def mq(t, pb=pb, hc=hc, slot=slot,
                                           qh2=qh2, last=last):
                                        ins = t.matmul(
                                            pb[:],
                                            wqblk[:, slot + hc * 256 +
                                                  qh2 * 128:
                                                  slot + hc * 256 +
                                                  (qh2 + 1) * 128],
                                            ybig[:, hc * 512:
                                                 (hc + 1) * 512],
                                            start=(hc == 0), stop=last)
                                        if last:
                                            ins.then_inc(SEM["sP"], 1)
                                    pr.on("t", mq)
                                q_mm[blkid] = pr.inc("sP")
                                pr.wait("v", "sP", q_mm[blkid])
                                co = qh * 512
                                pr.op("v", lambda v, pb=pb:
                                      v.tensor_mul(rt1[:], pb[0:64, :],
                                                   cosq[:]), selfwait=True)
                                pr.op("v", lambda v, pb=pb:
                                      v.tensor_mul(rt2[:], pb[64:128, :],
                                                   sinq[:]), selfwait=True)
                                pr.op("v", lambda v, pb=pb:
                                      v.tensor_mul(rt3[:], pb[64:128, :],
                                                   cosq[:]), selfwait=True)
                                pr.op("v", lambda v, pb=pb:
                                      v.tensor_mul(rt4[:], pb[0:64, :],
                                                   sinq[:]), selfwait=True)
                                pr.op("v", lambda v, co=co:
                                      v.tensor_sub(qbuf[0:64, co:co + 512],
                                                   rt1[:], rt2[:]),
                                      selfwait=True)
                                q_rope[qh] = pr.op(
                                    "v", lambda v, co=co:
                                    v.tensor_add(qbuf[64:128, co:co + 512],
                                                 rt3[:], rt4[:]),
                                    selfwait=True)
                    pr.barrier()

                sc2 = ExitStack()
                outT = sc2.enter_context(nc.sbuf_tensor(f"outT{qg}",
                                                        [128, NQ * 512],
                                                        BF16))
                sc3 = sc2.enter_context(ExitStack())
                maskg = sc3.enter_context(nc.sbuf_tensor(f"maskg{qg}",
                                                         [128, MC1], BF16))
                pbuf = sc3.enter_context(nc.sbuf_tensor(f"pbuf{qg}",
                                                        [128, 2048], BF16))
                ptb = sc3.enter_context(nc.sbuf_tensor(f"ptb{qg}",
                                                       [128, 16 * 512],
                                                       BF16))
                if True:
                    mcols = MC0 if qg == 0 else MC1
                    moff0 = 0 if qg == 0 else MC0
                    dma("y", maskg[:, 0:mcols],
                        cmask[:, moff0:moff0 + mcols])
                    vm = relay("y")
                    ptb3 = ptb[:].rearrange("p (kt q) -> p kt q", kt=16)
                    exp_dn = {}
                    ptc_dn = {}
                    ptm_dn = {}
                    av_ev = {}
                    v_sc = {}

                    def emit_S(kvh, qh, qt):
                        E = EMAX[qg][qt]
                        W = E * 128
                        nch = (W + 511) // 512
                        if (qh, qt - 1) in exp_dn:
                            pr.wait("t", "sA", exp_dn[(qh, qt - 1)])
                        elif qt == 0 and qh > 0:
                            pr.wait("t", "sA", exp_dn[(qh - 1, 3)])
                        for c in range(nch):
                            Wc = min(512, W - c * 512)
                            last = (c == nch - 1)

                            def ms(t, c=c, Wc=Wc, qh=qh, qt=qt, kvh=kvh,
                                   last=last):
                                ins = t.matmul(
                                    ps_a[:, c * 512:c * 512 + Wc],
                                    qbuf[:, qh * 512 + qt * 128:
                                         qh * 512 + (qt + 1) * 128],
                                    kbuf[:, kvh * S + c * 512:
                                         kvh * S + c * 512 + Wc],
                                    start=True, stop=True)
                                if last:
                                    ins.then_inc(SEM["sP"], 1)
                            pr.on("t", ms)
                        return pr.inc("sP")

                    def emit_soft(qh, qt, p_s):
                        E = EMAX[qg][qt]
                        W = E * 128
                        mo = MOFF[(qg, qt)]
                        pr.wait("v", "sP", p_s)
                        v_mx = pr.op(
                            "v", lambda v, W=W:
                            v.reduce_max(negm[:], ps_a[:, 0:W],
                                         axis=AX.X, negate=True),
                            selfwait=True)
                        pr.wait("a", "sV", v_mx)
                        if (qh, qt - 1) in ptm_dn:
                            pr.wait("a", "sP", ptm_dn[(qh, qt - 1)])
                        elif qt == 0 and qh > 0:
                            pr.wait("a", "sP", ptm_dn[(qh - 1, 3)])
                        exp_dn[(qh, qt)] = pr.op(
                            "a", lambda a, W=W:
                            a.activation(pbuf[:, 0:W], ps_a[:, 0:W],
                                         AF.Exp, bias=negm[:, 0:1]),
                            selfwait=True)
                        pr.wait("v", "sA", exp_dn[(qh, qt)])
                        pr.wait("v", *vm)
                        pr.op("v", lambda v, W=W, mo=mo:
                              v.scalar_tensor_tensor(
                                  pbuf[:, 0:W], pbuf[:, 0:W], 1.0,
                                  maskg[:, mo:mo + W],
                                  op0=ALU.mult, op1=ALU.mult,
                                  accum_out=dsumb[:]), selfwait=True)
                        pr.op("v", lambda v: v.reciprocal(rdsum[:],
                                                          dsumb[:]),
                              selfwait=True)
                        v_sc[(qh, qt)] = pr.op(
                            "v", lambda v, W=W:
                            v.tensor_scalar(pbuf[:, 0:W], pbuf[:, 0:W],
                                            rdsum[:, 0:1], None,
                                            op0=ALU.mult), selfwait=True)

                    def emit_PT(qh, qt):
                        E = EMAX[qg][qt]
                        pr.wait("t", "sV", v_sc[(qh, qt)])
                        if (qh, qt - 1) in ptc_dn:
                            pr.wait("t", "sV", ptc_dn[(qh, qt - 1)])
                        elif qt == 0 and qh > 0:
                            pr.wait("t", "sV", ptc_dn[(qh - 1, 3)])
                        if qt == 0 and qh > 0:
                            pr.wait("t", "sV", av_ev[qh - 1])
                        for kt in range(E):
                            pb = ps_b if kt < 8 else ps_b2
                            jj = kt % 8
                            last = (kt == E - 1)

                            def mt(t, pb=pb, jj=jj, kt=kt, last=last):
                                ins = t.transpose(
                                    pb[:, jj * 128:(jj + 1) * 128],
                                    pbuf[:, kt * 128:(kt + 1) * 128],
                                    ident[:])
                                if last:
                                    ins.then_inc(SEM["sP"], 1)
                            pr.on("t", mt)
                        ptm_dn[(qh, qt)] = pr.inc("sP")
                        pr.wait("v", "sP", ptm_dn[(qh, qt)])
                        n1 = min(E, 8)
                        pr.op("v", lambda v, n1=n1, qt=qt:
                              v.tensor_copy(
                                  ptb3[:, 0:n1, qt * 128:(qt + 1) * 128],
                                  ps_b[:, 0:n1 * 128]), selfwait=True)
                        if E > 8:
                            pr.op("v", lambda v, E=E, qt=qt:
                                  v.tensor_copy(
                                      ptb3[:, 8:E,
                                           qt * 128:(qt + 1) * 128],
                                      ps_b2[:, 0:(E - 8) * 128]),
                                  selfwait=True)
                        ptc_dn[(qh, qt)] = pr.cnt["sV"]

                    pend_S = None
                    pend_av = None
                    for kvh in range(NKV):
                        for qh4 in range(4):
                            qh = kvh * 4 + qh4
                            p_s = (pend_S if pend_S is not None
                                   else emit_S(kvh, qh, 0))
                            pend_S = None
                            for qt in range(4):
                                emit_soft(qh, qt, p_s)
                                if qt == 0 and pend_av is not None:
                                    pend_av()
                                    pend_av = None
                                if qt < 3:
                                    p_s = emit_S(kvh, qh, qt + 1)
                                elif qh + 1 < NQ:
                                    pend_S = emit_S((qh + 1) // 4,
                                                    qh + 1, 0)
                                emit_PT(qh, qt)
                            pr.wait("t", "sV", ptc_dn[(qh, 3)])
                            if qh > 0:
                                pr.wait("t", "sV", av_ev[qh - 1])
                            Emax3 = EMAX[qg][3]
                            for kt in range(Emax3):
                                qtmin = 0 if qg == 0 else max(0, kt - 12)
                                off = qtmin * 128
                                last = (kt == Emax3 - 1)

                                def mav(t, kt=kt, off=off, kvh=kvh,
                                        last=last):
                                    ins = t.matmul(
                                        ps_c[:, off:512],
                                        vbuf[:, (kt * NKV + kvh) * 128:
                                             (kt * NKV + kvh + 1) * 128],
                                        ptb[:, kt * 512 + off:
                                            (kt + 1) * 512],
                                        start=(kt == 0), stop=last)
                                    if last:
                                        ins.then_inc(SEM["sP"], 1)
                                pr.on("t", mav)
                            av_mm = pr.inc("sP")

                            def do_av_ev(qh=qh, av_mm=av_mm):
                                pr.wait("v", "sP", av_mm)
                                av_ev[qh] = pr.op(
                                    "v", lambda v, qh=qh:
                                    v.tensor_copy(outT[:, qh * 512:
                                                       (qh + 1) * 512],
                                                  ps_c[:]), selfwait=True)
                            pend_av = do_av_ev
                    pend_av()
                    pr.barrier()
                sc3.close()

                sw = ExitStack()
                woblk = sw.enter_context(nc.sbuf_tensor(f"wob{qg}",
                                                        [128, 2 * HC * 256],
                                                        BF16))
                xstrip = sw.enter_context(nc.sbuf_tensor(f"xsw{qg}",
                                                         [128, 4 * 256],
                                                         F32))
                xnw = sw.enter_context(nc.sbuf_tensor(f"xnw{qg}",
                                                      [128, 4 * 256], F32))
                with sw:
                    mm_o = {}
                    ev_o = {}
                    st_o = {}
                    for cp in range(NCP):
                        slot = (cp % 2) * HC * 256
                        if cp >= 2:
                            pr.wait("g", "sP", mm_o[(cp - 2, 3)])
                        dma("g", woblk[:, slot:slot + HC * 256],
                            wo[cp], "dG")
                        lw = relay("g")
                        for rt in range(4):
                            pr.wait("t", *lw)
                            if cp >= 1:
                                pr.wait("t", "sV", ev_o[(cp - 1, rt)])
                            for hc in range(HC):
                                last = (hc == HC - 1)

                                def mo_(t, hc=hc, rt=rt, slot=slot,
                                        last=last):
                                    ins = t.matmul(
                                        ps_a[:, rt * 512:rt * 512 + 256],
                                        outT[:, hc * 512 + rt * 128:
                                             hc * 512 + (rt + 1) * 128],
                                        woblk[:, slot + hc * 256:
                                              slot + (hc + 1) * 256],
                                        start=(hc == 0), stop=last)
                                    if last:
                                        ins.then_inc(SEM["sP"], 1)
                                pr.on("t", mo_)
                            mm_o[(cp, rt)] = pr.inc("sP")
                            sx = rt
                            if cp >= 1:
                                pr.wait("y", "sV", ev_o[(cp - 1, rt)])
                            dma("y",
                                xstrip[:, sx * 256:(sx + 1) * 256],
                                x_strip_ap(qg, rt, cp))
                            lx = relay("y")
                            pr.wait("v", "sP", mm_o[(cp, rt)])
                            pr.wait("v", *lx)
                            if cp >= 1:
                                pr.wait("v", *st_o[(cp - 1, rt)])
                            pr.on("v", lambda v, sx=sx, rt=rt:
                                  v.scalar_tensor_tensor(
                                      xnw[:, sx * 256:(sx + 1) * 256],
                                      ps_a[:, rt * 512:rt * 512 + 256],
                                      1.0,
                                      xstrip[:, sx * 256:(sx + 1) * 256],
                                      op0=ALU.mult, op1=ALU.add)
                                  .then_inc(SEM["sV"], 1))
                            ev_o[(cp, rt)] = pr.inc("sV")
                            pr.wait("g", "sV", ev_o[(cp, rt)])
                            dma("g", x_strip_ap(qg, rt, cp),
                                xnw[:, sx * 256:(sx + 1) * 256], "dG")
                            st_o[(cp, rt)] = relay("g")
                sc2.close()
            pr.barrier()

        dma("y", ident[:], c_ident)
        dma("y", onesr[:], c_ones)
        dma("y", halfr[:], c_half)
        relay("y")
        pr.barrier()

        for pi in range(npass):
            r0 = pi * 1024

            def s512(gi, rt, r0=r0):
                a = r0 + gi * 512 + rt * 128
                return xin[a:a + 128, :]

            def sstrip(gi, rt, cp, r0=r0):
                a = r0 + gi * 512 + rt * 128
                return xin[a:a + 128, cp * 256:(cp + 1) * 256]

            def dstrip(gi, rt, cp, r0=r0):
                a = r0 + gi * 512 + rt * 128
                return x_scr[a:a + 128, cp * 256:(cp + 1) * 256]

            ffn_pass(f"f1{pi}", s512, sstrip, dstrip,
                     fused=(pi == 0), wd_fused=(pi == 0), sub=sub)

        if dump == "x1":
            dma("y", o_dbg, x_scr)
            pr.barrier()

        if phases >= 2:
            kv_outer = ExitStack()
            cosk = kv_outer.enter_context(nc.sbuf_tensor("cosk", [64, S],
                                                         F32))
            sink = kv_outer.enter_context(nc.sbuf_tensor("sink", [64, S],
                                                         F32))
            rt1 = kv_outer.enter_context(nc.sbuf_tensor("rt1", [64, 512],
                                                        F32))
            rt2 = kv_outer.enter_context(nc.sbuf_tensor("rt2", [64, 512],
                                                        F32))
            rt3 = kv_outer.enter_context(nc.sbuf_tensor("rt3", [64, 512],
                                                        F32))
            rt4 = kv_outer.enter_context(nc.sbuf_tensor("rt4", [64, 512],
                                                        F32))
            kbuf = kv_outer.enter_context(
                nc.sbuf_tensor("kbuf", [128, NKV * S], BF16))
            vbuf = kv_outer.enter_context(
                nc.sbuf_tensor("vbuf", [128, (S // 128) * NKV * DH], BF16))
            with kv_outer:
                dma("y", cosk[:], ck_in)
                dma("y", sink[:], sk_in)
                relay("y")
                sk = ExitStack()
                wkvbf = sk.enter_context(nc.sbuf_tensor(
                    "wkvbf", [128, HC * 512], BF16))
                ybig_kv = sk.enter_context(nc.sbuf_tensor(
                    "ybig_kv", [128, HC * 512], BF16))
                xt_kv = sk.enter_context(nc.sbuf_tensor("xt_kv", [128, H],
                                                        F32))
                yb_kv = sk.enter_context(nc.sbuf_tensor("yb_kv", [128, H],
                                                        BF16))
                with sk:
                    wkbf = wkvbf
                    wvbf = wkvbf
                    for ch in range(4):
                        for rt in range(4):
                            ln_tile(x_scr[ch * 512 + rt * 128:
                                          ch * 512 + (rt + 1) * 128, :],
                                    ybig_kv, 512, rt * 128, xt_kv, yb_kv)
                        pr.barrier()
                        rope_dn = {}
                        for kvh in range(NKV):
                            if kvh % 4 == 0:
                                kh = kvh // 4
                                if ch + kh > 0:
                                    pr.wait("g", "sP", pr.cnt["sP"])
                                dma("g", wkvbf[:], wk[kh], "dG")
                                lw = relay("g")
                                pr.wait("t", *lw)
                            kvl = kvh % 4
                            pb = ps_c if (kvh % 2 == 0) else ps_d
                            if kvh >= 2:
                                pr.wait("t", "sV", rope_dn[kvh - 2])
                            for hc in range(HC):
                                last = (hc == HC - 1)

                                def mk(t, pb=pb, hc=hc, kvl=kvl, last=last):
                                    ins = t.matmul(
                                        pb[:],
                                        wkvbf[:, hc * 512 + kvl * 128:
                                              hc * 512 + (kvl + 1) * 128],
                                        ybig_kv[:, hc * 512:
                                                (hc + 1) * 512],
                                        start=(hc == 0), stop=last)
                                    if last:
                                        ins.then_inc(SEM["sP"], 1)
                                pr.on("t", mk)
                            p_k = pr.inc("sP")
                            pr.wait("v", "sP", p_k)
                            co = kvh * S + ch * 512
                            cs = (ch * 512, (ch + 1) * 512)
                            pr.op("v", lambda v, pb=pb, cs=cs:
                                  v.tensor_mul(rt1[:], pb[0:64, :],
                                               cosk[:, cs[0]:cs[1]]),
                                  selfwait=True)
                            pr.op("v", lambda v, pb=pb, cs=cs:
                                  v.tensor_mul(rt2[:], pb[64:128, :],
                                               sink[:, cs[0]:cs[1]]),
                                  selfwait=True)
                            pr.op("v", lambda v, pb=pb, cs=cs:
                                  v.tensor_mul(rt3[:], pb[64:128, :],
                                               cosk[:, cs[0]:cs[1]]),
                                  selfwait=True)
                            pr.op("v", lambda v, pb=pb, cs=cs:
                                  v.tensor_mul(rt4[:], pb[0:64, :],
                                               sink[:, cs[0]:cs[1]]),
                                  selfwait=True)
                            pr.op("v", lambda v, co=co:
                                  v.tensor_sub(kbuf[0:64, co:co + 512],
                                               rt1[:], rt2[:]),
                                  selfwait=True)
                            rope_dn[kvh] = pr.op(
                                "v", lambda v, co=co:
                                v.tensor_add(kbuf[64:128, co:co + 512],
                                             rt3[:], rt4[:]),
                                selfwait=True)
                        vev = {}
                        for j in range(8):
                            dsp, kt = divmod(j, 4)
                            if kt == 0:
                                pr.wait("g", "sP", pr.cnt["sP"])
                                dma("g", wkvbf[:], wv[dsp], "dG")
                                lwv = relay("g")
                                pr.wait("t", *lwv)
                            pb = ps_c if (j % 2 == 0) else ps_d
                            if j >= 2:
                                pr.wait("t", "sV", vev[j - 2])
                            else:
                                pr.wait("t", "sV", rope_dn[NKV - 2 + j])
                            for hc in range(HC):
                                last = (hc == HC - 1)

                                def mv(t, pb=pb, hc=hc, kt=kt, last=last):
                                    ins = t.matmul(
                                        pb[:],
                                        ybig_kv[:, hc * 512 + kt * 128:
                                                hc * 512 +
                                                (kt + 1) * 128],
                                        wkvbf[:, hc * 512:(hc + 1) * 512],
                                        start=(hc == 0), stop=last)
                                    if last:
                                        ins.then_inc(SEM["sP"], 1)
                                pr.on("t", mv)
                            p_v = pr.inc("sP")
                            pr.wait("v", "sP", p_v)
                            ktg = ch * 4 + kt
                            vev[j] = pr.op(
                                "v", lambda v, pb=pb, ktg=ktg, dsp=dsp:
                                v.tensor_copy(
                                    vbuf[:, ktg * 1024 + dsp * 512:
                                         ktg * 1024 + (dsp + 1) * 512],
                                    pb[:]), selfwait=True)
                        pr.barrier()

                if phases >= 3:
                    for qg in range(2):
                        run_attn_qg(qg, kbuf, vbuf)

            if dump == "x2":
                dma("y", o_dbg, x_scr)
                pr.barrier()

            if phases >= 4:
                def s512(gi, rt):
                    return x_rows(gi, rt)

                def sstrip(gi, rt, cp):
                    return x_strip_ap(gi, rt, cp)

                def dstrip(gi, rt, cp):
                    a = gi * 512 + rt * 128
                    return xout[a:a + 128, cp * 256:(cp + 1) * 256]

                ffn_pass("f3", s512, sstrip, dstrip, fused=False,
                         wd_fused=False)

        pr.barrier()

        @block.tensor
        def _(t):
            for f in pr.ops["t"]:
                f(t)

        @block.vector
        def _(v):
            for f in pr.ops["v"]:
                f(v)

        @block.scalar
        def _(a):
            for f in pr.ops["a"]:
                f(a)

        @block.gpsimd
        def _(g):
            for f in pr.ops["g"]:
                f(g)

        @block.sync
        def _(y):
            for f in pr.ops["y"]:
                f(y)

    return nc


def prepare_in_maps(inp):
    x = np.ascontiguousarray(np.asarray(inp["L0_attn_x"]))
    gq_a = np.asarray(inp["L0_ffn_gate_q"])
    uq_a = np.asarray(inp["L0_ffn_up_q"])
    dq_a = np.asarray(inp["L0_ffn_down_q"])
    gs_a = np.asarray(inp["L0_ffn_gate_s"])
    us_a = np.asarray(inp["L0_ffn_up_s"])
    ds_a = np.asarray(inp["L0_ffn_down_s"])
    wq_a = np.asarray(inp["L0_attn_Wq"])
    wk_a = np.asarray(inp["L0_attn_Wk"])
    wv_a = np.asarray(inp["L0_attn_Wv"])
    wo_a = np.asarray(inp["L0_attn_Wo"])

    tables = {h: _host_tables(h) for h in (0, 1)}
    perms = {h: _perm_rows(h) for h in (0, 1)}

    gq_b = np.ascontiguousarray(
        gq_a.reshape(HC, 128, IC, 128).transpose(2, 1, 0, 3)
        .reshape(IC, 128, H))
    uq_b = np.ascontiguousarray(
        uq_a.reshape(HC, 128, IC, 128).transpose(2, 1, 0, 3)
        .reshape(IC, 128, H))
    dq_b = np.ascontiguousarray(
        dq_a.reshape(IC, 128, NCP, 256).transpose(1, 0, 2, 3)
        .reshape(128, IC, NCP * 256))
    gs_b = np.ascontiguousarray(
        gs_a.reshape(HC, IC, 128).transpose(1, 0, 2).reshape(IC, H))
    us_b = np.ascontiguousarray(
        us_a.reshape(HC, IC, 128).transpose(1, 0, 2).reshape(IC, H))
    ds_b = np.ascontiguousarray(
        ds_a.reshape(IC, NCP, 256).transpose(1, 0, 2).reshape(NCP, IC * 256))
    wq_b = np.ascontiguousarray(
        wq_a.reshape(HC, 128, 16, 256).transpose(2, 1, 0, 3)
        .reshape(16, 128, HC * 256))
    wk_b = np.ascontiguousarray(
        wk_a.reshape(HC, 128, 2, 512).transpose(2, 1, 0, 3)
        .reshape(2, 128, HC * 512))
    wv_b = np.ascontiguousarray(
        wv_a.reshape(HC, 128, 2, 512).transpose(2, 1, 0, 3)
        .reshape(2, 128, HC * 512))
    wo_b = np.ascontiguousarray(
        wo_a.reshape(HC, 128, 16, 256).transpose(2, 1, 0, 3)
        .reshape(16, 128, HC * 256))

    in_maps = []
    for c in range(8):
        b, h = c // 2, c % 2
        ck, sk, cq, sq, masks = tables[h]
        in_maps.append(dict(
            xin=np.ascontiguousarray(x[b][perms[h]]),
            gq=gq_b, uq=uq_b, dq=dq_b, gs=gs_b, us=us_b, dsc=ds_b,
            wq=wq_b, wk=wk_b, wv=wv_b, wo=wo_b,
            ck=ck, sk=sk, cq=cq, sq=sq, cmask=masks,
        ))
    return in_maps


def assemble_out(results):
    out = np.empty((4, S, H), np.float32)
    for c in range(8):
        b, h = c // 2, c % 2
        xo = results[c]["xout"]
        for qg in (0, 1):
            base = ROW_BASE[h][qg]
            out[b, base:base + 512] = xo[qg * 512:(qg + 1) * 512]
    return out


def kernel(L0_attn_x, L0_attn_Wq, L0_attn_Wk, L0_attn_Wv, L0_attn_Wo,
           L0_ffn_gate_q, L0_ffn_gate_s, L0_ffn_up_q, L0_ffn_up_s,
           L0_ffn_down_q, L0_ffn_down_s):
    inp = dict(L0_attn_x=L0_attn_x, L0_attn_Wq=L0_attn_Wq,
               L0_attn_Wk=L0_attn_Wk, L0_attn_Wv=L0_attn_Wv,
               L0_attn_Wo=L0_attn_Wo, L0_ffn_gate_q=L0_ffn_gate_q,
               L0_ffn_gate_s=L0_ffn_gate_s, L0_ffn_up_q=L0_ffn_up_q,
               L0_ffn_up_s=L0_ffn_up_s, L0_ffn_down_q=L0_ffn_down_q,
               L0_ffn_down_s=L0_ffn_down_s)
    in_maps = prepare_in_maps(inp)
    nc = build()
    res = run_bass_kernel_spmd(nc, in_maps, core_ids=list(range(8)))
    return assemble_out(res.results)


# version 09
# speedup vs baseline: 1.2253x; 1.0100x over previous
import numpy as np
import ml_dtypes
from contextlib import ExitStack
import concourse.bass as bass
import concourse.mybir as mybir
from concourse.bass_utils import run_bass_kernel_spmd

F32 = mybir.dt.float32
BF16 = mybir.dt.bfloat16
I32 = mybir.dt.int32
I8 = mybir.dt.int8
AF = mybir.ActivationFunctionType
ALU = mybir.AluOpType
AX = mybir.AxisListType

H = 4096
II = 11008
S = 2048
NKV = 8
NQ = 32
DH = 128
HC = H // 128
IC = II // 128
NCP = H // 256
EPS = 1e-5
RSQD = 1.0 / float(np.sqrt(DH))

EMAX = {0: [8, 8, 8, 8], 1: [13, 14, 15, 16]}
ROW_BASE = {0: (0, 1536), 1: (512, 1024)}
PERM_ABS = {0: [0, 1, 2, 3], 1: [1, 0, 3, 2]}
QPBASE = {0: 0, 1: 1536}


def _mask_offsets():
    offs = {}
    for qg in (0, 1):
        o = 0
        for qt in range(4):
            offs[(qg, qt)] = o
            o += EMAX[qg][qt] * 128
        offs[(qg, "total")] = o
    return offs


MOFF = _mask_offsets()
MC0 = MOFF[(0, "total")]
MC1 = MOFF[(1, "total")]


def _build_consts():
    ident = np.eye(128, dtype=ml_dtypes.bfloat16)
    ones_row = np.ones((1, 128), dtype=np.float32)
    half_row = np.full((1, 128), 0.5, dtype=np.float32)
    return ident, ones_row, half_row


def _perm_rows(half):
    p = np.arange(S)
    blk = p // 512
    return np.array([PERM_ABS[half][b] * 512 + (r % 512)
                     for r, b in zip(p, blk)], dtype=np.int64)


def _host_tables(half):
    inv = 1.0 / (10000.0 ** (np.arange(0, DH, 2, dtype=np.float64)[: DH // 2]
                             / DH))
    perm = _perm_rows(half)
    fr = perm[None, :].astype(np.float64) * inv[:, None]
    ck = np.cos(fr).astype(np.float32)
    sk = np.sin(fr).astype(np.float32)
    cq = (np.cos(fr) * RSQD).astype(np.float32)
    sq = (np.sin(fr) * RSQD).astype(np.float32)
    masks = np.zeros((128, MC0 + MC1), dtype=ml_dtypes.bfloat16)
    for qg in (0, 1):
        for qt in range(4):
            o = (0 if qg == 0 else MC0) + MOFF[(qg, qt)]
            E = EMAX[qg][qt] * 128
            qrows = perm[QPBASE[qg] + qt * 128 + np.arange(128)][:, None]
            krows = perm[np.arange(E)][None, :]
            masks[:, o: o + E] = (krows <= qrows).astype(ml_dtypes.bfloat16)
    return ck, sk, cq, sq, masks


NROT = 24


class Prog:
    ENGS = ("t", "v", "a", "g", "y")

    def __init__(self, sem):
        self.ops = {e: [] for e in self.ENGS}
        self.cnt = {k: 0 for k in sem}
        self.sem = sem
        self.rot = {"y": 0, "g": 0, "a": 0}
        self.fresh = {"y": False, "g": False, "a": False}
        self.gidx = 0

    QPFX = {"y": "dS%d", "g": "dG%d", "a": "dA%d"}

    def qsem(self, q):
        return self.QPFX[q] % self.rot[q]

    def on(self, e, fn):
        self.gidx += 1
        self.ops[e].append(fn)

    def inc(self, nm, n=1):
        self.cnt[nm] += n
        return self.cnt[nm]

    ESEM = {"t": "sP", "v": "sV", "a": "sA"}

    def wait(self, e, nm, v):
        if v <= 0:
            return
        s = self.sem[nm]
        self.ops[e].append(lambda eng, s=s, v=v: eng.wait_ge(s, v))

    def op(self, e, fn, selfwait=False):
        nm = self.ESEM[e]
        if selfwait and self.cnt[nm] > 0:
            self.wait(e, nm, self.cnt[nm])
        s = self.sem[nm]
        self.on(e, lambda eng, fn=fn, s=s: fn(eng).then_inc(s, 1))
        return self.inc(nm)

    def barrier(self):
        snap = dict(self.cnt)
        for e in self.ENGS:
            for nm, v in snap.items():
                self.wait(e, nm, v)


def build(dump=None, phases=99, sub=99, npass=2):
    nc = bass.Bass("TRN2", target_bir_lowering=False)

    xin = nc.dram_tensor("xin", [S, H], F32, kind="ExternalInput").ap()
    gq = nc.dram_tensor("gq", [IC, 128, H], I8, kind="ExternalInput").ap()
    uq = nc.dram_tensor("uq", [IC, 128, H], I8, kind="ExternalInput").ap()
    dq = nc.dram_tensor("dq", [128, IC, 256 * NCP], I8,
                        kind="ExternalInput").ap()
    gs = nc.dram_tensor("gs", [IC, H], F32, kind="ExternalInput").ap()
    us = nc.dram_tensor("us", [IC, H], F32, kind="ExternalInput").ap()
    dsc = nc.dram_tensor("dsc", [NCP, IC * 256], F32,
                         kind="ExternalInput").ap()
    wq = nc.dram_tensor("wq", [16, 128, HC * 256], F32,
                        kind="ExternalInput").ap()
    wk = nc.dram_tensor("wk", [2, 128, HC * 512], F32,
                        kind="ExternalInput").ap()
    wv = nc.dram_tensor("wv", [2, 128, HC * 512], F32,
                        kind="ExternalInput").ap()
    wo = nc.dram_tensor("wo", [16, 128, HC * 256], F32,
                        kind="ExternalInput").ap()
    ck_in = nc.dram_tensor("ck", [64, S], F32, kind="ExternalInput").ap()
    sk_in = nc.dram_tensor("sk", [64, S], F32, kind="ExternalInput").ap()
    cq_in = nc.dram_tensor("cq", [64, S], F32, kind="ExternalInput").ap()
    sq_in = nc.dram_tensor("sq", [64, S], F32, kind="ExternalInput").ap()
    cmask = nc.dram_tensor("cmask", [128, MC0 + MC1], BF16,
                           kind="ExternalInput").ap()

    xout = nc.dram_tensor("xout", [1024, H], F32, kind="ExternalOutput").ap()
    o_dbg = None
    if dump is not None:
        o_dbg = nc.dram_tensor("o_dbg", [S, H], F32,
                               kind="ExternalOutput").ap()

    ident_np, ones_np, half_np = _build_consts()
    c_ident = nc.inline_tensor(ident_np, name="c_ident").ap()
    c_ones = nc.inline_tensor(ones_np, name="c_ones").ap()
    c_half = nc.inline_tensor(half_np, name="c_half").ap()

    wg_bf = nc.dram_tensor("wg_bf", [IC, 128, H], BF16, kind="Internal").ap()
    wu_bf = nc.dram_tensor("wu_bf", [IC, 128, H], BF16, kind="Internal").ap()
    wd_bf = nc.dram_tensor("wd_bf", [NCP, 128, IC * 256], BF16,
                           kind="Internal").ap()
    hT_scr = nc.dram_tensor("hT_scr", [2, 128, IC * 512], BF16,
                            kind="Internal").ap()
    x_scr = nc.dram_tensor("x_scr", [S, H], F32, kind="Internal").ap()

    es = ExitStack()
    with es:
        sbt = lambda n, sh, dt: es.enter_context(nc.sbuf_tensor(n, sh, dt))
        ident = sbt("ident", [128, 128], BF16)
        onesr = sbt("onesr", [1, 128], F32)
        halfr = sbt("halfr", [1, 128], F32)
        st1 = sbt("st1", [128, 1], F32)
        st2 = sbt("st2", [128, 1], F32)
        st3 = sbt("st3", [128, 1], F32)
        st4 = sbt("st4", [128, 1], F32)
        dsumb = sbt("dsumb", [128, 1], F32)
        rdsum = sbt("rdsum", [128, 1], F32)
        negm = sbt("negm", [128, 1], F32)

        ps_a = es.enter_context(nc.psum_tensor("ps_a", [128, 2048], F32))
        ps_b = es.enter_context(nc.psum_tensor("ps_b", [128, 1024], BF16))
        ps_b2 = es.enter_context(nc.psum_tensor("ps_b2", [128, 1024], BF16))
        ps_c = es.enter_context(nc.psum_tensor("ps_c", [128, 512], F32))
        ps_d = es.enter_context(nc.psum_tensor("ps_d", [128, 512], F32))

        sem = {}
        semnames = ["sP", "sV", "sA"]
        semnames += ["dS%d" % i for i in range(NROT)]
        semnames += ["dG%d" % i for i in range(NROT)]
        semnames += ["dA%d" % i for i in range(NROT)]
        for nm in semnames:
            sem[nm] = es.enter_context(nc.semaphore(nm))
        block = es.enter_context(nc.Block())
        pr = Prog(sem)
        SEM = sem

        def dma(q, out, in_, nm=None):
            nm = pr.qsem(q)
            v = pr.inc(nm, 16)
            s = sem[nm]
            pr.fresh[q] = True

            def go(e, out=out, in_=in_, s=s):
                o = out(e) if callable(out) else out
                i = in_(e) if callable(in_) else in_
                e.dma_start(out=o, in_=i).then_inc(s, 16)
            pr.on(q, go)
            return (nm, v)

        def relay(q):
            nm = pr.qsem(q)
            h = (nm, pr.cnt[nm])
            if pr.fresh[q]:
                pr.rot[q] = (pr.rot[q] + 1) % NROT
                pr.fresh[q] = False
            return h

        def x_rows(qg, rt):
            a = QPBASE[qg] + rt * 128
            return x_scr[a:a + 128, :]

        def x_strip_ap(qg, rt, cp):
            a = QPBASE[qg] + rt * 128
            return x_scr[a:a + 128, cp * 256:(cp + 1) * 256]

        ln_prev = {"ay": 0, "tr": 0}

        def ln_tile(src_rows, ybig, R, col0, xtile, ybuf):
            pr.wait("y", "sV", ln_prev["ay"])
            dma("y", xtile[:], src_rows)
            vload = relay("y")
            pr.wait("v", *vload)
            pr.op("v", lambda v: v.reduce_sum(st1[:], xtile[:], axis=AX.X,
                                              negate=True), selfwait=True)
            v_mu = pr.op("v", lambda v: v.tensor_scalar(
                st1[:], st1[:], 1.0 / H, None, op0=ALU.mult), selfwait=True)
            pr.wait("a", "sV", v_mu)
            pr.wait("a", *vload)
            pr.wait("a", "sP", ln_prev["tr"])
            a_sq = pr.op("a", lambda a: a.activation(
                ybuf[:], xtile[:], AF.Square, accum_out=st2[:]),
                selfwait=True)
            pr.wait("v", "sA", a_sq)
            pr.op("v", lambda v: v.tensor_mul(st3[:], st1[:], st1[:]),
                  selfwait=True)
            pr.op("v", lambda v: v.scalar_tensor_tensor(
                st3[:], st2[:], 1.0 / H, st3[:], op0=ALU.mult,
                op1=ALU.subtract), selfwait=True)
            v_var = pr.op("v", lambda v: v.tensor_scalar(
                st3[:], st3[:], EPS, None, op0=ALU.add), selfwait=True)
            pr.wait("a", "sV", v_var)
            a_ln = pr.op("a", lambda a: a.activation(st2[:], st3[:], AF.Ln),
                         selfwait=True)
            pr.wait("v", "sA", a_ln)
            v_nh = pr.op("v", lambda v: v.tensor_scalar(
                st3[:], st2[:], -0.5, None, op0=ALU.mult), selfwait=True)
            pr.wait("a", "sV", v_nh)
            a_rstd = pr.op("a", lambda a: a.activation(st2[:], st3[:],
                                                       AF.Exp),
                           selfwait=True)
            pr.wait("v", "sA", a_rstd)
            a_y = pr.op("v", lambda v: v.tensor_scalar(
                ybuf[:], xtile[:], st1[:, 0:1], st2[:, 0:1],
                op0=ALU.add, op1=ALU.mult), selfwait=True)
            pr.wait("t", "sV", a_y)
            cps = []
            y3 = ybig[:].rearrange("p (hc r) -> p hc r", hc=HC)
            for g8 in range(4):
                pb = ps_b if (g8 % 2 == 0) else ps_b2
                if g8 >= 2:
                    pr.wait("t", "sV", cps[g8 - 2])
                for j in range(8):
                    hc = g8 * 8 + j
                    if j < 7:
                        pr.on("t", lambda t, pb=pb, j=j, hc=hc:
                              t.transpose(pb[:, j * 128:(j + 1) * 128],
                                          ybuf[:, hc * 128:(hc + 1) * 128],
                                          ident[:]))
                    else:
                        pr.op("t", lambda t, pb=pb, j=j, hc=hc:
                              t.transpose(pb[:, j * 128:(j + 1) * 128],
                                          ybuf[:, hc * 128:(hc + 1) * 128],
                                          ident[:]))
                p_tr = pr.cnt["sP"]
                pr.wait("v", "sP", p_tr)
                cps.append(pr.op("v", lambda v, pb=pb, g8=g8, y3=y3,
                                 col0=col0:
                                 v.tensor_copy(
                                     y3[:, g8 * 8:(g8 + 1) * 8,
                                        col0:col0 + 128], pb[:]),
                                 selfwait=True))
            ln_prev["ay"] = a_y
            ln_prev["tr"] = pr.cnt["sP"]

        def ffn_pass(tag, src512, sstrip, dstrip, fused, wd_fused,
                     sub=99):
            sc = ExitStack()
            ybig = sc.enter_context(nc.sbuf_tensor(f"ybig_{tag}",
                                                   [128, HC * 1024], BF16))
            with sc:
                sgu = ExitStack()
                wblk = sgu.enter_context(nc.sbuf_tensor(f"wblk_{tag}",
                                                        [128, 4 * H], BF16))

                hstage = sgu.enter_context(nc.sbuf_tensor(f"hst_{tag}",
                                                          [128, 1024], BF16))
                silu_t = sgu.enter_context(nc.sbuf_tensor(f"sil_{tag}",
                                                          [128, 1024], BF16))
                with sgu:
                    if sub < 1:
                        return
                    sln = ExitStack()
                    xtile = sln.enter_context(nc.sbuf_tensor(
                        f"xt_{tag}", [128, H], F32))
                    ybuf = sln.enter_context(nc.sbuf_tensor(
                        f"yb_{tag}", [128, H], BF16))
                    with sln:
                        for gi in range(2):
                            for rt in range(4):
                                ln_tile(src512(gi, rt), ybig, 1024,
                                        gi * 512 + rt * 128, xtile, ybuf)
                        pr.barrier()
                    qint = sgu.enter_context(nc.sbuf_tensor(f"qint_{tag}",
                                                            [128, 3 * H],
                                                            I8))
                    sF = sgu.enter_context(nc.sbuf_tensor(f"sF_{tag}",
                                                          [128, 4 * H],
                                                          F32))
                    if sub < 2:
                        return
                    sttj = {}
                    stt_last = {}
                    mm_done = {}
                    ev_done = {}
                    h_store = {}
                    w_load = {}
                    scr_w = {}
                    for ib in range(IC):
                        slot = (ib % 2) * 2 * H
                        if fused:
                            ldq_m = {}
                            qs_m = {}
                            flc = {}
                            for m in range(2):
                                qs_m[m] = ((2 * ib + m) % 3) * H
                                if 2 * ib + m >= 3:
                                    pm = 2 * ib + m - 3
                                    pr.wait("y", "sV",
                                            sttj[(pm // 2,
                                                  (pm % 2) * 8 + 7)])
                                dma("y", qint[:, qs_m[m]:qs_m[m] + H],
                                    (gq if m == 0 else uq)[ib])
                                ldq_m[m] = relay("y")
                            sfs = (ib % 2) * 2 * H
                            for m in range(2):
                                src = gs if m == 0 else us
                                for c in range(4):
                                    if ib >= 2:
                                        pr.wait("g", "sV",
                                                sttj[(ib - 2,
                                                      m * 8 + 2 * c + 1)])
                                    dma("g", sF[:, sfs + m * H + c * 1024:
                                                sfs + m * H +
                                                (c + 1) * 1024],
                                        src[ib:ib + 1,
                                            c * 1024:(c + 1) * 1024]
                                        .partition_broadcast(128), "dG")
                                    flc[(m, c)] = relay("g")
                            if ib >= 2:
                                pr.wait("t", "sV", ev_done[(ib - 2) * 2 + 1])
                            for m in range(2):
                                for sp in range(8):
                                    j = m * 8 + sp
                                    if sp == 0:
                                        pr.wait("v", *ldq_m[m])
                                    if sp % 2 == 0:
                                        pr.wait("v", *flc[(m, sp // 2)])
                                    if j == 0 and ib >= 2:
                                        pr.wait("v", "sP",
                                                mm_done[(ib - 2) * 2
                                                        + 1])
                                        if (ib - 2) in scr_w:
                                            pr.wait("v",
                                                    *scr_w[ib - 2])
                                    pr.on("v", lambda v, m=m, sp=sp,
                                          slot=slot, qs=qs_m[m], sfs=sfs:
                                          v.scalar_tensor_tensor(
                                              wblk[:, slot + m * H +
                                                   sp * 512:
                                                   slot + m * H +
                                                   (sp + 1) * 512],
                                              qint[:, qs + sp * 512:
                                                   qs + (sp + 1) * 512],
                                              8.0,
                                              sF[:, sfs + m * H +
                                                 sp * 512:
                                                 sfs + m * H +
                                                 (sp + 1) * 512],
                                              op0=ALU.subtract,
                                              op1=ALU.mult)
                                          .then_inc(SEM["sV"], 1))
                                    sttj[(ib, j)] = pr.inc("sV")
                            stt_last[ib] = sttj[(ib, 15)]
                            pr.wait("a", "sV", stt_last[ib])
                            dma("a", wg_bf[ib], wblk[:, slot:slot + H])
                            dma("a", wu_bf[ib],
                                wblk[:, slot + H:slot + 2 * H])
                            scr_w[ib] = relay("a")
                        else:
                            if ib >= 2:
                                pr.wait("y", "sP", mm_done[(ib - 2) * 2 + 1])
                            dma("y", wblk[:, slot:slot + H], wg_bf[ib])
                            dma("y", wblk[:, slot + H:slot + 2 * H],
                                wu_bf[ib])
                            w_load[ib] = relay("y")
                        for rc in range(2):
                            step = ib * 2 + rc
                            setb = (step % 2) * 1024
                            if fused:
                                pr.wait("t", "sV", stt_last[ib])
                            else:
                                pr.wait("t", *w_load[ib])
                            if step >= 2:
                                pr.wait("t", "sV", ev_done[step - 2])
                            for m in range(2):
                                for hc in range(HC):
                                    last = (hc == HC - 1)

                                    def mmf(t, m=m, hc=hc, slot=slot,
                                            setb=setb, rc=rc, last=last):
                                        ins = t.matmul(
                                            ps_a[:, setb + m * 512:
                                                 setb + (m + 1) * 512],
                                            wblk[:, slot + m * H + hc * 128:
                                                 slot + m * H +
                                                 (hc + 1) * 128],
                                            ybig[:, hc * 1024 + rc * 512:
                                                 hc * 1024 +
                                                 (rc + 1) * 512],
                                            start=(hc == 0), stop=last)
                                        if last:
                                            ins.then_inc(SEM["sP"], 1)
                                    pr.on("t", mmf)
                                pr.inc("sP")
                            mm_done[step] = pr.cnt["sP"]
                            par = step % 2
                            pr.wait("a", "sP", mm_done[step])
                            if step >= 2:
                                pr.wait("a", *h_store[step - 2])
                            pr.on("a", lambda a, setb=setb, par=par:
                                  a.activation(silu_t[:, par * 512:
                                                      (par + 1) * 512],
                                               ps_a[:, setb:setb + 512],
                                               AF.Silu)
                                  .then_inc(SEM["sA"], 1))
                            a_si = pr.inc("sA")
                            pr.wait("v", "sA", a_si)
                            if step >= 2:
                                pr.wait("v", *h_store[step - 2])
                            pr.on("v", lambda v, setb=setb, par=par:
                                  v.scalar_tensor_tensor(
                                      hstage[:, par * 512:(par + 1) * 512],
                                      silu_t[:, par * 512:(par + 1) * 512],
                                      0.5,
                                      ps_a[:, setb + 512:setb + 1024],
                                      op0=ALU.mult, op1=ALU.mult)
                                  .then_inc(SEM["sV"], 1))
                            ev_done[step] = pr.inc("sV")
                            pr.wait("a", "sV", ev_done[step])
                            dma("a",
                                hT_scr[rc, :, ib * 512:(ib + 1) * 512],
                                hstage[:, par * 512:(par + 1) * 512])
                            h_store[step] = relay("a")
                    pr.barrier()
                sc.close()

                if sub < 3:
                    return
                if wd_fused:
                    sd = ExitStack()
                    wdq = (IC + 3) // 4
                    qd = sd.enter_context(nc.sbuf_tensor(f"qd_{tag}",
                                                         [128,
                                                          2 * wdq * 256],
                                                         I8))
                    wdp = sd.enter_context(nc.sbuf_tensor(f"wdp_{tag}",
                                                          [128, IC * 256],
                                                          BF16))
                    srd = sd.enter_context(nc.sbuf_tensor(f"srd_{tag}",
                                                          [1, 2 * wdq * 256],
                                                          F32))
                    with sd:
                        import itertools
                        qpick = itertools.cycle(("y", "g", "a"))
                        wd_bcq = {}
                        wd_str = {}
                        slot_prev = {0: 0, 1: 0}
                        gpair = 0
                        for cp in range(NCP):
                            for qt4 in range(4):
                                ic0 = qt4 * wdq
                                nic = min(wdq, IC - ic0)
                                if nic <= 0:
                                    continue
                                qs = (qt4 % 2) * wdq * 256
                                qq = next(qpick)
                                pr.wait(qq, "sV", slot_prev[qt4 % 2])
                                dma(qq, qd[:, qs:qs + nic * 256],
                                    dq[:, ic0:ic0 + nic,
                                       cp * 256:(cp + 1) * 256])
                                dma(qq, srd[:, qs:qs + nic * 256],
                                    dsc[cp:cp + 1,
                                        ic0 * 256:(ic0 + nic) * 256])
                                ldq = relay(qq)
                                pr.wait("t", *ldq)
                                for pj in range((nic + 1) // 2):
                                    np2 = min(2, nic - pj * 2)
                                    pb = ps_c if (gpair % 2 == 0) else ps_d
                                    if gpair >= 2 and (gpair - 2) in wd_bcq:
                                        pr.wait("t", "sV", wd_bcq[gpair - 2])
                                    pr.on("t", lambda t, pb=pb, qs=qs,
                                          pj=pj, np2=np2:
                                          t.matmul(pb[:, 0:np2 * 256],
                                                   onesr[:],
                                                   srd[0:1,
                                                       qs + pj * 512:
                                                       qs + pj * 512 +
                                                       np2 * 256],
                                                   start=True, stop=True)
                                          .then_inc(SEM["sP"], 1))
                                    p_bc = pr.inc("sP")
                                    pr.wait("v", "sP", p_bc)
                                    if pj == 0:
                                        pr.wait("v", *ldq)
                                        if cp >= 1 and qt4 == 0:
                                            pr.wait("v", *wd_str[cp - 1])
                                    for s2 in range(np2):
                                        ii = pj * 2 + s2
                                        icb = ic0 + ii
                                        pr.on("v", lambda v, pb=pb, qs=qs,
                                              ii=ii, icb=icb, s2=s2:
                                              v.scalar_tensor_tensor(
                                                  wdp[:, icb * 256:
                                                      (icb + 1) * 256],
                                                  qd[:, qs + ii * 256:
                                                     qs + (ii + 1) * 256],
                                                  8.0,
                                                  pb[:, s2 * 256:
                                                     (s2 + 1) * 256],
                                                  op0=ALU.subtract,
                                                  op1=ALU.mult)
                                              .then_inc(SEM["sV"], 1))
                                        last_stt = pr.inc("sV")
                                    wd_bcq[gpair] = last_stt
                                    gpair += 1
                                slot_prev[qt4 % 2] = last_stt
                            sq4 = next(qpick)
                            pr.wait(sq4, "sV", last_stt)
                            dma(sq4, wd_bf[cp], wdp[:])
                            wd_str[cp] = relay(sq4)
                    pr.barrier()

                if sub < 4:
                    return
                for gi in range(2):
                    sg = ExitStack()
                    hTg = sg.enter_context(nc.sbuf_tensor(
                        f"hTg_{tag}{gi}", [128, IC * 512], BF16))
                    wdpan = sg.enter_context(nc.sbuf_tensor(
                        f"wdpan_{tag}{gi}", [128, 2 * IC * 256], BF16))
                    xstrip = sg.enter_context(nc.sbuf_tensor(
                        f"xs_{tag}{gi}", [128, 4 * 256], F32))
                    xnw = sg.enter_context(nc.sbuf_tensor(
                        f"xn_{tag}{gi}", [128, 4 * 256], F32))
                    with sg:
                        dma("y", hTg[:], hT_scr[gi])
                        mm_d = {}
                        ev_d = {}
                        st_d = {}
                        for cp in range(NCP):
                            slot = (cp % 2) * IC * 256
                            if cp >= 2:
                                pr.wait("y", "sP", mm_d[(cp - 2, 3)])
                            dma("y", wdpan[:, slot:slot + IC * 256],
                                wd_bf[cp])
                            ldw = relay("y")
                            for rt in range(4):
                                pr.wait("t", *ldw)
                                if cp >= 1:
                                    pr.wait("t", "sV", ev_d[(cp - 1, rt)])
                                for icb in range(IC):
                                    last = (icb == IC - 1)

                                    def mmd(t, icb=icb, rt=rt, slot=slot,
                                            last=last, hTg=hTg,
                                            wdpan=wdpan):
                                        ins = t.matmul(
                                            ps_a[:, rt * 512:
                                                 rt * 512 + 256],
                                            hTg[:, icb * 512 + rt * 128:
                                                icb * 512 +
                                                (rt + 1) * 128],
                                            wdpan[:, slot + icb * 256:
                                                  slot + (icb + 1) * 256],
                                            start=(icb == 0), stop=last)
                                        if last:
                                            ins.then_inc(SEM["sP"], 1)
                                    pr.on("t", mmd)
                                mm_d[(cp, rt)] = pr.inc("sP")
                                sx = rt
                                if cp >= 1:
                                    pr.wait("y", "sV", ev_d[(cp - 1, rt)])
                                dma("y", xstrip[:, sx * 256:
                                                (sx + 1) * 256],
                                    sstrip(gi, rt, cp))
                                lx = relay("y")
                                pr.wait("v", "sP", mm_d[(cp, rt)])
                                pr.wait("v", *lx)
                                if cp >= 1:
                                    pr.wait("v", *st_d[(cp - 1, rt)])
                                pr.on("v", lambda v, sx=sx, rt=rt,
                                      xnw=xnw, xstrip=xstrip:
                                      v.scalar_tensor_tensor(
                                          xnw[:, sx * 256:(sx + 1) * 256],
                                          ps_a[:, rt * 512:rt * 512 + 256],
                                          1.0,
                                          xstrip[:, sx * 256:
                                                 (sx + 1) * 256],
                                          op0=ALU.mult, op1=ALU.add)
                                      .then_inc(SEM["sV"], 1))
                                ev_d[(cp, rt)] = pr.inc("sV")
                                pr.wait("g", "sV", ev_d[(cp, rt)])
                                dma("g", dstrip(gi, rt, cp),
                                    xnw[:, sx * 256:(sx + 1) * 256], "dG")
                                st_d[(cp, rt)] = relay("g")
                    pr.barrier()

        def run_attn_qg(qg, kbuf, vbuf):
            sa = ExitStack()
            qbuf = sa.enter_context(nc.sbuf_tensor(f"qbuf{qg}",
                                                   [128, NQ * 512], BF16))
            cosq = sa.enter_context(nc.sbuf_tensor(f"cosq{qg}", [64, 512],
                                                   F32))
            sinq = sa.enter_context(nc.sbuf_tensor(f"sinq{qg}", [64, 512],
                                                   F32))
            with sa:
                dma("y", cosq[:], cq_in[:, QPBASE[qg]:QPBASE[qg] + 512])
                dma("y", sinq[:], sq_in[:, QPBASE[qg]:QPBASE[qg] + 512])
                relay("y")
                sq = ExitStack()
                ybig = sq.enter_context(nc.sbuf_tensor(f"ybig_q{qg}",
                                                       [128, HC * 512],
                                                       BF16))
                with sq:
                    sln2 = ExitStack()
                    xtile = sln2.enter_context(nc.sbuf_tensor(
                        f"xt_q{qg}", [128, H], F32))
                    ybuf = sln2.enter_context(nc.sbuf_tensor(
                        f"yb_q{qg}", [128, H], BF16))
                    with sln2:
                        for rt in range(4):
                            ln_tile(x_rows(qg, rt), ybig, 512, rt * 128,
                                    xtile, ybuf)
                        pr.barrier()
                    wqblk = sq.enter_context(nc.sbuf_tensor(
                        f"wqb{qg}", [128, 2 * HC * 256], BF16))
                    q_rope = {}
                    q_mm = {}
                    for kvh in range(NKV):
                        for hf in range(2):
                            blkid = kvh * 2 + hf
                            slot = (blkid % 2) * HC * 256
                            if blkid >= 2:
                                pr.wait("g", "sP", q_mm[blkid - 2])
                            dma("g", wqblk[:, slot:slot + HC * 256],
                                wq[blkid], "dG")
                            lw = relay("g")
                            for qh2 in range(2):
                                qh = kvh * 4 + hf * 2 + qh2
                                pb = ps_c if (qh % 2 == 0) else ps_d
                                pr.wait("t", *lw)
                                if qh >= 2:
                                    pr.wait("t", "sV", q_rope[qh - 2])
                                for hc in range(HC):
                                    last = (hc == HC - 1)

                                    def mq(t, pb=pb, hc=hc, slot=slot,
                                           qh2=qh2, last=last):
                                        ins = t.matmul(
                                            pb[:],
                                            wqblk[:, slot + hc * 256 +
                                                  qh2 * 128:
                                                  slot + hc * 256 +
                                                  (qh2 + 1) * 128],
                                            ybig[:, hc * 512:
                                                 (hc + 1) * 512],
                                            start=(hc == 0), stop=last)
                                        if last:
                                            ins.then_inc(SEM["sP"], 1)
                                    pr.on("t", mq)
                                q_mm[blkid] = pr.inc("sP")
                                pr.wait("v", "sP", q_mm[blkid])
                                co = qh * 512
                                pr.op("v", lambda v, pb=pb:
                                      v.tensor_mul(rt1[:], pb[0:64, :],
                                                   cosq[:]), selfwait=True)
                                pr.op("v", lambda v, pb=pb:
                                      v.tensor_mul(rt2[:], pb[64:128, :],
                                                   sinq[:]), selfwait=True)
                                pr.op("v", lambda v, pb=pb:
                                      v.tensor_mul(rt3[:], pb[64:128, :],
                                                   cosq[:]), selfwait=True)
                                pr.op("v", lambda v, pb=pb:
                                      v.tensor_mul(rt4[:], pb[0:64, :],
                                                   sinq[:]), selfwait=True)
                                pr.op("v", lambda v, co=co:
                                      v.tensor_sub(qbuf[0:64, co:co + 512],
                                                   rt1[:], rt2[:]),
                                      selfwait=True)
                                q_rope[qh] = pr.op(
                                    "v", lambda v, co=co:
                                    v.tensor_add(qbuf[64:128, co:co + 512],
                                                 rt3[:], rt4[:]),
                                    selfwait=True)
                    pr.barrier()

                sc2 = ExitStack()
                outT = sc2.enter_context(nc.sbuf_tensor(f"outT{qg}",
                                                        [128, NQ * 512],
                                                        BF16))
                sc3 = sc2.enter_context(ExitStack())
                maskg = sc3.enter_context(nc.sbuf_tensor(f"maskg{qg}",
                                                         [128, MC1], BF16))
                pbuf = sc3.enter_context(nc.sbuf_tensor(f"pbuf{qg}",
                                                        [128, 2048], BF16))
                ptb = sc3.enter_context(nc.sbuf_tensor(f"ptb{qg}",
                                                       [128, 16 * 512],
                                                       BF16))
                if True:
                    mcols = MC0 if qg == 0 else MC1
                    moff0 = 0 if qg == 0 else MC0
                    dma("y", maskg[:, 0:mcols],
                        cmask[:, moff0:moff0 + mcols])
                    vm = relay("y")
                    ptb3 = ptb[:].rearrange("p (kt q) -> p kt q", kt=16)
                    exp_dn = {}
                    ptc_dn = {}
                    ptm_dn = {}
                    av_ev = {}
                    v_sc = {}

                    def emit_S(kvh, qh, qt):
                        E = EMAX[qg][qt]
                        W = E * 128
                        nch = (W + 511) // 512
                        if (qh, qt - 1) in exp_dn:
                            pr.wait("t", "sA", exp_dn[(qh, qt - 1)])
                        elif qt == 0 and qh > 0:
                            pr.wait("t", "sA", exp_dn[(qh - 1, 3)])
                        for c in range(nch):
                            Wc = min(512, W - c * 512)
                            last = (c == nch - 1)

                            def ms(t, c=c, Wc=Wc, qh=qh, qt=qt, kvh=kvh,
                                   last=last):
                                ins = t.matmul(
                                    ps_a[:, c * 512:c * 512 + Wc],
                                    qbuf[:, qh * 512 + qt * 128:
                                         qh * 512 + (qt + 1) * 128],
                                    kbuf[:, kvh * S + c * 512:
                                         kvh * S + c * 512 + Wc],
                                    start=True, stop=True)
                                if last:
                                    ins.then_inc(SEM["sP"], 1)
                            pr.on("t", ms)
                        return pr.inc("sP")

                    def emit_soft(qh, qt, p_s):
                        E = EMAX[qg][qt]
                        W = E * 128
                        mo = MOFF[(qg, qt)]
                        pr.wait("v", "sP", p_s)
                        v_mx = pr.op(
                            "v", lambda v, W=W:
                            v.reduce_max(negm[:], ps_a[:, 0:W],
                                         axis=AX.X, negate=True),
                            selfwait=True)
                        pr.wait("a", "sV", v_mx)
                        if (qh, qt - 1) in ptm_dn:
                            pr.wait("a", "sP", ptm_dn[(qh, qt - 1)])
                        elif qt == 0 and qh > 0:
                            pr.wait("a", "sP", ptm_dn[(qh - 1, 3)])
                        exp_dn[(qh, qt)] = pr.op(
                            "a", lambda a, W=W:
                            a.activation(pbuf[:, 0:W], ps_a[:, 0:W],
                                         AF.Exp, bias=negm[:, 0:1]),
                            selfwait=True)
                        pr.wait("v", "sA", exp_dn[(qh, qt)])
                        pr.wait("v", *vm)
                        pr.op("v", lambda v, W=W, mo=mo:
                              v.scalar_tensor_tensor(
                                  pbuf[:, 0:W], pbuf[:, 0:W], 1.0,
                                  maskg[:, mo:mo + W],
                                  op0=ALU.mult, op1=ALU.mult,
                                  accum_out=dsumb[:]), selfwait=True)
                        pr.op("v", lambda v: v.reciprocal(rdsum[:],
                                                          dsumb[:]),
                              selfwait=True)
                        v_sc[(qh, qt)] = pr.op(
                            "v", lambda v, W=W:
                            v.tensor_scalar(pbuf[:, 0:W], pbuf[:, 0:W],
                                            rdsum[:, 0:1], None,
                                            op0=ALU.mult), selfwait=True)

                    def emit_PT(qh, qt):
                        E = EMAX[qg][qt]
                        pr.wait("t", "sV", v_sc[(qh, qt)])
                        if (qh, qt - 1) in ptc_dn:
                            pr.wait("t", "sV", ptc_dn[(qh, qt - 1)])
                        elif qt == 0 and qh > 0:
                            pr.wait("t", "sV", ptc_dn[(qh - 1, 3)])
                        if qt == 0 and qh > 0:
                            pr.wait("t", "sV", av_ev[qh - 1])
                        for kt in range(E):
                            pb = ps_b if kt < 8 else ps_b2
                            jj = kt % 8
                            last = (kt == E - 1)

                            def mt(t, pb=pb, jj=jj, kt=kt, last=last):
                                ins = t.transpose(
                                    pb[:, jj * 128:(jj + 1) * 128],
                                    pbuf[:, kt * 128:(kt + 1) * 128],
                                    ident[:])
                                if last:
                                    ins.then_inc(SEM["sP"], 1)
                            pr.on("t", mt)
                        ptm_dn[(qh, qt)] = pr.inc("sP")
                        pr.wait("v", "sP", ptm_dn[(qh, qt)])
                        n1 = min(E, 8)
                        pr.op("v", lambda v, n1=n1, qt=qt:
                              v.tensor_copy(
                                  ptb3[:, 0:n1, qt * 128:(qt + 1) * 128],
                                  ps_b[:, 0:n1 * 128]), selfwait=True)
                        if E > 8:
                            pr.op("v", lambda v, E=E, qt=qt:
                                  v.tensor_copy(
                                      ptb3[:, 8:E,
                                           qt * 128:(qt + 1) * 128],
                                      ps_b2[:, 0:(E - 8) * 128]),
                                  selfwait=True)
                        ptc_dn[(qh, qt)] = pr.cnt["sV"]

                    pend_S = None
                    pend_av = None
                    for kvh in range(NKV):
                        for qh4 in range(4):
                            qh = kvh * 4 + qh4
                            p_s = (pend_S if pend_S is not None
                                   else emit_S(kvh, qh, 0))
                            pend_S = None
                            for qt in range(4):
                                emit_soft(qh, qt, p_s)
                                if qt == 0 and pend_av is not None:
                                    pend_av()
                                    pend_av = None
                                if qt < 3:
                                    p_s = emit_S(kvh, qh, qt + 1)
                                elif qh + 1 < NQ:
                                    pend_S = emit_S((qh + 1) // 4,
                                                    qh + 1, 0)
                                emit_PT(qh, qt)
                            pr.wait("t", "sV", ptc_dn[(qh, 3)])
                            if qh > 0:
                                pr.wait("t", "sV", av_ev[qh - 1])
                            Emax3 = EMAX[qg][3]
                            for kt in range(Emax3):
                                qtmin = 0 if qg == 0 else max(0, kt - 12)
                                off = qtmin * 128
                                last = (kt == Emax3 - 1)

                                def mav(t, kt=kt, off=off, kvh=kvh,
                                        last=last):
                                    ins = t.matmul(
                                        ps_c[:, off:512],
                                        vbuf[:, (kt * NKV + kvh) * 128:
                                             (kt * NKV + kvh + 1) * 128],
                                        ptb[:, kt * 512 + off:
                                            (kt + 1) * 512],
                                        start=(kt == 0), stop=last)
                                    if last:
                                        ins.then_inc(SEM["sP"], 1)
                                pr.on("t", mav)
                            av_mm = pr.inc("sP")

                            def do_av_ev(qh=qh, av_mm=av_mm):
                                pr.wait("v", "sP", av_mm)
                                av_ev[qh] = pr.op(
                                    "v", lambda v, qh=qh:
                                    v.tensor_copy(outT[:, qh * 512:
                                                       (qh + 1) * 512],
                                                  ps_c[:]), selfwait=True)
                            pend_av = do_av_ev
                    pend_av()
                    pr.barrier()
                sc3.close()

                sw = ExitStack()
                woblk = sw.enter_context(nc.sbuf_tensor(f"wob{qg}",
                                                        [128, 2 * HC * 256],
                                                        BF16))
                xstrip = sw.enter_context(nc.sbuf_tensor(f"xsw{qg}",
                                                         [128, 4 * 256],
                                                         F32))
                xnw = sw.enter_context(nc.sbuf_tensor(f"xnw{qg}",
                                                      [128, 4 * 256], F32))
                with sw:
                    mm_o = {}
                    ev_o = {}
                    st_o = {}
                    for cp in range(NCP):
                        slot = (cp % 2) * HC * 256
                        if cp >= 2:
                            pr.wait("g", "sP", mm_o[(cp - 2, 3)])
                        dma("g", woblk[:, slot:slot + HC * 256],
                            wo[cp], "dG")
                        lw = relay("g")
                        for rt in range(4):
                            pr.wait("t", *lw)
                            if cp >= 1:
                                pr.wait("t", "sV", ev_o[(cp - 1, rt)])
                            for hc in range(HC):
                                last = (hc == HC - 1)

                                def mo_(t, hc=hc, rt=rt, slot=slot,
                                        last=last):
                                    ins = t.matmul(
                                        ps_a[:, rt * 512:rt * 512 + 256],
                                        outT[:, hc * 512 + rt * 128:
                                             hc * 512 + (rt + 1) * 128],
                                        woblk[:, slot + hc * 256:
                                              slot + (hc + 1) * 256],
                                        start=(hc == 0), stop=last)
                                    if last:
                                        ins.then_inc(SEM["sP"], 1)
                                pr.on("t", mo_)
                            mm_o[(cp, rt)] = pr.inc("sP")
                            sx = rt
                            if cp >= 1:
                                pr.wait("y", "sV", ev_o[(cp - 1, rt)])
                            dma("y",
                                xstrip[:, sx * 256:(sx + 1) * 256],
                                x_strip_ap(qg, rt, cp))
                            lx = relay("y")
                            pr.wait("v", "sP", mm_o[(cp, rt)])
                            pr.wait("v", *lx)
                            if cp >= 1:
                                pr.wait("v", *st_o[(cp - 1, rt)])
                            pr.on("v", lambda v, sx=sx, rt=rt:
                                  v.scalar_tensor_tensor(
                                      xnw[:, sx * 256:(sx + 1) * 256],
                                      ps_a[:, rt * 512:rt * 512 + 256],
                                      1.0,
                                      xstrip[:, sx * 256:(sx + 1) * 256],
                                      op0=ALU.mult, op1=ALU.add)
                                  .then_inc(SEM["sV"], 1))
                            ev_o[(cp, rt)] = pr.inc("sV")
                            pr.wait("g", "sV", ev_o[(cp, rt)])
                            dma("g", x_strip_ap(qg, rt, cp),
                                xnw[:, sx * 256:(sx + 1) * 256], "dG")
                            st_o[(cp, rt)] = relay("g")
                sc2.close()
            pr.barrier()

        dma("y", ident[:], c_ident)
        dma("y", onesr[:], c_ones)
        dma("y", halfr[:], c_half)
        relay("y")
        pr.barrier()

        for pi in range(npass):
            r0 = pi * 1024

            def s512(gi, rt, r0=r0):
                a = r0 + gi * 512 + rt * 128
                return xin[a:a + 128, :]

            def sstrip(gi, rt, cp, r0=r0):
                a = r0 + gi * 512 + rt * 128
                return xin[a:a + 128, cp * 256:(cp + 1) * 256]

            def dstrip(gi, rt, cp, r0=r0):
                a = r0 + gi * 512 + rt * 128
                return x_scr[a:a + 128, cp * 256:(cp + 1) * 256]

            ffn_pass(f"f1{pi}", s512, sstrip, dstrip,
                     fused=(pi == 0), wd_fused=(pi == 0), sub=sub)

        if dump == "x1":
            dma("y", o_dbg, x_scr)
            pr.barrier()

        if phases >= 2:
            kv_outer = ExitStack()
            cosk = kv_outer.enter_context(nc.sbuf_tensor("cosk", [64, S],
                                                         F32))
            sink = kv_outer.enter_context(nc.sbuf_tensor("sink", [64, S],
                                                         F32))
            rt1 = kv_outer.enter_context(nc.sbuf_tensor("rt1", [64, 512],
                                                        F32))
            rt2 = kv_outer.enter_context(nc.sbuf_tensor("rt2", [64, 512],
                                                        F32))
            rt3 = kv_outer.enter_context(nc.sbuf_tensor("rt3", [64, 512],
                                                        F32))
            rt4 = kv_outer.enter_context(nc.sbuf_tensor("rt4", [64, 512],
                                                        F32))
            kbuf = kv_outer.enter_context(
                nc.sbuf_tensor("kbuf", [128, NKV * S], BF16))
            vbuf = kv_outer.enter_context(
                nc.sbuf_tensor("vbuf", [128, (S // 128) * NKV * DH], BF16))
            with kv_outer:
                dma("y", cosk[:], ck_in)
                dma("y", sink[:], sk_in)
                relay("y")
                sk = ExitStack()
                wkvbf = sk.enter_context(nc.sbuf_tensor(
                    "wkvbf", [128, HC * 512], BF16))
                ybig_kv = sk.enter_context(nc.sbuf_tensor(
                    "ybig_kv", [128, HC * 512], BF16))
                xt_kv = sk.enter_context(nc.sbuf_tensor("xt_kv", [128, H],
                                                        F32))
                yb_kv = sk.enter_context(nc.sbuf_tensor("yb_kv", [128, H],
                                                        BF16))
                with sk:
                    wkbf = wkvbf
                    wvbf = wkvbf
                    for ch in range(4):
                        for rt in range(4):
                            ln_tile(x_scr[ch * 512 + rt * 128:
                                          ch * 512 + (rt + 1) * 128, :],
                                    ybig_kv, 512, rt * 128, xt_kv, yb_kv)
                        pr.barrier()
                        rope_dn = {}
                        for kvh in range(NKV):
                            if kvh % 4 == 0:
                                kh = kvh // 4
                                if ch + kh > 0:
                                    pr.wait("g", "sP", pr.cnt["sP"])
                                dma("g", wkvbf[:], wk[kh], "dG")
                                lw = relay("g")
                                pr.wait("t", *lw)
                            kvl = kvh % 4
                            pb = ps_c if (kvh % 2 == 0) else ps_d
                            if kvh >= 2:
                                pr.wait("t", "sV", rope_dn[kvh - 2])
                            for hc in range(HC):
                                last = (hc == HC - 1)

                                def mk(t, pb=pb, hc=hc, kvl=kvl, last=last):
                                    ins = t.matmul(
                                        pb[:],
                                        wkvbf[:, hc * 512 + kvl * 128:
                                              hc * 512 + (kvl + 1) * 128],
                                        ybig_kv[:, hc * 512:
                                                (hc + 1) * 512],
                                        start=(hc == 0), stop=last)
                                    if last:
                                        ins.then_inc(SEM["sP"], 1)
                                pr.on("t", mk)
                            p_k = pr.inc("sP")
                            pr.wait("v", "sP", p_k)
                            co = kvh * S + ch * 512
                            cs = (ch * 512, (ch + 1) * 512)
                            pr.op("v", lambda v, pb=pb, cs=cs:
                                  v.tensor_mul(rt1[:], pb[0:64, :],
                                               cosk[:, cs[0]:cs[1]]),
                                  selfwait=True)
                            pr.op("v", lambda v, pb=pb, cs=cs:
                                  v.tensor_mul(rt2[:], pb[64:128, :],
                                               sink[:, cs[0]:cs[1]]),
                                  selfwait=True)
                            pr.op("v", lambda v, pb=pb, cs=cs:
                                  v.tensor_mul(rt3[:], pb[64:128, :],
                                               cosk[:, cs[0]:cs[1]]),
                                  selfwait=True)
                            pr.op("v", lambda v, pb=pb, cs=cs:
                                  v.tensor_mul(rt4[:], pb[0:64, :],
                                               sink[:, cs[0]:cs[1]]),
                                  selfwait=True)
                            pr.op("v", lambda v, co=co:
                                  v.tensor_sub(kbuf[0:64, co:co + 512],
                                               rt1[:], rt2[:]),
                                  selfwait=True)
                            rope_dn[kvh] = pr.op(
                                "v", lambda v, co=co:
                                v.tensor_add(kbuf[64:128, co:co + 512],
                                             rt3[:], rt4[:]),
                                selfwait=True)
                        vev = {}
                        for j in range(8):
                            dsp, kt = divmod(j, 4)
                            if kt == 0:
                                pr.wait("g", "sP", pr.cnt["sP"])
                                dma("g", wkvbf[:], wv[dsp], "dG")
                                lwv = relay("g")
                                pr.wait("t", *lwv)
                            pb = ps_c if (j % 2 == 0) else ps_d
                            if j >= 2:
                                pr.wait("t", "sV", vev[j - 2])
                            else:
                                pr.wait("t", "sV", rope_dn[NKV - 2 + j])
                            for hc in range(HC):
                                last = (hc == HC - 1)

                                def mv(t, pb=pb, hc=hc, kt=kt, last=last):
                                    ins = t.matmul(
                                        pb[:],
                                        ybig_kv[:, hc * 512 + kt * 128:
                                                hc * 512 +
                                                (kt + 1) * 128],
                                        wkvbf[:, hc * 512:(hc + 1) * 512],
                                        start=(hc == 0), stop=last)
                                    if last:
                                        ins.then_inc(SEM["sP"], 1)
                                pr.on("t", mv)
                            p_v = pr.inc("sP")
                            pr.wait("v", "sP", p_v)
                            ktg = ch * 4 + kt
                            vev[j] = pr.op(
                                "v", lambda v, pb=pb, ktg=ktg, dsp=dsp:
                                v.tensor_copy(
                                    vbuf[:, ktg * 1024 + dsp * 512:
                                         ktg * 1024 + (dsp + 1) * 512],
                                    pb[:]), selfwait=True)
                        pr.barrier()

                if phases >= 3:
                    for qg in range(2):
                        run_attn_qg(qg, kbuf, vbuf)

            if dump == "x2":
                dma("y", o_dbg, x_scr)
                pr.barrier()

            if phases >= 4:
                def s512(gi, rt):
                    return x_rows(gi, rt)

                def sstrip(gi, rt, cp):
                    return x_strip_ap(gi, rt, cp)

                def dstrip(gi, rt, cp):
                    a = gi * 512 + rt * 128
                    return xout[a:a + 128, cp * 256:(cp + 1) * 256]

                ffn_pass("f3", s512, sstrip, dstrip, fused=False,
                         wd_fused=False)

        pr.barrier()

        @block.tensor
        def _(t):
            for f in pr.ops["t"]:
                f(t)

        @block.vector
        def _(v):
            for f in pr.ops["v"]:
                f(v)

        @block.scalar
        def _(a):
            for f in pr.ops["a"]:
                f(a)

        @block.gpsimd
        def _(g):
            for f in pr.ops["g"]:
                f(g)

        @block.sync
        def _(y):
            for f in pr.ops["y"]:
                f(y)

    return nc


def prepare_in_maps(inp):
    x = np.ascontiguousarray(np.asarray(inp["L0_attn_x"]))
    gq_a = np.asarray(inp["L0_ffn_gate_q"])
    uq_a = np.asarray(inp["L0_ffn_up_q"])
    dq_a = np.asarray(inp["L0_ffn_down_q"])
    gs_a = np.asarray(inp["L0_ffn_gate_s"])
    us_a = np.asarray(inp["L0_ffn_up_s"])
    ds_a = np.asarray(inp["L0_ffn_down_s"])
    wq_a = np.asarray(inp["L0_attn_Wq"])
    wk_a = np.asarray(inp["L0_attn_Wk"])
    wv_a = np.asarray(inp["L0_attn_Wv"])
    wo_a = np.asarray(inp["L0_attn_Wo"])

    tables = {h: _host_tables(h) for h in (0, 1)}
    perms = {h: _perm_rows(h) for h in (0, 1)}

    gq_b = np.ascontiguousarray(
        gq_a.reshape(HC, 128, IC, 128).transpose(2, 1, 0, 3)
        .reshape(IC, 128, H).astype(np.int8))
    uq_b = np.ascontiguousarray(
        uq_a.reshape(HC, 128, IC, 128).transpose(2, 1, 0, 3)
        .reshape(IC, 128, H).astype(np.int8))
    dq_b = np.ascontiguousarray(
        dq_a.reshape(IC, 128, NCP, 256).transpose(1, 0, 2, 3)
        .reshape(128, IC, NCP * 256).astype(np.int8))
    gs_b = np.ascontiguousarray(
        gs_a.reshape(HC, IC, 128).transpose(1, 0, 2).reshape(IC, H))
    us_b = np.ascontiguousarray(
        us_a.reshape(HC, IC, 128).transpose(1, 0, 2).reshape(IC, H))
    ds_b = np.ascontiguousarray(
        ds_a.reshape(IC, NCP, 256).transpose(1, 0, 2).reshape(NCP, IC * 256))
    wq_b = np.ascontiguousarray(
        wq_a.reshape(HC, 128, 16, 256).transpose(2, 1, 0, 3)
        .reshape(16, 128, HC * 256))
    wk_b = np.ascontiguousarray(
        wk_a.reshape(HC, 128, 2, 512).transpose(2, 1, 0, 3)
        .reshape(2, 128, HC * 512))
    wv_b = np.ascontiguousarray(
        wv_a.reshape(HC, 128, 2, 512).transpose(2, 1, 0, 3)
        .reshape(2, 128, HC * 512))
    wo_b = np.ascontiguousarray(
        wo_a.reshape(HC, 128, 16, 256).transpose(2, 1, 0, 3)
        .reshape(16, 128, HC * 256))

    in_maps = []
    for c in range(8):
        b, h = c // 2, c % 2
        ck, sk, cq, sq, masks = tables[h]
        in_maps.append(dict(
            xin=np.ascontiguousarray(x[b][perms[h]]),
            gq=gq_b, uq=uq_b, dq=dq_b, gs=gs_b, us=us_b, dsc=ds_b,
            wq=wq_b, wk=wk_b, wv=wv_b, wo=wo_b,
            ck=ck, sk=sk, cq=cq, sq=sq, cmask=masks,
        ))
    return in_maps


def assemble_out(results):
    out = np.empty((4, S, H), np.float32)
    for c in range(8):
        b, h = c // 2, c % 2
        xo = results[c]["xout"]
        for qg in (0, 1):
            base = ROW_BASE[h][qg]
            out[b, base:base + 512] = xo[qg * 512:(qg + 1) * 512]
    return out


def kernel(L0_attn_x, L0_attn_Wq, L0_attn_Wk, L0_attn_Wv, L0_attn_Wo,
           L0_ffn_gate_q, L0_ffn_gate_s, L0_ffn_up_q, L0_ffn_up_s,
           L0_ffn_down_q, L0_ffn_down_s):
    inp = dict(L0_attn_x=L0_attn_x, L0_attn_Wq=L0_attn_Wq,
               L0_attn_Wk=L0_attn_Wk, L0_attn_Wv=L0_attn_Wv,
               L0_attn_Wo=L0_attn_Wo, L0_ffn_gate_q=L0_ffn_gate_q,
               L0_ffn_gate_s=L0_ffn_gate_s, L0_ffn_up_q=L0_ffn_up_q,
               L0_ffn_up_s=L0_ffn_up_s, L0_ffn_down_q=L0_ffn_down_q,
               L0_ffn_down_s=L0_ffn_down_s)
    in_maps = prepare_in_maps(inp)
    nc = build()
    res = run_bass_kernel_spmd(nc, in_maps, core_ids=list(range(8)))
    return assemble_out(res.results)
